# version 06
# baseline (speedup 1.0000x reference)
import contextlib
import numpy as np
import concourse.bass as bass
import concourse.mybir as mybir
from concourse.bass_utils import run_bass_kernel_spmd

F32 = mybir.dt.float32
BF16 = mybir.dt.bfloat16
I8 = mybir.dt.int8
I32 = mybir.dt.int32
AF = mybir.ActivationFunctionType
ALU = mybir.AluOpType
AX = mybir.AxisListType

NCORE = 8
D = 2048
H = 16
HC = H // NCORE
DK = 128
DV = 128
BINS = 64
CH = 128
CB = 4
EPS = 1e-6
NORM_EPS = 1e-5
LN_EPS = 1e-5
DKS = float(DK) ** -0.5
LVLS = (8, 16, 32, 64, 128)
LOG64 = float(np.log(64.0))


class Prog:

    ENGS = ("tensor", "vector", "scalar", "gpsimd", "sync")
    QUEUES = ("sync", "scalar", "gpsimd")

    def __init__(self, nc):
        self.nc = nc
        self.thunks = []
        self.cnt = {e: 0 for e in self.ENGS}
        self.dcnt = {q: 0 for q in self.QUEUES}
        self.waited = {}
        self.writer = {}
        self.readers = {}
        self.sems = {}
        self.dsems = {}

    NDS = 8

    def init_sems(self, ctx):
        for e in self.ENGS:
            self.sems[e] = ctx.enter_context(self.nc.semaphore(f"se_{e}"))
        for q in self.QUEUES:
            self.dsems[q] = [ctx.enter_context(self.nc.semaphore(f"sd_{q}{j}"))
                             for j in range(self.NDS)]
        self.duse = {q: [0] * self.NDS for q in self.QUEUES}

    def _wait(self, eng, dep, allow_self=False):
        kind, name, cnt = dep
        if kind == "e" and name == eng and not allow_self:
            return
        if kind == "e":
            key = (eng, kind, name)
            if self.waited.get(key, 0) >= cnt:
                return
            self.waited[key] = cnt
            sem = self.sems[name]
            self.thunks.append((eng, lambda e, s=sem, t=cnt: e.wait_ge(s, t)))
        else:
            q, slot = name, (cnt - 1) % self.NDS
            tgt = 16 * ((cnt - 1) // self.NDS + 1)
            key = (eng, "d", q, slot)
            if self.waited.get(key, 0) >= tgt:
                return
            self.waited[key] = tgt
            sem = self.dsems[q][slot]
            self.thunks.append((eng, lambda e, s=sem, t=tgt: e.wait_ge(s, t)))

    def _record(self, owner, reads, writes):
        for k in reads:
            self.readers.setdefault(k, []).append(owner)
        for k in writes:
            self.writer[k] = owner
            self.readers[k] = []

    def _deps(self, eng, reads, writes):
        for k in reads:
            w = self.writer.get(k)
            if w:
                self._wait(eng, w, allow_self=True)
        for k in writes:
            w = self.writer.get(k)
            if w:
                self._wait(eng, w)
            for r in self.readers.get(k, ()):
                self._wait(eng, r)

    def op(self, eng, fn, reads=(), writes=()):
        self._deps(eng, reads, writes)
        self.cnt[eng] += 1
        self.thunks.append(
            (eng, lambda e, f=fn, g=eng: f(e).then_inc(self.sems[g], 1)))
        self._record(("e", eng, self.cnt[eng]), reads, writes)

    def raw(self, eng, fn):
        self.thunks.append((eng, lambda e, f=fn: f(e)))

    def dma(self, queue, out, in_, reads=(), writes=(), offset_reg=None):
        self._deps(queue, reads, writes)
        self.dcnt[queue] += 1
        slot = (self.dcnt[queue] - 1) % self.NDS
        prev_uses = self.duse[queue][slot]
        self.duse[queue][slot] = prev_uses + 1
        sem = self.dsems[queue][slot]
        if prev_uses:
            key = (queue, "d", queue, slot)
            if self.waited.get(key, 0) < 16 * prev_uses:
                self.waited[key] = 16 * prev_uses
                self.thunks.append(
                    (queue, lambda e, s=sem, t=16 * prev_uses: e.wait_ge(s, t)))
        if offset_reg is None:
            self.thunks.append(
                (queue, lambda e, o=out, i=in_, s=sem:
                 e.dma_start(out=o, in_=i).then_inc(s, 16)))
        else:
            self.thunks.append(
                (queue, lambda e, o=out, i=in_, s=sem, r=offset_reg:
                 e.dma_start(out=o, in_=bass.AP(i[0], r, i[1]))
                 .then_inc(s, 16)))
        self._record(("d", queue, self.dcnt[queue]), reads, writes)

    def collective(self, kind, rg, in_ap, out_ap, reads=(), writes=()):
        self._deps("gpsimd", reads, writes)
        self.cnt["gpsimd"] += 1

        def thunk(e, k=kind, r=rg, i=in_ap, o=out_ap):
            e.collective_compute(k, ALU.bypass, replica_groups=r,
                                 ins=[i], outs=[o]).then_inc(self.sems["gpsimd"], 1)
        self.thunks.append(("gpsimd", thunk))
        self._record(("e", "gpsimd", self.cnt["gpsimd"]), reads, writes)

    def wait_keys(self, eng, keys):
        for k in keys:
            w = self.writer.get(k)
            if w:
                self._wait(eng, w)

    def barrier(self):
        for e in self.ENGS:
            for o in self.ENGS:
                if o != e and self.cnt[o] > 0:
                    self._wait(e, ("e", o, self.cnt[o]))
            for q in self.QUEUES:
                n = self.dcnt[q]
                for back in range(min(self.NDS, n)):
                    self._wait(e, ("d", q, n - back))

    def finalize(self, block):
        per = {e: [] for e in self.ENGS}
        for eng, thunk in self.thunks:
            per[eng].append(thunk)

        def run(e, lst):
            for t in lst:
                t(e)
        block.tensor(lambda e: run(e, per["tensor"]))
        block.vector(lambda e: run(e, per["vector"]))
        block.scalar(lambda e: run(e, per["scalar"]))
        block.gpsimd(lambda e: run(e, per["gpsimd"]))
        block.sync(lambda e: run(e, per["sync"]))


def host_consts():
    idx = np.arange(CH)
    ms = []
    for w in LVLS:
        if w == 8:
            m = (idx[:, None] // 8 == idx[None, :] // 8) & (idx[:, None] > idx[None, :])
        else:
            blk, half = idx // w, (idx % w) >= (w // 2)
            m = (blk[:, None] == blk[None, :]) & half[:, None] & (~half)[None, :]
        ms.append(m)
    for m in list(ms):
        ms.append(m.T)
    m8x = (idx[:, None] // 8 == idx[None, :] // 8) & (idx[:, None] >= idx[None, :])
    ms.append(m8x.T)
    allm = np.tile(np.stack(ms).astype(np.float32), (1, 1, 4))
    masksf = np.ascontiguousarray(allm[[0, 5, 10]])
    masksu = np.ascontiguousarray(allm[[1, 2, 3, 4, 6, 7, 8, 9]]).astype(np.uint8)
    ident = np.eye(128, dtype=np.float32)
    ones = np.ones((128, 128), dtype=np.float32)
    stag = np.zeros((128, 31), dtype=np.float32)
    stag[:, 15] = 1.0
    rowsel = np.zeros((16, 16 * 128), dtype=np.float32)
    for s in range(16):
        rowsel[s, s * 128:(s + 1) * 128] = 1.0
    return ident, ones, stag, rowsel, masksf, masksu


def build(T, debug=()):
    TC = T // NCORE
    SEG = min(512, TC)
    NSEG = T // SEG
    NCH = T // CH
    NB = NCH // CB
    SB4 = CB * CH
    OSEG = min(512, TC)
    NOSEG = TC // OSEG
    NR = (T + 511) // 512
    NRo = (TC + 511) // 512
    WS = 12288 // NCORE

    nc = bass.Bass("TRN2", target_bir_lowering=False,
                   detect_race_conditions=False)

    def inp(name, shape, dt=F32):
        return nc.dram_tensor(name, shape, dt, kind="ExternalInput").ap()

    x_ownT = inp("x_ownT", [16, 128, TC])
    wqkv = inp("wqkv", [128, 16 * 3 * HC * 128])
    convw_i = inp("convw_i", [128, 3 * HC * 4])
    alog = inp("alog", [1, HC])
    dtb = inp("dtb", [128, HC])
    proxy = inp("proxy", [128, 128])
    uncw = inp("uncw", [128, 16 * BINS])
    he = inp("he", [4, HC])
    w1d_i = inp("w1d_i", [4, 32])
    w1e_i = inp("w1e_i", [4, 32])
    b1_i = inp("b1_i", [32, 1])
    w2_i = inp("w2_i", [32, 32])
    b2_i = inp("b2_i", [32, 1])
    w5_i = inp("w5_i", [32, 5])
    b5_i = inp("b5_i", [1, 5])
    g1_i = inp("g1_i", [128, 16 * 128])
    g2_i = inp("g2_i", [128, 16 * 128])
    g2b_i = inp("g2b_i", [128, 16])
    onorm_i = inp("onorm_i", [128, 1])
    wo_i = inp("wo_i", [16, 128, 16 * 128])
    wg8 = inp("wg8", [64, 128, 16 * 128], I8)
    wu8 = inp("wu8", [64, 128, 16 * 128], I8)
    wd8 = inp("wd8", [16, 4, 128, 16 * 128], I8)
    sg_i = inp("sg_i", [128, 64])
    su_i = inp("su_i", [128, 64])
    sd_i = inp("sd_i", [128, 16])
    cvecT = inp("cvecT", [128, 16])
    wmod_i = inp("wmod_i", [16, 128, WS])
    ident_i = inp("ident_i", [128, 128])
    ones_i = inp("ones_i", [128, 128])
    stag_i = inp("stag_i", [128, 31])
    rowsel_i = inp("rowsel_i", [16, 16 * 128])
    masksf_i = inp("masksf_i", [3, 128, 512])
    masksu_i = inp("masksu_i", [8, 128, 512], mybir.dt.uint8)
    rank_i = inp("rank_i", [1, 1], I32)

    out_sl = nc.dram_tensor("out_sl", [TC, D], F32, kind="ExternalOutput").ap()
    dbg_out = {}
    for name, shape in debug:
        dbg_out[name] = nc.dram_tensor("dbg_" + name, list(shape), F32,
                                       kind="ExternalOutput").ap()

    cc_x_in = nc.dram_tensor("cc_x_in", [16, 128, TC], F32).ap()
    cc_x_out = nc.dram_tensor("cc_x_out", [NCORE, 16, 128, TC], F32).ap()
    cc_m_in = nc.dram_tensor("cc_m_in", [1, WS], F32).ap()
    cc_m_out = nc.dram_tensor("cc_m_out", [NCORE, WS], F32).ap()
    cc_s_in = nc.dram_tensor("cc_s_in", [3, TC], F32).ap()
    cc_s_out = nc.dram_tensor("cc_s_out", [NCORE, 3, TC], F32).ap()
    cc_o_in = nc.dram_tensor("cc_o_in", [HC, T, DV], F32).ap()
    cc_o_out_h = nc.dram_tensor("cc_o_out", [NCORE * HC * T, DV], F32)
    cc_o_out = cc_o_out_h.ap()
    o_own = nc.dram_tensor("o_own", [16, min(512, T // NCORE) * DV], F32).ap()
    qkv_pre = nc.dram_tensor("qkv_pre", [3, HC, 128, T], F32).ap()
    qkv_post = nc.dram_tensor("qkv_post", [3, HC, 128, T], F32).ap()

    RG = [list(range(NCORE))]
    ctx = contextlib.ExitStack()
    P = Prog(nc)
    P.init_sems(ctx)

    def sb(name, p, f, dt=F32, cx=None):
        return (cx or ctx).enter_context(nc.sbuf_tensor(name, [p, f], dt))

    pb = [ctx.enter_context(nc.psum_tensor(f"pb{i}", [128, 512], F32))
          for i in range(6)]
    pm = [ctx.enter_context(nc.psum_tensor(f"pm{i}", [128, 512], F32))
          for i in range(2)]
    PBK = [f"pb{i}" for i in range(6)]
    PMK = ["pm0", "pm1"]

    ident = sb("ident", 128, 128)
    ones = sb("ones", 128, 128)
    stag = sb("stag", 128, 31)
    rowsel = sb("rowsel", 16, 16 * 128)

    modv = sb("modv", 128, 96)
    ca1 = sb("ca1", 128, 16)
    saca = sb("saca", 128, 16)
    cb1 = sb("cb1", 128, 16)
    gacl = sb("gacl", 128, 16)
    sdgb = sb("sdgb", 128, 16)
    small = sb("small", 128, 64)
    gchan = sb("gchan", 128, HC)
    onorms = sb("onorms", 128, 1)
    onbc = sb("onbc", 128, 128)
    convw = sb("convw", 128, 3 * HC * 4)
    proxys = sb("proxys", 128, 128)
    W5b = sb("W5b", 33, 5)
    W5Z = sb("W5Z", 33, 4 * 31)
    w1ds = sb("w1ds", 4, 32)
    w1es = sb("w1es", 4, 32)
    w2s = sb("w2s", 32, 32)
    b1s = sb("b1s", 32, 1)
    b2s = sb("b2s", 32, 1)
    hes = sb("hes", 4, HC)
    b1e = sb("b1e", 32, HC)

    def mm(out, lhsT, rhs, reads, writes, start=True, stop=True):
        assert isinstance(out, bass.AP), ("mm out", out)
        assert isinstance(lhsT, bass.AP), ("mm lhsT", lhsT)
        assert isinstance(rhs, bass.AP), ("mm rhs", rhs)
        P.op("tensor", lambda e, o=out, l=lhsT, r=rhs, s1=start, s2=stop:
             e.matmul(o, l, r, start=s1, stop=s2), reads=reads, writes=writes)

    def tr(out, in_, k, reads, writes):
        P.op("tensor", lambda e, o=out, i=in_, d=ident[0:k, 0:k]:
             e.transpose(o, i, d), reads=list(reads) + ["ident"], writes=writes)

    def tt(out, a, b, op, reads, writes, eng="vector"):
        P.op(eng, lambda e, o=out, x=a, y=b, p=op: e.tensor_tensor(o, x, y, p),
             reads=reads, writes=writes)

    def tsc(out, in_, s1, op0, reads, writes, s2=None, op1=None, eng="vector"):
        if s2 is None:
            P.op(eng, lambda e, o=out, i=in_, s=s1, p=op0:
                 e.tensor_single_scalar(o, i, s, p), reads=reads, writes=writes)
        else:
            P.op(eng, lambda e, o=out, i=in_, a=s1, b=s2, p0=op0, p1=op1:
                 e.tensor_scalar(o, i, a, b, p0, p1), reads=reads, writes=writes)

    def stt(out, in0, scalar, in1, op0, op1, reads, writes, eng="vector"):
        P.op(eng, lambda e, o=out, i0=in0, s=scalar, i1=in1, p0=op0, p1=op1:
             e.scalar_tensor_tensor(o, i0, s, i1, p0, p1),
             reads=reads, writes=writes)

    def act(out, in_, func, reads, writes, bias=0.0, scale=1.0, accum=None):
        P.op("scalar", lambda e, o=out, i=in_, f=func, b=bias, s=scale, a=accum:
             e.activation(out=o, in_=i, func=f, bias=b, scale=s, accum_out=a),
             reads=reads, writes=writes)

    def cpy(out, in_, reads, writes, eng="vector"):
        if eng == "scalar":
            P.op(eng, lambda e, o=out, i=in_: e.copy(o, i),
                 reads=reads, writes=writes)
        else:
            P.op(eng, lambda e, o=out, i=in_: e.tensor_copy(o, i),
                 reads=reads, writes=writes)

    def cpred(out, mask, data, reads, writes):
        P.op("vector", lambda e, o=out, m=mask, d=data:
             e.copy_predicated(o, m, d), reads=reads, writes=writes)

    def mems(ap_, val, writes, eng="vector"):
        P.op(eng, lambda e, a=ap_, v=val: e.memset(a, v), reads=(), writes=writes)

    def scan(out, d0, d1, init, reads, writes):
        P.op("vector", lambda e, o=out, a=d0, b=d1, i=init:
             e.tensor_tensor_scan(o, a, b, i, ALU.add, ALU.add),
             reads=reads, writes=writes)

    def red(out, in_, op, reads, writes):
        P.op("vector", lambda e, o=out, i=in_, p=op:
             e.tensor_reduce(o, i, AX.X, p), reads=reads, writes=writes)

    def recip(out, in_, reads, writes):
        P.op("vector", lambda e, o=out, i=in_: e.reciprocal(o, i),
             reads=reads, writes=writes)

    def dbgw(name, src_ap, reads):
        if name in dbg_out:
            P.dma("sync", dbg_out[name], src_ap, reads=reads)

    def rowbc(dst, dstk, rowt, rowk, s, c0, width, pmk=1):
        mm(pm[pmk][0:128, 0:width], rowsel[0:NR if rowt is not None else NR, 0:0],
           None, [], [])

    def bc(dst, dstk, rowt, rowk, s, c0, width, nr):
        mm(pm[1][0:128, 0:width], rowsel[0:nr, s * 128:s * 128 + 128],
           rowt[0:nr, c0:c0 + width], ["rowsel", rowk], ["pm1"])
        cpy(dst, pm[1][0:128, 0:width], ["pm1"], [dstk])

    P.dma("sync", ident[:, :], ident_i, writes=["ident"])
    P.dma("sync", ones[:, :], ones_i, writes=["ones"])
    P.dma("sync", stag[:, :], stag_i, writes=["stag"])
    P.dma("sync", rowsel[:, :], rowsel_i, writes=["rowsel"])
    P.dma("sync", convw[:, :], convw_i, writes=["convw"])
    P.dma("sync", proxys[:, :], proxy, writes=["proxys"])
    P.dma("sync", onorms[:, :], onorm_i, writes=["onorms"])
    P.dma("sync", w1ds[:, :], w1d_i, writes=["w1ds"])
    P.dma("sync", w1es[:, :], w1e_i, writes=["w1es"])
    P.dma("sync", w2s[:, :], w2_i, writes=["w2s"])
    P.dma("sync", b1s[:, :], b1_i, writes=["b1s"])
    P.dma("sync", b2s[:, :], b2_i, writes=["b2s"])
    P.dma("sync", hes[:, :], he, writes=["hes"])
    P.dma("sync", W5b[0:32, 0:5], w5_i, writes=["W5b"])
    P.dma("sync", W5b[32:33, 0:5], b5_i, writes=["W5b"])
    P.dma("sync", small[:, 0:HC], dtb, writes=["small"])

    mems(W5Z[:, :], 0.0, ["W5Z"])
    tt(W5Z[:, 0 * 31 + 15:0 * 31 + 16], W5b[:, 0:1], W5b[:, 1:2], ALU.add,
       ["W5b"], ["W5Z"])
    tt(W5Z[:, 1 * 31 + 15:1 * 31 + 16], W5b[:, 0:1], W5b[:, 1:2], ALU.subtract,
       ["W5b"], ["W5Z"])
    tt(W5Z[:, 2 * 31 + 15:2 * 31 + 16], W5b[:, 3:4], W5b[:, 4:5], ALU.add,
       ["W5b"], ["W5Z"])
    tt(W5Z[:, 3 * 31 + 15:3 * 31 + 16], W5b[:, 3:4], W5b[:, 4:5], ALU.subtract,
       ["W5b"], ["W5Z"])
    tt(small[0:33, 32:33], W5b[:, 0:1], W5b[:, 1:2], ALU.add, ["W5b"], ["small"])
    tt(small[0:33, 33:34], W5b[:, 0:1], W5b[:, 1:2], ALU.subtract, ["W5b"], ["small"])
    tt(small[0:33, 34:35], W5b[:, 3:4], W5b[:, 4:5], ALU.add, ["W5b"], ["small"])
    tt(small[0:33, 35:36], W5b[:, 3:4], W5b[:, 4:5], ALU.subtract, ["W5b"], ["small"])
    cpy(W5b[:, 0:1], small[0:33, 32:33], ["small"], ["W5b"])
    cpy(W5b[:, 3:4], small[0:33, 34:35], ["small"], ["W5b"])
    cpy(W5b[:, 4:5], small[0:33, 35:36], ["small"], ["W5b"])
    cpy(W5b[:, 1:2], small[0:33, 33:34], ["small"], ["W5b"])

    P.dma("gpsimd", cc_x_in, x_ownT, writes=["ccx"])
    P.collective("AllGather", RG, cc_x_in.opt(), cc_x_out.opt(),
                 reads=["ccx"], writes=["ccxo"])

    aexp = sb("aexp", 1, HC)
    P.dma("sync", aexp[:, :], alog, writes=["aexp"])
    act(aexp[:, :], aexp[:, :], AF.Exp, ["aexp"], ["aexp"])
    act(small[:, 16:16 + HC], small[:, 0:HC], AF.Abs, ["small"], ["small"])
    tsc(small[:, 24:24 + HC], small[:, 16:16 + HC], -1.0, ALU.mult,
        ["small"], ["small"])
    act(small[:, 24:24 + HC], small[:, 24:24 + HC], AF.Exp, ["small"], ["small"])
    tsc(small[:, 24:24 + HC], small[:, 24:24 + HC], 1.0, ALU.add,
        ["small"], ["small"])
    act(small[:, 16:16 + HC], small[:, 24:24 + HC], AF.Ln, ["small"], ["small"])
    tsc(small[:, 24:24 + HC], small[:, 0:HC], 0.0, ALU.max, ["small"], ["small"])
    tt(small[:, 0:HC], small[:, 16:16 + HC], small[:, 24:24 + HC], ALU.add,
       ["small"], ["small"])
    mm(pm[0][0:128, 0:HC], ones[0:1, 0:128], aexp[0:1, 0:HC],
       ["ones", "aexp"], ["pm0"])
    tt(gchan[:, :], small[:, 0:HC], pm[0][0:128, 0:HC], ALU.mult,
       ["small", "pm0"], ["gchan"])
    tsc(gchan[:, :], gchan[:, :], -1.0, ALU.mult, ["gchan"], ["gchan"])
    dbgw("gchan", gchan[:, :], ["gchan"])
    dbgw("sfp", small[:, 0:64], ["small"])

    tr(pm[0][0:1, 0:128], onorms[:, 0:1], 128, ["onorms"], ["pm0"])
    cpy(small[0:1, 0:1], pm[0][0:1, 0:1], ["pm0"], ["small"])
    cpy(aexp[0:1, 0:1], aexp[0:1, 0:1], ["aexp"], ["aexp"])
    onr = sb("onr", 1, 128)
    cpy(onr[0:1, :], pm[0][0:1, 0:128], ["pm0"], ["onr"])
    mm(pm[0][0:128, 0:128], ones[0:1, 0:128], onr[0:1, 0:128], ["ones", "onr"],
       ["pm0"])
    cpy(onbc[:, :], pm[0][0:128, 0:128], ["pm0"], ["onbc"])

    cvec_s = sb("cvec_s", 128, 16)
    modrow = sb("modrow", 1, WS)
    P.dma("sync", cvec_s[:, :], cvecT, writes=["cvec_s"])
    act(cvec_s[:, :], cvec_s[:, :], AF.Silu, ["cvec_s"], ["cvec_s"])
    nblk = (WS + 511) // 512
    with contextlib.ExitStack() as c0:
        wma = sb("wma", 128, WS, cx=c0)
        wmb = sb("wmb", 128, WS, cx=c0)
        for dc in range(16):
            wt, wk_ = (wma, "wma") if dc % 2 == 0 else (wmb, "wmb")
            P.dma("sync", wt[:, :], wmod_i[dc], writes=[wk_])
            for b in range(nblk):
                w = min(512, WS - b * 512)
                mm(pb[b][0:1, 0:w], cvec_s[:, dc:dc + 1],
                   wt[:, b * 512:b * 512 + w], reads=["cvec_s", wk_],
                   writes=[PBK[b]], start=(dc == 0), stop=(dc == 15))
        for b in range(nblk):
            w = min(512, WS - b * 512)
            cpy(modrow[0:1, b * 512:b * 512 + w], pb[b][0:1, 0:w], [PBK[b]],
                ["modrow"])
        dbgw("modrow", modrow[0:1, 0:WS], ["modrow"])
        P.dma("gpsimd", cc_m_in, modrow[0:1, 0:WS], reads=["modrow"], writes=["ccm"])
        P.collective("AllGather", RG, cc_m_in.opt(), cc_m_out.opt(),
                     reads=["ccm"], writes=["ccmo"])
        mvT = sb("mvT", 96, 128, cx=c0)
        P.dma("sync", mvT[:, :],
              cc_m_out.flatten().rearrange("(a b) -> a b", b=128),
              reads=["ccmo"], writes=["mvT"])
        dbgw("mvT", mvT[:, :], ["mvT"])
        tr(pm[0][0:128, 0:96], mvT[0:96, 0:128], 96, ["mvT"], ["pm0"])
        cpy(modv[:, :], pm[0][0:128, 0:96], ["pm0"], ["modv"])
        dbgw("modv", modv[:, :], ["modv"])
        P.barrier()
    tsc(ca1[:, :], modv[:, 16:32], 1.0, ALU.add, ["modv"], ["ca1"])
    tsc(cb1[:, :], modv[:, 64:80], 1.0, ALU.add, ["modv"], ["cb1"])
    recip(small[:, 0:16], ca1[:, :], ["ca1"], ["small"])
    tt(saca[:, :], modv[:, 0:16], small[:, 0:16], ALU.mult, ["modv", "small"],
       ["saca"])
    dbgw("ca1d", ca1[:, :], ["ca1"])
    dbgw("recd", small[:, 0:16], ["small"])
    dbgw("sacaE", saca[:, :], ["saca"])
    if "sacaX" in dbg_out:
        def vflush():
            P.thunks.append(("vector",
                             lambda e, t=P.cnt["vector"]:
                             e.wait_ge(P.sems["vector"], t)))
        rX = sb("rX", 128, 16)
        sacaX = sb("sacaX", 128, 16)
        recip(rX[:, :], ca1[:, :], ["ca1"], ["rX"])
        vflush()
        tt(sacaX[:, :], modv[:, 0:16], rX[:, :], ALU.mult, ["modv", "rX"],
           ["sacaX"])
        vflush()
        dbgw("sacaX", sacaX[:, :], ["sacaX"])
        rZ = sb("rZ", 128, 16)
        sacaZ = sb("sacaZ", 128, 16)
        recip(rZ[:, :], ca1[:, :], ["ca1"], ["rZ"])
        tt(sacaZ[:, :], modv[:, 0:16], rZ[:, :], ALU.mult, ["modv", "rZ"],
           ["sacaZ"])
        dbgw("sacaZ", sacaZ[:, :], ["sacaZ"])
    for srcc, isga in ((32, True), (80, False)):
        src = modv[:, srcc:srcc + 16]
        act(small[:, 0:16], src, AF.Abs, ["modv"], ["small"])
        tsc(small[:, 0:16], small[:, 0:16], 0.25, ALU.max, ["small"], ["small"])
        tsc(small[:, 16:32], src, 0.0, ALU.is_ge, ["modv"], ["small"])
        tsc(small[:, 16:32], small[:, 16:32], 2.0, ALU.mult, ["small"], ["small"],
            s2=1.0, op1=ALU.subtract)
        if isga:
            tt(gacl[:, :], small[:, 0:16], small[:, 16:32], ALU.mult,
               ["small"], ["gacl"])
        else:
            tt(small[:, 32:48], small[:, 0:16], small[:, 16:32], ALU.mult,
               ["small"], ["small"])
            P.dma("sync", sdgb[:, :], sd_i, writes=["sdgb"])
            tt(sdgb[:, :], sdgb[:, :], small[:, 32:48], ALU.mult,
               ["sdgb", "small"], ["sdgb"])

    mur = sb("mur", NRo, 512)
    rr_ = sb("rr_", NRo, 512)
    entr = sb("entr", 1, TC)
    with contextlib.ExitStack() as c1:
        xs0 = sb("xs0", 128, OSEG, cx=c1)
        sq0 = sb("sq0", 128, OSEG, cx=c1)
        for os_ in range(NOSEG):
            o0 = os_ * OSEG
            for dc in range(16):
                P.dma("sync", xs0[:, :], x_ownT[dc, :, o0:o0 + OSEG], writes=["xs0"])
                act(sq0[:, :], xs0[:, :], AF.Square, ["xs0"], ["sq0"])
                mm(pb[0][0:NRo, 0:OSEG],
                   stag[:, 15 - os_:15 - os_ + NRo], xs0[:, :],
                   ["stag", "xs0"], ["pb0"],
                   start=(dc == 0 and os_ == 0), stop=(dc == 15 and os_ == NOSEG - 1))
                mm(pb[1][0:NRo, 0:OSEG],
                   stag[:, 15 - os_:15 - os_ + NRo], sq0[:, :],
                   ["stag", "sq0"], ["pb1"],
                   start=(dc == 0 and os_ == 0), stop=(dc == 15 and os_ == NOSEG - 1))
        tsc(mur[:, 0:OSEG if NRo == 1 else 512], pb[0][0:NRo, 0:OSEG if NRo == 1 else 512],
            1.0 / D, ALU.mult, ["pb0"], ["mur"])
        tsc(rr_[:, 0:OSEG if NRo == 1 else 512], pb[1][0:NRo, 0:OSEG if NRo == 1 else 512],
            1.0 / D, ALU.mult, ["pb1"], ["rr_"])
        tt(sq0[0:NRo, 0:OSEG if NRo == 1 else 512], mur[:, 0:OSEG if NRo == 1 else 512],
           mur[:, 0:OSEG if NRo == 1 else 512], ALU.mult, ["mur"], ["sq0"])
        tt(rr_[:, 0:OSEG if NRo == 1 else 512], rr_[:, 0:OSEG if NRo == 1 else 512],
           sq0[0:NRo, 0:OSEG if NRo == 1 else 512], ALU.subtract,
           ["rr_", "sq0"], ["rr_"])
        tsc(rr_[:, 0:OSEG if NRo == 1 else 512], rr_[:, 0:OSEG if NRo == 1 else 512],
            LN_EPS, ALU.add, ["rr_"], ["rr_"])
        act(rr_[:, 0:OSEG if NRo == 1 else 512], rr_[:, 0:OSEG if NRo == 1 else 512],
            AF.Sqrt, ["rr_"], ["rr_"])
        recip(rr_[:, 0:OSEG if NRo == 1 else 512], rr_[:, 0:OSEG if NRo == 1 else 512],
              ["rr_"], ["rr_"])
        P.barrier()

    with contextlib.ExitStack() as c2:
        xs0 = sb("xs0b", 128, OSEG, cx=c2)
        ht0 = sb("ht0", 128, OSEG, cx=c2)
        bc1 = sb("bc1e", 128, OSEG, cx=c2)
        bc2 = sb("bc2e", 128, OSEG, cx=c2)
        uncs = sb("uncs", 128, 16 * BINS, cx=c2)
        zrows = sb("zrows", BINS, OSEG, cx=c2)
        zsc = sb("zsc", 128, BINS, cx=c2)
        ecol = sb("ecol", 128, BINS, cx=c2)
        tcol = sb("tcol", 128, 8, cx=c2)
        P.dma("sync", uncs[:, :], uncw, writes=["uncs"])
        for os_ in range(NOSEG):
            o0 = os_ * OSEG
            sro, cco = (o0 // 512, o0 % 512)
            bc(bc1[:, :], "bc1e", mur, "mur", sro, cco, OSEG, NRo)
            bc(bc2[:, :], "bc2e", rr_, "rr_", sro, cco, OSEG, NRo)
            for dc in range(16):
                P.dma("sync", xs0[:, :], x_ownT[dc, :, o0:o0 + OSEG], writes=["xs0b"])
                tt(ht0[:, :], xs0[:, :], bc1[:, :], ALU.subtract,
                   ["xs0b", "bc1e"], ["ht0"])
                tt(ht0[:, :], ht0[:, :], bc2[:, :], ALU.mult, ["ht0", "bc2e"], ["ht0"])
                tsc(ht0[:, :], ht0[:, :], ca1[:, dc:dc + 1], ALU.mult,
                    ["ht0", "ca1", "modv"], ["ht0"],
                    s2=modv[:, dc:dc + 1], op1=ALU.add)
                mm(pb[2][0:BINS, 0:OSEG], uncs[:, dc * BINS:(dc + 1) * BINS],
                   ht0[:, :], ["uncs", "ht0"], ["pb2"],
                   start=(dc == 0), stop=(dc == 15))
            cpy(zrows[:, :], pb[2][0:BINS, 0:OSEG], ["pb2"], ["zrows"])
            for tb in range(OSEG // 128):
                t0 = tb * 128
                tr(pm[0][0:128, 0:BINS], zrows[0:BINS, t0:t0 + 128], BINS,
                   ["zrows"], ["pm0"])
                cpy(zsc[:, :], pm[0][0:128, 0:BINS], ["pm0"], ["zsc"])
                if os_ == 0 and tb == 0:
                    dbgw("zsc0", zsc[:, :], ["zsc"])
                red(tcol[:, 0:1], zsc[:, :], ALU.max, ["zsc"], ["tcol"])
                tsc(tcol[:, 1:2], tcol[:, 0:1], -1.0, ALU.mult, ["tcol"], ["tcol"])
                act(ecol[:, :], zsc[:, :], AF.Exp, ["zsc", "tcol"],
                    ["ecol", "tcol"], bias=tcol[:, 1:2], accum=tcol[:, 2:3])
                tt(zsc[:, :], ecol[:, :], zsc[:, :], ALU.mult, ["ecol", "zsc"],
                   ["zsc"])
                red(tcol[:, 3:4], zsc[:, :], ALU.add, ["zsc"], ["tcol"])
                recip(tcol[:, 4:5], tcol[:, 2:3], ["tcol"], ["tcol"])
                tt(tcol[:, 5:6], tcol[:, 3:4], tcol[:, 4:5], ALU.mult,
                   ["tcol"], ["tcol"])
                act(tcol[:, 6:7], tcol[:, 2:3], AF.Ln, ["tcol"], ["tcol"])
                tt(tcol[:, 6:7], tcol[:, 6:7], tcol[:, 0:1], ALU.add,
                   ["tcol"], ["tcol"])
                tt(tcol[:, 7:8], tcol[:, 6:7], tcol[:, 5:6], ALU.subtract,
                   ["tcol"], ["tcol"])
                tsc(tcol[:, 7:8], tcol[:, 7:8], 1.0 / LOG64, ALU.mult,
                    ["tcol"], ["tcol"])
                if os_ == 0 and tb == 0:
                    dbgw("tcol0", tcol[:, :], ["tcol"])
                mm(pm[0][0:1, 0:128], tcol[:, 7:8], ident[:, :],
                   ["tcol", "ident"], ["pm0"])
                cpy(entr[0:1, o0 + t0:o0 + t0 + 128], pm[0][0:1, 0:128],
                    ["pm0"], ["entr"])
        P.barrier()

    st3 = sb("st3", 3, TC)
    for s in range(NRo):
        w = min(512, TC - s * 512)
        P.dma("sync", st3[0:1, s * 512:s * 512 + w], mur[s:s + 1, 0:w],
              reads=["mur"], writes=["st3"])
        P.dma("sync", st3[1:2, s * 512:s * 512 + w], rr_[s:s + 1, 0:w],
              reads=["rr_"], writes=["st3"])
    P.dma("sync", st3[2:3, 0:TC], entr[0:1, 0:TC], reads=["entr"], writes=["st3"])
    P.dma("gpsimd", cc_s_in, st3[0:3, 0:TC], reads=["st3"], writes=["ccs"])
    P.collective("AllGather", RG, cc_s_in.opt(), cc_s_out.opt(),
                 reads=["ccs"], writes=["ccso"])
    cmid = contextlib.ExitStack()
    murow = sb("murow", NR, 512, cx=cmid)
    rrow = sb("rrow", NR, 512, cx=cmid)
    entrow = sb("entrow", NR, 512, cx=cmid)
    masksf = sb("masksf", 128, 3 * 512, cx=cmid)
    masksu = sb("masksu", 128, 8 * 512, mybir.dt.uint8, cx=cmid)
    for i in range(3):
        P.dma("sync", masksf[:, i * 512:(i + 1) * 512], masksf_i[i], writes=["masks"])
    for i in range(8):
        P.dma("sync", masksu[:, i * 512:(i + 1) * 512], masksu_i[i], writes=["masks"])
    for r in range(NCORE):
        t0 = r * TC
        sr0, cc0 = t0 // 512, t0 % 512
        npart = max(1, TC // 512)
        w = min(512, TC)
        for pp in range(npart):
            P.dma("sync", murow[sr0 + pp:sr0 + pp + 1, cc0:cc0 + w],
                  cc_s_out[r, 0, pp * 512:pp * 512 + w], reads=["ccso"],
                  writes=["murow"])
            P.dma("sync", rrow[sr0 + pp:sr0 + pp + 1, cc0:cc0 + w],
                  cc_s_out[r, 1, pp * 512:pp * 512 + w], reads=["ccso"],
                  writes=["rrow"])
            P.dma("sync", entrow[sr0 + pp:sr0 + pp + 1, cc0:cc0 + w],
                  cc_s_out[r, 2, pp * 512:pp * 512 + w], reads=["ccso"],
                  writes=["entrow"])
    dbgw("murow", murow[0:NR, :], ["murow"])
    dbgw("rrow", rrow[0:NR, :], ["rrow"])
    dbgw("entrow", entrow[0:NR, :], ["entrow"])

    with contextlib.ExitStack() as c3:
        wtil = sb("wtil", 128, 16 * 3 * HC * 128, cx=c3)
        ncsum = sb("ncsum", 128, 3 * HC, cx=c3)
        cjt = sb("cjt", 128, 3 * HC, cx=c3)
        xsa = sb("xsa", 128, SEG, cx=c3)
        xsb = sb("xsb", 128, SEG, cx=c3)
        ppre = sb("ppre", 128, SEG, cx=c3)
        bc1 = sb("bc1q", 128, SEG, cx=c3)
        bc2 = sb("bc2q", 128, SEG, cx=c3)

        P.dma("sync", wtil[:, :], wqkv, writes=["wtil"])
        for dc in range(16):
            w0 = dc * (3 * HC * 128)
            tsc(wtil[:, w0:w0 + 3 * HC * 128], wtil[:, w0:w0 + 3 * HC * 128],
                ca1[:, dc:dc + 1], ALU.mult, ["wtil", "ca1"], ["wtil"])

        def wtsl(dc, proj, hcol):
            o = dc * (3 * HC * 128) + proj * (HC * 128) + hcol * 128
            return wtil[:, o:o + 128]

        P.barrier()
        for proj in range(3):
            for hcol in range(HC):
                j = proj * HC + hcol
                for dc in range(16):
                    mm(pm[0][0:128, j:j + 1], wtsl(dc, proj, hcol),
                       ones[:, 0:1], ["ones", "wtil"], ["pm0"],
                       start=(dc == 0), stop=(dc == 15))
                    mm(pm[1][0:128, j:j + 1], wtsl(dc, proj, hcol),
                       saca[:, dc:dc + 1], ["saca", "wtil"], ["pm1"],
                       start=(dc == 0), stop=(dc == 15))
        dbgw("onesd", ones[:, :], ["ones"])
        dbgw("sacad", saca[:, :], ["saca"])
        cpy(ncsum[:, :], pm[0][0:128, 0:3 * HC], ["pm0"], ["ncsum"])
        tsc(ncsum[:, :], ncsum[:, :], -1.0, ALU.mult, ["ncsum"], ["ncsum"])
        cpy(cjt[:, :], pm[1][0:128, 0:3 * HC], ["pm1"], ["cjt"])
        P.barrier()

        for sgi in range(NSEG):
            t0 = sgi * SEG
            rk, loc = t0 // TC, t0 % TC
            sr0, cc0 = t0 // 512, t0 % 512
            bc(bc1[:, :], "bc1q", murow, "murow", sr0, cc0, SEG, NR)
            bc(bc2[:, :], "bc2q", rrow, "rrow", sr0, cc0, SEG, NR)
            for dc in range(16):
                xt, xk = ((xsa, "xsa") if dc % 2 == 0 else (xsb, "xsb"))
                P.dma("sync", xt[:, :], cc_x_out[rk, dc, :, loc:loc + SEG],
                      reads=["ccxo"], writes=[xk])
                for proj in range(3):
                    for hcol in range(HC):
                        j = proj * HC + hcol
                        mm(pb[j][0:128, 0:SEG], wtsl(dc, proj, hcol), xt[:, :],
                           ["wtil", xk], [PBK[j]],
                           start=(dc == 0), stop=(dc == 15))
            if sgi == 0:
                dbgw("bc1q", bc1[:, :], ["bc1q"])
                dbgw("bc2q", bc2[:, :], ["bc2q"])
                dbgw("wtl", wtil[:, 0:2048], ["wtil"])
                dbgw("ncs", ncsum[:, :], ["ncsum"])
                dbgw("cjt", cjt[:, :], ["cjt"])
            for proj in range(3):
                for hcol in range(HC):
                    j = proj * HC + hcol
                    stt(ppre[:, :], bc1[:, :], ncsum[:, j:j + 1],
                        pb[j][0:128, 0:SEG], ALU.mult, ALU.add,
                        ["bc1q", "ncsum", PBK[j]], ["ppre"])
                    tt(ppre[:, :], ppre[:, :], bc2[:, :], ALU.mult,
                       ["ppre", "bc2q"], ["ppre"])
                    tsc(ppre[:, :], ppre[:, :], cjt[:, j:j + 1], ALU.add,
                        ["ppre", "cjt"], ["ppre"])
                    P.dma("scalar", qkv_pre[proj, hcol, :, t0:t0 + SEG],
                          ppre[:, :], reads=["ppre"], writes=["qkvpre"])
        P.barrier()

    dbgw("q_pre", qkv_pre[0, 0], ["qkvpre"])
    with contextlib.ExitStack() as c4:
        convin = sb("convin", 128, T + 3, cx=c4)
        convacc = sb("convacc", 128, T, cx=c4)
        for proj in range(3):
            for hcol in range(HC):
                j = proj * HC + hcol
                mems(convin[:, 0:3], 0.0, ["convin"])
                P.dma("sync", convin[:, 3:T + 3], qkv_pre[proj, hcol],
                      reads=["qkvpre"], writes=["convin"])
                tsc(convacc[:, :], convin[:, 0:T], convw[:, j * 4:j * 4 + 1],
                    ALU.mult, ["convin", "convw"], ["convacc"])
                for tau in range(1, 4):
                    stt(convacc[:, :], convin[:, tau:T + tau],
                        convw[:, j * 4 + tau:j * 4 + tau + 1], convacc[:, :],
                        ALU.mult, ALU.add, ["convin", "convw", "convacc"],
                        ["convacc"])
                act(convacc[:, :], convacc[:, :], AF.Silu, ["convacc"],
                    ["convacc"])
                P.dma("scalar", qkv_post[proj, hcol], convacc[:, :],
                      reads=["convacc"], writes=["qkvpost"])
        P.barrier()
    dbgw("q_post", qkv_post[0, 0], ["qkvpost"])
    dbgw("k_post", qkv_post[1, 0], ["qkvpost"])
    dbgw("v_post", qkv_post[2, 0], ["qkvpost"])

    def build_feats(h):
        with contextlib.ExitStack() as cf:
            kt = sb(f"f_kt{h}", 128, 512, cx=cf)
            vt = sb(f"f_vt{h}", 128, 512, cx=cf)
            vh = sb(f"f_vh{h}", 128, 512, cx=cf)
            sq = sb(f"f_sq{h}", 128, 512, cx=cf)
            kn2 = sb(f"f_kn2{h}", NR, 512, cx=cf)
            vn2 = sb(f"f_vn2{h}", NR, 512, cx=cf)
            vhn2 = sb(f"f_vhn2{h}", NR, 512, cx=cf)
            vd = sb(f"f_vd{h}", NR, 512, cx=cf)
            s1 = sb(f"f_s1{h}", NR, 512, cx=cf)
            s2 = sb(f"f_s2{h}", NR, 512, cx=cf)
            s3 = sb(f"f_s3{h}", NR, 512, cx=cf)
            h1t = sb(f"f_h1t{h}", 32, 512, cx=cf)
            afT = sb(f"f_afT{h}", NR, 512, cx=cf)
            asT = sb(f"f_asT{h}", NR, 512, cx=cf)
            zr = sb(f"f_zr{h}", 16, 128, cx=cf)
            feats = sb(f"feats{h}", 4, T, cx=cf)
            hf33 = sb(f"hf33{h}", 33, T, cx=cf)
            for s in range(NR):
                t0 = s * 512
                P.dma("sync", kt[:, :], qkv_post[1, h, :, t0:t0 + 512],
                      reads=["qkvpost"], writes=["f_kt"])
                P.dma("sync", vt[:, :], qkv_post[2, h, :, t0:t0 + 512],
                      reads=["qkvpost"], writes=["f_vt"])
                mm(pm[0][0:128, 0:512], proxys[:, :], kt[:, :], ["proxys", "f_kt"],
                   ["pm0"])
                cpy(vh[:, :], pm[0][0:128, 0:512], ["pm0"], ["f_vh"])
                st_, sp_ = (s == 0), (s == NR - 1)
                act(sq[:, :], kt[:, :], AF.Square, ["f_kt"], ["f_sq"])
                mm(pb[0][0:NR, 0:512], stag[:, 15 - s:15 - s + NR], sq[:, :],
                   ["stag", "f_sq"], ["pb0"], start=st_, stop=sp_)
                act(sq[:, :], vt[:, :], AF.Square, ["f_vt"], ["f_sq"])
                mm(pb[1][0:NR, 0:512], stag[:, 15 - s:15 - s + NR], sq[:, :],
                   ["stag", "f_sq"], ["pb1"], start=st_, stop=sp_)
                act(sq[:, :], vh[:, :], AF.Square, ["f_vh"], ["f_sq"])
                mm(pb[2][0:NR, 0:512], stag[:, 15 - s:15 - s + NR], sq[:, :],
                   ["stag", "f_sq"], ["pb2"], start=st_, stop=sp_)
                tt(sq[:, :], vt[:, :], vh[:, :], ALU.mult, ["f_vt", "f_vh"],
                   ["f_sq"])
                mm(pb[3][0:NR, 0:512], stag[:, 15 - s:15 - s + NR], sq[:, :],
                   ["stag", "f_sq"], ["pb3"], start=st_, stop=sp_)
            cpy(kn2[:, :], pb[0][0:NR, 0:512], ["pb0"], ["f_kn2"])
            cpy(vn2[:, :], pb[1][0:NR, 0:512], ["pb1"], ["f_vn2"])
            cpy(vhn2[:, :], pb[2][0:NR, 0:512], ["pb2"], ["f_vhn2"])
            cpy(vd[:, :], pb[3][0:NR, 0:512], ["pb3"], ["f_vd"])
            act(s1[:, :], kn2[:, :], AF.Sqrt, ["f_kn2"], ["f_s1"])
            tsc(s1[:, :], s1[:, :], EPS, ALU.add, ["f_s1"], ["f_s1"])
            recip(rknT[0:NR, h * 512:(h + 1) * 512], s1[:, :], ["f_s1"], ["rknT"])
            act(s1[:, :], vn2[:, :], AF.Sqrt, ["f_vn2"], ["f_s1"])
            act(s2[:, :], vhn2[:, :], AF.Sqrt, ["f_vhn2"], ["f_s2"])
            tt(s3[:, :], vn2[:, :], vhn2[:, :], ALU.add, ["f_vn2", "f_vhn2"],
               ["f_s3"])
            stt(s3[:, :], vd[:, :], -2.0, s3[:, :], ALU.mult, ALU.add,
                ["f_vd", "f_s3"], ["f_s3"])
            act(s3[:, :], s3[:, :], AF.Sqrt, ["f_s3"], ["f_s3"])
            tsc(vn2[:, :], s1[:, :], EPS, ALU.add, ["f_s1"], ["f_vn2"])
            recip(vn2[:, :], vn2[:, :], ["f_vn2"], ["f_vn2"])
            tt(vn2[:, :], s3[:, :], vn2[:, :], ALU.mult, ["f_s3", "f_vn2"],
               ["f_vn2"])
            tt(vhn2[:, :], s1[:, :], s2[:, :], ALU.mult, ["f_s1", "f_s2"],
               ["f_vhn2"])
            tsc(vhn2[:, :], vhn2[:, :], EPS, ALU.add, ["f_vhn2"], ["f_vhn2"])
            recip(vhn2[:, :], vhn2[:, :], ["f_vhn2"], ["f_vhn2"])
            tt(vhn2[:, :], vd[:, :], vhn2[:, :], ALU.mult, ["f_vd", "f_vhn2"],
               ["f_vhn2"])
            tsc(s3[:, :], s3[:, :], 1.0, ALU.add, ["f_s3"], ["f_s3"])
            act(s3[:, :], s3[:, :], AF.Ln, ["f_s3"], ["f_s3"])
            for s in range(NR):
                P.dma("sync", feats[0:1, s * 512:(s + 1) * 512],
                      entrow[s:s + 1, 0:512], reads=["entrow"], writes=["feats"])
                P.dma("sync", feats[1:2, s * 512:(s + 1) * 512],
                      vn2[s:s + 1, 0:512], reads=["f_vn2"], writes=["feats"])
                P.dma("sync", feats[2:3, s * 512:(s + 1) * 512],
                      vhn2[s:s + 1, 0:512], reads=["f_vhn2"], writes=["feats"])
                P.dma("sync", feats[3:4, s * 512:(s + 1) * 512],
                      s3[s:s + 1, 0:512], reads=["f_s3"], writes=["feats"])
            mm(pm[0][0:32, 0:1], w1es[:, :], hes[:, h:h + 1], ["w1es", "hes"], ["pm0"])
            tt(b1e[:, h:h + 1], pm[0][0:32, 0:1], b1s[:, :], ALU.add,
               ["pm0", "b1s"], ["b1e"])
            mems(hf33[32:33, 0:T], 1.0, ["hf33"])
            for s in range(NR):
                t0 = s * 512
                mm(pm[0][0:32, 0:512], w1ds[:, :], feats[0:4, t0:t0 + 512],
                   ["w1ds", "feats"], ["pm0"])
                act(h1t[:, :], pm[0][0:32, 0:512], AF.Silu, ["pm0", "b1e"],
                    ["f_h1t"], bias=b1e[:, h:h + 1])
                mm(pm[1][0:32, 0:512], w2s[:, :], h1t[:, :], ["w2s", "f_h1t"], ["pm1"])
                act(hf33[0:32, t0:t0 + 512], pm[1][0:32, 0:512], AF.Silu,
                    ["pm1", "b2s"], ["hf33"], bias=b2s[:, :])
            if h == 0:
                dbgw("hf33", hf33[0:33, 0:T], ["hf33"])
            for jj, (j, dstt, dk) in enumerate(
                    ((0, bfT, "bfT"), (1, bsT, "bsT"),
                     (2, afT, "f_afT"), (3, asT, "f_asT"))):
                for s in range(NR):
                    t0 = s * 512
                    mm(pb[jj][0:NR, 0:512],
                       W5Z[:, j * 31 + 15 - s:j * 31 + 15 - s + NR],
                       hf33[:, t0:t0 + 512], ["W5Z", "hf33"], [PBK[jj]],
                       start=(s == 0), stop=(s == NR - 1))
                coff = h * 512 if dstt in (bfT, bsT) else 0
                act(dstt[0:NR, coff:coff + 512], pb[jj][0:NR, 0:512],
                    AF.Sigmoid, [PBK[jj]], [dk])
            for cn in range(NCH):
                mm(pm[0][0:128, cn * 5:cn * 5 + 5], hf33[0:33, cn * 128:cn * 128 + 128],
                   W5b[0:33, 0:5], ["hf33", "W5b"], ["pm0"])
            act(colpk[:, h * 5 * NCH:(h + 1) * 5 * NCH], pm[0][0:128, 0:5 * NCH],
                AF.Sigmoid, ["pm0"], ["colpk"])
            tsc(omlam[:, h * NCH:(h + 1) * NCH],
                colpk[:, h * 5 * NCH + 2:(h + 1) * 5 * NCH:5], -1.0, ALU.mult,
                ["colpk"], ["omlam"], s2=1.0, op1=ALU.add)
            mems(zr[:, :], 0.0, ["f_zr"])
            zrb = zr[0:NR, 0:128]
            for r, srcT, sk in ((0, afT, "f_afT"), (1, asT, "f_asT")):
                for cc in range(4):
                    c0 = cc * 128
                    scan(arT[0:NR, (h * 2 + r) * 512 + c0:(h * 2 + r) * 512 + c0 + 128],
                         srcT[0:NR, c0:c0 + 128], zrb, 0.0,
                         [sk, "f_zr"], ["arT"])
            P.barrier()

    def build_delta(h):
        with contextlib.ExitStack() as cd:
            kap = sb(f"d_kap{h}", 128, SB4, cx=cd)
            qap = sb(f"d_qap{h}", 128, SB4, cx=cd)
            vap = sb(f"d_vap{h}", 128, SB4, cx=cd)
            kup = sb(f"d_kup{h}", 128, SB4, cx=cd)
            kbt = sb(f"d_kbt{h}", 128, SB4, cx=cd)
            khf = sb(f"d_khf{h}", 128, SB4, cx=cd)
            vT4 = sb(f"d_vT4{h}", 128, SB4, cx=cd)
            argt = sb(f"d_argt{h}", 128, 12 * SB4, cx=cd)
            expt = argt
            fkt = sb(f"d_fkt{h}", 128, 5 * SB4, cx=cd)
            fqt = sb(f"d_fqt{h}", 128, 5 * SB4, cx=cd)
            gt = sb(f"d_gt{h}", 128, 5 * SB4, cx=cd)
            khTt = sb(f"d_khTt{h}", 128, 2 * SB4, cx=cd)
            kck = sb(f"d_kck{h}", 128, 2 * SB4, cx=cd)
            qck = sb(f"d_qck{h}", 128, 2 * SB4, cx=cd)
            gend = sb(f"d_gend{h}", 128, 2 * CB, cx=cd)
            Mt = sb(f"d_Mt{h}", 128, 2 * SB4, cx=cd)
            MTt = sb(f"d_MTt{h}", 128, 2 * SB4, cx=cd)
            Xt = sb(f"d_Xt{h}", 128, 2 * SB4, cx=cd)
            powt = sb(f"d_powt{h}", 128, 8 * 768, cx=cd)
            powm = sb(f"d_powm{h}", 128, 5 * 128, cx=cd)
            Yt = sb(f"d_Yt{h}", 128, 128, cx=cd)
            Wt0 = sb(f"d_Wt0{h}", 128, 128, cx=cd)
            Wt1 = sb(f"d_Wt1{h}", 128, 128, cx=cd)
            ocm = sb(f"d_ocm{h}", 128, 128, cx=cd)

            def m3(ap_, w):
                return ap_.rearrange("p (b w) -> p b w", w=w)

            GAM = argt[:, 11 * SB4:12 * SB4]
            for bi in range(NB):
                b0 = bi * SB4
                P.dma("sync", kap[:, :], qkv_post[1, h, :, b0:b0 + SB4],
                      reads=["qkvpost"], writes=["d_kap"])
                P.dma("sync", qap[:, :], qkv_post[0, h, :, b0:b0 + SB4],
                      reads=["qkvpost"], writes=["d_qap"])
                P.dma("sync", vap[:, :], qkv_post[2, h, :, b0:b0 + SB4],
                      reads=["qkvpost"], writes=["d_vap"])
                mm(pm[0][0:128, 0:SB4], rowsel[0:NR, bi * 128:bi * 128 + 128],
                   rknT[0:NR, h * 512:(h + 1) * 512], ["rowsel", "rknT"], ["pm0"])
                tt(kup[:, :], kap[:, :], pm[0][0:128, 0:SB4], ALU.mult,
                   ["d_kap", "pm0"], ["d_kup"])
                for c4 in range(CB):
                    tr(pm[1][0:128, c4 * 128:c4 * 128 + 128],
                       vap[:, c4 * 128:c4 * 128 + 128], 128, ["d_vap"], ["pm1"])
                cpy(vT4[:, :], pm[1][0:128, 0:SB4], ["pm1"], ["d_vT4"])
                for r in range(2):
                    blk = (h * 2 + r) * 512
                    mm(pm[0][0:128, 0:SB4], rowsel[0:NR, bi * 128:bi * 128 + 128],
                       arT[0:NR, blk:blk + 512], ["rowsel", "arT"], ["pm0"])
                    cpy(GAM, pm[0][0:128, 0:SB4], ["pm0"], ["d_argt"])
                    for li, w in enumerate(LVLS):
                        bnd = m3(GAM, w)[:, :, w // 2 - 1:w // 2] \
                            .broadcast_to([128, SB4 // w, w])
                        tt(m3(argt[:, li * SB4:(li + 1) * SB4], w), m3(GAM, w),
                           bnd, ALU.subtract, ["d_argt"], ["d_argt"])
                        tt(m3(argt[:, (5 + li) * SB4:(6 + li) * SB4], w), bnd,
                           m3(GAM, w), ALU.subtract, ["d_argt"], ["d_argt"])
                    bnd = m3(GAM, 128)[:, :, 127:128].broadcast_to([128, CB, 128])
                    tt(m3(argt[:, 10 * SB4:11 * SB4], 128), bnd, m3(GAM, 128),
                       ALU.subtract, ["d_argt"], ["d_argt"])
                    tsc(argt[:, 1 * SB4:5 * SB4], argt[:, 1 * SB4:5 * SB4], 0.0,
                        ALU.max, ["d_argt"], ["d_argt"])
                    tsc(argt[:, 6 * SB4:10 * SB4], argt[:, 6 * SB4:10 * SB4], 0.0,
                        ALU.max, ["d_argt"], ["d_argt"])
                    act(expt[:, :], argt[:, :], AF.Exp, ["d_argt", "gchan"],
                        ["d_argt"], scale=gchan[:, h:h + 1])
                    cpy(gend[:, r * CB:(r + 1) * CB],
                        expt[:, 11 * SB4 + 127:12 * SB4:128], ["d_argt"],
                        ["d_gend"])
                    mm(pm[0][0:128, 0:SB4], rowsel[0:NR, bi * 128:bi * 128 + 128],
                       (bfT if r == 0 else bsT)[0:NR, h * 512:(h + 1) * 512],
                       ["rowsel", "bfT" if r == 0 else "bsT"], ["pm0"])
                    tt(kbt[:, :], kup[:, :], pm[0][0:128, 0:SB4], ALU.mult,
                       ["d_kup", "pm0"], ["d_kbt"])
                    kbr = kbt[:, :].unsqueeze(1).broadcast_to([128, 5, SB4])
                    kur = kup[:, :].unsqueeze(1).broadcast_to([128, 5, SB4])
                    qar = qap[:, :].unsqueeze(1).broadcast_to([128, 5, SB4])
                    e5 = expt[:, 0:5 * SB4].rearrange("p (l t) -> p l t", t=SB4)
                    g5 = expt[:, 5 * SB4:10 * SB4].rearrange("p (l t) -> p l t", t=SB4)
                    tt(fkt[:, :].rearrange("p (l t) -> p l t", t=SB4), e5, kbr,
                       ALU.mult, ["d_argt", "d_kbt"], ["d_fkt"])
                    stt(fqt[:, :].rearrange("p (l t) -> p l t", t=SB4), e5, DKS,
                        qar, ALU.mult, ALU.mult, ["d_argt", "d_qap"], ["d_fqt"])
                    tt(gt[:, :].rearrange("p (l t) -> p l t", t=SB4), g5, kur,
                       ALU.mult, ["d_argt", "d_kup"], ["d_gt"])
                    tt(khf[:, :], expt[:, 10 * SB4:11 * SB4], kup[:, :], ALU.mult,
                       ["d_argt", "d_kup"], ["d_khf"])
                    for c4 in range(CB):
                        tr(pm[1][0:128, c4 * 128:c4 * 128 + 128],
                           khf[:, c4 * 128:c4 * 128 + 128], 128, ["d_khf"], ["pm1"])
                    cpy(khTt[:, r * SB4:(r + 1) * SB4], pm[1][0:128, 0:SB4],
                        ["pm1"], ["d_khTt"])
                    tt(kck[:, r * SB4:(r + 1) * SB4], expt[:, 11 * SB4:12 * SB4],
                       kup[:, :], ALU.mult, ["d_argt", "d_kup"], ["d_kck"])
                    stt(qck[:, r * SB4:(r + 1) * SB4], expt[:, 11 * SB4:12 * SB4],
                        DKS, qap[:, :], ALU.mult, ALU.mult, ["d_argt", "d_qap"],
                        ["d_qck"])
                    if h == 0 and bi == 0 and r == 0:
                        dbgw("d_arg", argt[:, :], ["d_argt"])
                        dbgw("d_kck", kck[:, 0:SB4], ["d_kck"])
                        dbgw("d_qck", qck[:, 0:SB4], ["d_qck"])
                        dbgw("d_khT", khTt[:, 0:SB4], ["d_khTt"])
                        dbgw("d_vT4", vT4[:, :], ["d_vT4"])
                        dbgw("d_fkt", fkt[:, :], ["d_fkt"])
                        dbgw("d_gt", gt[:, :], ["d_gt"])
                    for dst, dk, lh, lk, rh, rk_, f8 in (
                            (Mt, "d_Mt", fkt, "d_fkt", gt, "d_gt", 0),
                            (MTt, "d_MTt", gt, "d_gt", fkt, "d_fkt", 1),
                            (Xt, "d_Xt", gt, "d_gt", fqt, "d_fqt", 2)):
                        for li, w in enumerate(LVLS):
                            for c4 in range(CB):
                                o = li * SB4 + c4 * 128
                                mm(pm[0][0:128, c4 * 128:c4 * 128 + 128],
                                   lh[:, o:o + 128], rh[:, o:o + 128],
                                   [lk, rk_], ["pm0"])
                            dsl = dst[:, r * SB4:(r + 1) * SB4]
                            if li == 0:
                                mrep = masksf[:, f8 * 512:f8 * 512 + 512]
                                tt(dsl, pm[0][0:128, 0:SB4], mrep, ALU.mult,
                                   ["pm0", "masks"], [dk])
                            else:
                                mu_ = (li - 1) if f8 == 0 else (3 + li)
                                mrep = masksu[:, mu_ * 512:mu_ * 512 + 512]
                                cpred(dsl, mrep, pm[0][0:128, 0:SB4],
                                      ["pm0", "masks", dk], [dk])
                    if h == 0 and bi == 0 and r == 0:
                        dbgw("d_M", Mt[:, 0:SB4], ["d_Mt"])
                        dbgw("d_MT", MTt[:, 0:SB4], ["d_MTt"])
                        dbgw("d_X", Xt[:, 0:SB4], ["d_Xt"])
                    for c4 in range(CB):
                        base = r * SB4 + c4 * 128
                        prev, prevk = Mt[:, base:base + 128], "d_Mt"
                        prevT, prevTk = MTt[:, base:base + 128], "d_MTt"
                        pslot = (r * CB + c4) * 768
                        for pi in range(6):
                            mm(pm[1][0:128, 0:128], prev, prevT, [prevk, prevTk],
                               ["pm1"])
                            if pi < 5:
                                mm(pm[0][0:128, 0:128], prevT, prev,
                                   [prevk, prevTk], ["pm0"])
                                cpy(powm[:, pi * 128:pi * 128 + 128],
                                    pm[0][0:128, 0:128], ["pm0"], ["d_powm"])
                            eng = "vector" if pi % 2 == 0 else "scalar"
                            if eng == "vector":
                                cpy(powt[:, pslot + pi * 128:pslot + pi * 128 + 128],
                                    pm[1][0:128, 0:128], ["pm1"], ["d_powt"])
                            else:
                                act(powt[:, pslot + pi * 128:pslot + pi * 128 + 128],
                                    pm[1][0:128, 0:128], AF.Copy, ["pm1"],
                                    ["d_powt"])
                            if pi < 5:
                                prev = powm[:, pi * 128:pi * 128 + 128]
                                prevk = "d_powm"
                                prevT = powt[:, pslot + pi * 128:pslot + pi * 128 + 128]
                                prevTk = "d_powt"
                for c4 in range(CB):
                    cn = bi * CB + c4
                    for r in range(2):
                        base = r * SB4 + c4 * 128
                        pslot = (r * CB + c4) * 768
                        Ssl = Sst[:, (h * 2 + r) * 128:(h * 2 + r) * 128 + 128]
                        cb_ = (h * NCH + cn) * 5
                        bcol = colpk[:, cb_ + r:cb_ + r + 1]
                        lcol = colpk[:, cb_ + 2:cb_ + 3]
                        olcol = omlam[:, h * NCH + cn:h * NCH + cn + 1]
                        gcol = gend[:, r * CB + c4:r * CB + c4 + 1]
                        mm(pm[0][0:128, 0:128], kck[:, base:base + 128], Ssl,
                           ["d_kck", "Sst"], ["pm0"])
                        tt(Yt[:, :], vT4[:, c4 * 128:c4 * 128 + 128],
                           pm[0][0:128, 0:128], ALU.subtract, ["d_vT4", "pm0"],
                           ["d_Yt"])
                        tsc(Yt[:, :], Yt[:, :], bcol, ALU.mult,
                            ["d_Yt", "colpk"], ["d_Yt"])
                        cur, curk = Yt[:, :], "d_Yt"
                        buf = [(Wt0[:, :], "d_Wt0"), (Wt1[:, :], "d_Wt1")]
                        bidx = 0
                        for pi in (5, 4, 3, 2, 1, 0):
                            mm(pm[1][0:128, 0:128],
                               powt[:, pslot + pi * 128:pslot + pi * 128 + 128],
                               cur, ["d_powt", curk], ["pm1"])
                            nxt, nxtk = buf[bidx]
                            tt(nxt[:, :], cur, pm[1][0:128, 0:128], ALU.add,
                               [curk, "pm1"], [nxtk])
                            cur, curk = nxt, nxtk
                            bidx ^= 1
                        mm(pm[1][0:128, 0:128], MTt[:, base:base + 128], cur,
                           ["d_MTt", curk], ["pm1"])
                        nxt, nxtk = buf[bidx]
                        tt(nxt[:, :], cur, pm[1][0:128, 0:128], ALU.subtract,
                           [curk, "pm1"], [nxtk])
                        Wf, Wfk = nxt, nxtk
                        if h == 0 and bi == 0 and c4 == 0 and r == 0:
                            dbgw("d_Y", Yt[:, :], ["d_Yt"])
                            dbgw("d_W", Wf[:, :], [Wfk])
                            dbgw("d_powt", powt[:, 0:768], ["d_powt"])
                        mm(pm[0][0:128, 0:128], Xt[:, base:base + 128], Wf[:, :],
                           ["d_Xt", Wfk], ["pm0"], start=True, stop=False)
                        mm(pm[0][0:128, 0:128], qck[:, base:base + 128], Ssl,
                           ["d_qck", "Sst"], ["pm0"], start=False, stop=True)
                        if r == 0:
                            tsc(ocm[:, :], pm[0][0:128, 0:128], lcol, ALU.mult,
                                ["pm0", "colpk"], ["d_ocm"])
                        else:
                            stt(ocm[:, :], pm[0][0:128, 0:128], olcol, ocm[:, :],
                                ALU.mult, ALU.add, ["pm0", "omlam", "d_ocm"],
                                ["d_ocm"])
                        mm(pm[1][0:128, 0:128], khTt[:, base:base + 128],
                           Wf[:, :], ["d_khTt", Wfk], ["pm1"])
                        stt(Ssl, Ssl, gcol, pm[1][0:128, 0:128], ALU.mult,
                            ALU.add, ["Sst", "d_gend", "pm1"], ["Sst"])
                        if h == 0 and bi == 0 and c4 == 0 and r == 0:
                            dbgw("d_S1", Ssl, ["Sst"])
                    P.dma("scalar", cc_o_in[h, cn * 128:cn * 128 + 128, :],
                          ocm[:, :], reads=["d_ocm"], writes=["ccoin"])
            P.barrier()

    def build_phase4():
        rnk = sb("rnk", 1, 1, I32)
        P.dma("sync", rnk[:, :], rank_i, writes=["rnk"])
        greg = ctx.enter_context(nc.gpsimd.register("gr_rank"))
        greg2 = ctx.enter_context(nc.gpsimd.register("gr_off"))
        P.wait_keys("gpsimd", ["rnk", "ccoout"])
        P.raw("gpsimd", lambda e: e.reg_load(greg, rnk[0:1, 0:1]))
        P.raw("gpsimd", lambda e: e.reg_mul(greg, greg, TC * DV))
        with contextlib.ExitStack() as c5:
            otok = sb("otok", 128, OSEG, cx=c5)
            sqo = sb("sqo", 128, 128, cx=c5)
            rmsr = sb("rmsr", 128, 16 * (OSEG // 128), cx=c5)
            xg_ = sb("xg_", 128, OSEG, cx=c5)
            sgg = sb("sgg", 128, OSEG, cx=c5)
            ogT = sb("ogT", 128, 16 * OSEG, cx=c5)
            x2T = sb("x2T", 128, 16 * OSEG, cx=c5)
            h2hi = sb("h2hi", 128, 16 * OSEG, BF16, cx=c5)
            h2lo = sb("h2lo", 128, 16 * OSEG, BF16, cx=c5)
            p4pool = sb("p4pool", 128, max(16 * OSEG, 8192), cx=c5)
            fhi = p4pool[:, 0:8 * OSEG].bitcast(BF16)
            flo = p4pool[:, 8 * OSEG:16 * OSEG].bitcast(BF16)
            dacc = ogT
            wos = p4pool[:, 4096:6144]
            g1s = p4pool[:, 0:2048]
            g2s = p4pool[:, 2048:4096]
            g2bs = sb("g2bs", 128, 16, cx=c5)
            sgc = sb("sgc", 128, 64, cx=c5)
            suc = sb("suc", 128, 64, cx=c5)
            wga = sb("wga", 128, 16 * 128, BF16, cx=c5)
            wua = sb("wua", 128, 16 * 128, BF16, cx=c5)
            wda = sb("wda", 128, 16 * 128, BF16, cx=c5)
            wgb_, wub_, wdb_ = wga, wua, wda
            xs0 = sb("p4x", 128, OSEG, cx=c5)
            ht0 = sb("p4h", 128, OSEG, cx=c5)
            bc1 = sb("p4b1", 128, OSEG, cx=c5)
            bc2 = sb("p4b2", 128, OSEG, cx=c5)
            pt1 = sb("p4t1", 128, OSEG, cx=c5)
            mrow = sb("p4mr", 1, OSEG, cx=c5)
            rrow2 = sb("p4rr", 1, OSEG, cx=c5)
            outt = sb("outt", 128, 128, cx=c5)
            P.dma("sync", g2bs[:, :], g2b_i, writes=["g2bs"])
            P.dma("sync", sgc[:, :], sg_i, writes=["sgc"])
            P.dma("sync", suc[:, :], su_i, writes=["suc"])
            for os_ in range(NOSEG):
                o0 = os_ * OSEG
                sro, cco = o0 // 512, o0 % 512
                P.raw("gpsimd", lambda e, o=o0 * DV: e.reg_add(greg2, greg, o))
                P.dma("gpsimd", o_own,
                      (cc_o_out_h, [[T * DV, 16], [DV, OSEG], [1, DV]]),
                      reads=["ccoout"], writes=["o_own"], offset_reg=greg2)
                P.dma("sync", g1s[:, :], g1_i, writes=["p4pool"])
                P.dma("sync", g2s[:, :], g2_i, writes=["p4pool"])
                bc(bc1[:, :], "p4b1", mur, "mur", sro, cco, OSEG, NRo)
                bc(bc2[:, :], "p4b2", rr_, "rr_", sro, cco, OSEG, NRo)
                for dc in range(16):
                    P.dma("sync", xs0[:, :], x_ownT[dc, :, o0:o0 + OSEG],
                          writes=["p4x"])
                    tt(ht0[:, :], xs0[:, :], bc1[:, :], ALU.subtract,
                       ["p4x", "p4b1"], ["p4h"])
                    tt(ht0[:, :], ht0[:, :], bc2[:, :], ALU.mult,
                       ["p4h", "p4b2"], ["p4h"])
                    tsc(ht0[:, :], ht0[:, :], ca1[:, dc:dc + 1], ALU.mult,
                        ["p4h", "ca1", "modv"], ["p4h"],
                        s2=modv[:, dc:dc + 1], op1=ALU.add)
                    mm(pb[3][0:128, 0:OSEG], g1s[:, dc * 128:dc * 128 + 128],
                       ht0[:, :], ["p4pool", "p4h"], ["pb3"],
                       start=(dc == 0), stop=(dc == 15))
                cpy(xg_[:, :], pb[3][0:128, 0:OSEG], ["pb3"], ["xg_"])
                for hv in range(16):
                    mm(pb[4][0:128, 0:OSEG], g2s[:, hv * 128:hv * 128 + 128],
                       xg_[:, :], ["p4pool", "xg_"], ["pb4"])
                    act(sgg[:, :], pb[4][0:128, 0:OSEG], AF.Sigmoid,
                        ["pb4", "g2bs"], ["sgg"], bias=g2bs[:, hv:hv + 1])
                    for tb in range(OSEG // 128):
                        P.dma("sync", otok[:, tb * 128:tb * 128 + 128],
                              o_own[hv, tb * 128 * DV:(tb + 1) * 128 * DV]
                              .rearrange("(t v) -> t v", v=DV),
                              reads=["o_own"], writes=["otok"])
                        act(sqo[:, :], otok[:, tb * 128:tb * 128 + 128],
                            AF.Square, ["otok"], ["sqo", "rmsr"],
                            accum=rmsr[:, hv * (OSEG // 128) + tb:
                                       hv * (OSEG // 128) + tb + 1])
                    rsl = rmsr[:, hv * (OSEG // 128):(hv + 1) * (OSEG // 128)]
                    tsc(rsl, rsl, 1.0 / DV, ALU.mult, ["rmsr"], ["rmsr"],
                        s2=NORM_EPS, op1=ALU.add)
                    act(rsl, rsl, AF.Sqrt, ["rmsr"], ["rmsr"])
                    recip(rsl, rsl, ["rmsr"], ["rmsr"])
                    for tb in range(OSEG // 128):
                        stt(otok[:, tb * 128:tb * 128 + 128],
                            otok[:, tb * 128:tb * 128 + 128],
                            rmsr[:, hv * (OSEG // 128) + tb:
                                 hv * (OSEG // 128) + tb + 1],
                            onbc[:, :], ALU.mult, ALU.mult,
                            ["otok", "rmsr", "onbc"], ["otok"])
                        tr(pm[0][0:128, tb * 128:tb * 128 + 128],
                           otok[:, tb * 128:tb * 128 + 128], 128, ["otok"],
                           ["pm0"])
                    cpy(ogT[:, hv * OSEG:(hv + 1) * OSEG],
                        pm[0][0:128, 0:OSEG], ["pm0"], ["ogT"])
                    tt(ogT[:, hv * OSEG:(hv + 1) * OSEG],
                       ogT[:, hv * OSEG:(hv + 1) * OSEG], sgg[:, :], ALU.mult,
                       ["ogT", "sgg"], ["ogT"])
                for dc in range(16):
                    P.dma("sync", wos[:, :], wo_i[dc], writes=["p4pool"])
                    for hv in range(16):
                        mm(pb[5][0:128, 0:OSEG], wos[:, hv * 128:hv * 128 + 128],
                           ogT[:, hv * OSEG:(hv + 1) * OSEG], ["p4pool", "ogT"],
                           ["pb5"], start=(hv == 0), stop=(hv == 15))
                    P.dma("sync", xs0[:, :], x_ownT[dc, :, o0:o0 + OSEG],
                          writes=["p4x"])
                    stt(x2T[:, dc * OSEG:(dc + 1) * OSEG], pb[5][0:128, 0:OSEG],
                        gacl[:, dc:dc + 1], xs0[:, :], ALU.mult, ALU.add,
                        ["pb5", "gacl", "p4x"], ["x2T"])
                if os_ == 0:
                    dbgw("x2T", x2T[:, :], ["x2T"])
                for dc in range(16):
                    xsl = x2T[:, dc * OSEG:(dc + 1) * OSEG]
                    act(pt1[:, :], xsl, AF.Square, ["x2T"], ["p4t1"])
                    mm(pb[0][0:1, 0:OSEG], ones[:, 0:1], xsl, ["ones", "x2T"],
                       ["pb0"], start=(dc == 0), stop=(dc == 15))
                    mm(pb[1][0:1, 0:OSEG], ones[:, 0:1], pt1[:, :],
                       ["ones", "p4t1"], ["pb1"], start=(dc == 0), stop=(dc == 15))
                tsc(mrow[0:1, :], pb[0][0:1, 0:OSEG], 1.0 / D, ALU.mult,
                    ["pb0"], ["p4mr"])
                tsc(rrow2[0:1, :], pb[1][0:1, 0:OSEG], 1.0 / D, ALU.mult,
                    ["pb1"], ["p4rr"])
                tt(pt1[0:1, :], mrow[0:1, :], mrow[0:1, :], ALU.mult,
                   ["p4mr"], ["p4t1"])
                tt(rrow2[0:1, :], rrow2[0:1, :], pt1[0:1, :], ALU.subtract,
                   ["p4rr", "p4t1"], ["p4rr"])
                tsc(rrow2[0:1, :], rrow2[0:1, :], LN_EPS, ALU.add,
                    ["p4rr"], ["p4rr"])
                act(rrow2[0:1, :], rrow2[0:1, :], AF.Sqrt, ["p4rr"], ["p4rr"])
                recip(rrow2[0:1, :], rrow2[0:1, :], ["p4rr"], ["p4rr"])
                mm(pm[1][0:128, 0:OSEG], ones[0:1, 0:128], mrow[0:1, 0:OSEG],
                   ["ones", "p4mr"], ["pm1"])
                cpy(bc1[:, :], pm[1][0:128, 0:OSEG], ["pm1"], ["p4b1"])
                mm(pm[1][0:128, 0:OSEG], ones[0:1, 0:128], rrow2[0:1, 0:OSEG],
                   ["ones", "p4rr"], ["pm1"])
                cpy(bc2[:, :], pm[1][0:128, 0:OSEG], ["pm1"], ["p4b2"])
                for dc in range(16):
                    xsl = x2T[:, dc * OSEG:(dc + 1) * OSEG]
                    tt(pt1[:, :], xsl, bc1[:, :], ALU.subtract,
                       ["x2T", "p4b1"], ["p4t1"])
                    tt(pt1[:, :], pt1[:, :], bc2[:, :], ALU.mult,
                       ["p4t1", "p4b2"], ["p4t1"])
                    tsc(pt1[:, :], pt1[:, :], cb1[:, dc:dc + 1], ALU.mult,
                        ["p4t1", "cb1", "modv"], ["p4t1"],
                        s2=modv[:, 48 + dc:48 + dc + 1], op1=ALU.add)
                    cpy(h2hi[:, dc * OSEG:(dc + 1) * OSEG], pt1[:, :],
                        ["p4t1"], ["h2hi"], eng="scalar")
                    tt(h2lo[:, dc * OSEG:(dc + 1) * OSEG], pt1[:, :],
                       h2hi[:, dc * OSEG:(dc + 1) * OSEG], ALU.subtract,
                       ["p4t1", "h2hi"], ["h2lo"])
                for icb in range(4):
                    for iq in range(16):
                        ici = icb * 16 + iq
                        wgt, wgk = wga, "wga"
                        wut, wuk = wua, "wua"
                        P.dma("gpsimd", wgt[:, :], wg8[ici], writes=[wgk])
                        P.dma("gpsimd", wut[:, :], wu8[ici], writes=[wuk])
                        for dc in range(16):
                            hsl = h2hi[:, dc * OSEG:(dc + 1) * OSEG]
                            lsl = h2lo[:, dc * OSEG:(dc + 1) * OSEG]
                            wsl = wgt[:, dc * 128:dc * 128 + 128]
                            usl = wut[:, dc * 128:dc * 128 + 128]
                            mm(pb[0][0:128, 0:OSEG], wsl, hsl, [wgk, "h2hi"],
                               ["pb0"], start=(dc == 0), stop=False)
                            mm(pb[0][0:128, 0:OSEG], wsl, lsl, [wgk, "h2lo"],
                               ["pb0"], start=False, stop=(dc == 15))
                            mm(pb[1][0:128, 0:OSEG], usl, hsl, [wuk, "h2hi"],
                               ["pb1"], start=(dc == 0), stop=False)
                            mm(pb[1][0:128, 0:OSEG], usl, lsl, [wuk, "h2lo"],
                               ["pb1"], start=False, stop=(dc == 15))
                        act(sgg[:, :], pb[0][0:128, 0:OSEG], AF.Silu,
                            ["pb0", "sgc"], ["sgg"], scale=sgc[:, ici:ici + 1])
                        stt(pt1[:, :], pb[1][0:128, 0:OSEG], suc[:, ici:ici + 1],
                            sgg[:, :], ALU.mult, ALU.mult,
                            ["pb1", "suc", "sgg"], ["p4t1"])
                        cpy(fhi[:, iq * OSEG:(iq + 1) * OSEG], pt1[:, :],
                            ["p4t1"], ["p4pool"], eng="scalar")
                        tt(flo[:, iq * OSEG:(iq + 1) * OSEG], pt1[:, :],
                           fhi[:, iq * OSEG:(iq + 1) * OSEG], ALU.subtract,
                           ["p4t1", "p4pool"], ["p4pool"])
                    for dcol in range(16):
                        wdt, wdk = wda, "wda"
                        P.dma("gpsimd", wdt[:, :], wd8[dcol, icb], writes=[wdk])
                        pk = 2 + (dcol % 4)
                        for iq in range(16):
                            wsl = wdt[:, iq * 128:iq * 128 + 128]
                            mm(pb[pk][0:128, 0:OSEG], wsl,
                               fhi[:, iq * OSEG:(iq + 1) * OSEG], [wdk, "p4pool"],
                               [PBK[pk]], start=(iq == 0), stop=False)
                            mm(pb[pk][0:128, 0:OSEG], wsl,
                               flo[:, iq * OSEG:(iq + 1) * OSEG], [wdk, "p4pool"],
                               [PBK[pk]], start=False, stop=(iq == 15))
                        dsl = dacc[:, dcol * OSEG:(dcol + 1) * OSEG]
                        if icb == 0:
                            cpy(dsl, pb[pk][0:128, 0:OSEG], [PBK[pk]], ["ogT"])
                        else:
                            tt(dsl, dsl, pb[pk][0:128, 0:OSEG], ALU.add,
                               ["ogT", PBK[pk]], ["ogT"])
                for dcol in range(16):
                    stt(pt1[:, :], dacc[:, dcol * OSEG:(dcol + 1) * OSEG],
                        sdgb[:, dcol:dcol + 1],
                        x2T[:, dcol * OSEG:(dcol + 1) * OSEG], ALU.mult, ALU.add,
                        ["ogT", "sdgb", "x2T"], ["p4t1"])
                    for tb in range(OSEG // 128):
                        tr(pm[0][0:128, 0:128], pt1[:, tb * 128:tb * 128 + 128],
                           128, ["p4t1"], ["pm0"])
                        cpy(outt[:, :], pm[0][0:128, 0:128], ["pm0"], ["outt"])
                        P.dma("sync", out_sl[o0 + tb * 128:o0 + tb * 128 + 128,
                                             dcol * 128:dcol * 128 + 128],
                              outt[:, :], reads=["outt"], writes=["outdma"])
            P.barrier()

    with contextlib.ExitStack() as cdel:
        colpk = sb("colpk", 128, 5 * NCH * HC, cx=cdel)
        omlam = sb("omlam", 128, NCH * HC, cx=cdel)
        rknT = sb("rknT", NR, 512 * HC, cx=cdel)
        bfT = sb("bfT", NR, 512 * HC, cx=cdel)
        bsT = sb("bsT", NR, 512 * HC, cx=cdel)
        arT = sb("arT", NR, 512 * HC * 2, cx=cdel)
        Sst = sb("Sst", 128, HC * 2 * 128, cx=cdel)
        mems(Sst[:, :], 0.0, ["Sst"])
        for h in range(HC):
            build_feats(h)
            build_delta(h)
        dbgw("colpk", colpk[:, :], ["colpk"])
        dbgw("o_comb", cc_o_in, ["ccoin"])
        P.collective("AllGather", RG, cc_o_in.opt(), cc_o_out.opt(),
                     reads=["ccoin"], writes=["ccoout"])
        P.barrier()
    cmid.close()
    build_phase4()
    P.barrier()
    with nc.Block() as block:
        P.finalize(block)
    ctx.close()
    return nc


def stage_inputs(inputs, T):
    ident, ones_, stag, rowsel, masksf, masksu = host_consts()
    TC = T // NCORE
    WS = 12288 // NCORE
    f = np.float32
    x = np.asarray(inputs["L0_attn_hidden_states"], f)[0]
    wq = np.asarray(inputs["L0_attn_wq"], f)
    wk = np.asarray(inputs["L0_attn_wk"], f)
    wv = np.asarray(inputs["L0_attn_wv"], f)
    cq = np.asarray(inputs["L0_attn_conv_q_w"], f)
    ck = np.asarray(inputs["L0_attn_conv_k_w"], f)
    cv = np.asarray(inputs["L0_attn_conv_v_w"], f)
    alog = np.asarray(inputs["L0_attn_A_log"], f)
    dtb = np.asarray(inputs["L0_attn_dt_bias"], f)
    proxy = np.asarray(inputs["L0_attn_proxy_w"], f)
    uncw = np.asarray(inputs["L0_attn_unc_w"], f)
    he = np.asarray(inputs["L0_attn_head_embed"], f)
    w1 = np.asarray(inputs["L0_attn_mlp_w1"], f)
    b1 = np.asarray(inputs["L0_attn_mlp_b1"], f)
    w2 = np.asarray(inputs["L0_attn_mlp_w2"], f)
    b2 = np.asarray(inputs["L0_attn_mlp_b2"], f)
    w5 = np.stack([np.asarray(inputs["L0_attn_" + n], f)
                   for n in ("bb_w", "bd_w", "lam_w", "ab_w", "ad_w")], 1)
    b5 = np.stack([np.asarray(inputs["L0_attn_" + n], f)[0]
                   for n in ("bb_b", "bd_b", "lam_b", "ab_b", "ad_b")])[None]
    g1 = np.asarray(inputs["L0_attn_g1_w"], f)
    g2 = np.asarray(inputs["L0_attn_g2_w"], f)
    g2b = np.asarray(inputs["L0_attn_g2_b"], f)
    onorm = np.asarray(inputs["L0_attn_onorm_w"], f)
    wo = np.asarray(inputs["L0_attn_wo"], f)
    wg = np.asarray(inputs["L0_ffn_w_gate"])
    wu = np.asarray(inputs["L0_ffn_w_up"])
    wd = np.asarray(inputs["L0_ffn_w_down"])
    sgv = np.asarray(inputs["L0_ffn_s_gate"], f)
    suv = np.asarray(inputs["L0_ffn_s_up"], f)
    sdv = np.asarray(inputs["L0_ffn_s_down"], f)
    cvec = np.asarray(inputs["L0_c"], f)
    wmod = np.asarray(inputs["L0_wmod"], f)

    g1h = np.ascontiguousarray(
        g1.reshape(16, 128, 128).transpose(1, 0, 2).reshape(128, 2048))
    g2h = np.ascontiguousarray(g2.reshape(128, 2048))
    g2bh = np.ascontiguousarray(g2b.reshape(16, 128).T)
    woh = np.ascontiguousarray(
        wo.reshape(16, 128, 16, 128).transpose(2, 1, 0, 3).reshape(16, 128, 2048))
    wgh = np.ascontiguousarray(
        wg.reshape(16, 128, 64, 128).transpose(2, 1, 0, 3).reshape(64, 128, 2048))
    wuh = np.ascontiguousarray(
        wu.reshape(16, 128, 64, 128).transpose(2, 1, 0, 3).reshape(64, 128, 2048))
    wdh = np.ascontiguousarray(
        wd.reshape(4, 16, 128, 16, 128).transpose(3, 0, 2, 1, 4)
        .reshape(16, 4, 128, 2048))
    sgh = np.ascontiguousarray(sgv.reshape(64, 128).T)
    suh = np.ascontiguousarray(suv.reshape(64, 128).T)
    sdh = np.ascontiguousarray(sdv.reshape(16, 128).T)
    unch = np.ascontiguousarray(
        uncw.reshape(16, 128, 64).transpose(1, 0, 2).reshape(128, 1024))
    cvh = np.ascontiguousarray(cvec[0].reshape(16, 128).T)

    in_maps = []
    for c in range(NCORE):
        hc0 = c * HC * 128
        wq_c = wq[:, hc0:hc0 + HC * 128].reshape(16, 128, HC, 128)
        wk_c = wk[:, hc0:hc0 + HC * 128].reshape(16, 128, HC, 128)
        wv_c = wv[:, hc0:hc0 + HC * 128].reshape(16, 128, HC, 128)
        wqkv_c = np.stack([wq_c, wk_c, wv_c], 1)
        wqkv_c = np.ascontiguousarray(
            wqkv_c.transpose(2, 0, 1, 3, 4).reshape(128, -1))
        convw_c = np.stack([cq[hc0:hc0 + HC * 128].reshape(HC, 128, 4),
                            ck[hc0:hc0 + HC * 128].reshape(HC, 128, 4),
                            cv[hc0:hc0 + HC * 128].reshape(HC, 128, 4)], 0)
        convw_c = np.ascontiguousarray(
            convw_c.transpose(3, 0, 1, 2).reshape(-1, 3 * HC * 4, order="F")
            if False else
            convw_c.transpose(1, 0, 3, 2).reshape(128, -1))
        convw_c = np.ascontiguousarray(
            np.stack([cq[hc0:hc0 + HC * 128], ck[hc0:hc0 + HC * 128],
                      cv[hc0:hc0 + HC * 128]], 0)
            .reshape(3, HC, 128, 4).transpose(2, 0, 1, 3).reshape(128, -1))
        m = {
            "x_ownT": np.ascontiguousarray(
                x[c * TC:(c + 1) * TC].T.reshape(16, 128, TC)),
            "wqkv": wqkv_c,
            "convw_i": convw_c,
            "alog": np.ascontiguousarray(alog[c * HC:(c + 1) * HC][None]),
            "dtb": np.ascontiguousarray(
                dtb[hc0:hc0 + HC * 128].reshape(HC, 128).T),
            "proxy": proxy,
            "uncw": unch,
            "he": np.ascontiguousarray(he[c * HC:(c + 1) * HC].T),
            "w1d_i": np.ascontiguousarray(w1[0:4]),
            "w1e_i": np.ascontiguousarray(w1[4:8]),
            "b1_i": np.ascontiguousarray(b1[:, None]),
            "w2_i": w2,
            "b2_i": np.ascontiguousarray(b2[:, None]),
            "w5_i": np.ascontiguousarray(w5),
            "b5_i": np.ascontiguousarray(b5),
            "g1_i": g1h,
            "g2_i": g2h,
            "g2b_i": g2bh,
            "onorm_i": np.ascontiguousarray(onorm[:, None]),
            "wo_i": woh,
            "wg8": wgh,
            "wu8": wuh,
            "wd8": wdh,
            "sg_i": sgh,
            "su_i": suh,
            "sd_i": sdh,
            "cvecT": cvh,
            "wmod_i": np.ascontiguousarray(
                wmod[:, c * WS:(c + 1) * WS].reshape(16, 128, WS)),
            "ident_i": ident,
            "ones_i": ones_,
            "stag_i": stag,
            "rowsel_i": rowsel,
            "masksf_i": masksf,
            "masksu_i": masksu,
            "rank_i": np.array([[c]], dtype=np.int32),
        }
        in_maps.append(m)
    return in_maps


def kernel(**inputs):
    T = int(np.asarray(inputs["L0_attn_hidden_states"]).shape[1])
    nc = build(T)
    in_maps = stage_inputs(inputs, T)
    res = run_bass_kernel_spmd(nc, in_maps, core_ids=list(range(NCORE)))
    out = np.concatenate(
        [np.asarray(res.results[i]["out_sl"]) for i in range(NCORE)], 0)
    return out.reshape(1, T, D).astype(np.float32)


# version 07
# speedup vs baseline: 1.3053x; 1.3053x over previous
import contextlib
import numpy as np
import concourse.bass as bass
import concourse.mybir as mybir
from concourse.bass_utils import run_bass_kernel_spmd

F32 = mybir.dt.float32
BF16 = mybir.dt.bfloat16
I8 = mybir.dt.int8
I32 = mybir.dt.int32
AF = mybir.ActivationFunctionType
ALU = mybir.AluOpType
AX = mybir.AxisListType

NCORE = 8
D = 2048
H = 16
HC = H // NCORE
DK = 128
DV = 128
BINS = 64
CH = 128
CB = 4
EPS = 1e-6
NORM_EPS = 1e-5
LN_EPS = 1e-5
DKS = float(DK) ** -0.5
LVLS = (8, 16, 32, 64, 128)
LOG64 = float(np.log(64.0))


class Prog:

    ENGS = ("tensor", "vector", "scalar", "gpsimd", "sync")
    QUEUES = ("sync", "scalar", "gpsimd")

    def __init__(self, nc):
        self.nc = nc
        self.thunks = []
        self.cnt = {e: 0 for e in self.ENGS}
        self.dcnt = {q: 0 for q in self.QUEUES}
        self.waited = {}
        self.writer = {}
        self.readers = {}
        self.sems = {}
        self.dsems = {}

    NDS = 8

    def init_sems(self, ctx):
        for e in self.ENGS:
            self.sems[e] = ctx.enter_context(self.nc.semaphore(f"se_{e}"))
        for q in self.QUEUES:
            self.dsems[q] = [ctx.enter_context(self.nc.semaphore(f"sd_{q}{j}"))
                             for j in range(self.NDS)]
        self.duse = {q: [0] * self.NDS for q in self.QUEUES}

    def _wait(self, eng, dep, allow_self=False):
        kind, name, cnt = dep
        if kind == "e" and name == eng and not allow_self:
            return
        if kind == "e":
            key = (eng, kind, name)
            if self.waited.get(key, 0) >= cnt:
                return
            self.waited[key] = cnt
            sem = self.sems[name]
            self.thunks.append((eng, lambda e, s=sem, t=cnt: e.wait_ge(s, t)))
        else:
            q, slot = name, (cnt - 1) % self.NDS
            tgt = 16 * ((cnt - 1) // self.NDS + 1)
            key = (eng, "d", q, slot)
            if self.waited.get(key, 0) >= tgt:
                return
            self.waited[key] = tgt
            sem = self.dsems[q][slot]
            self.thunks.append((eng, lambda e, s=sem, t=tgt: e.wait_ge(s, t)))

    def _record(self, owner, reads, writes):
        for k in reads:
            self.readers.setdefault(k, []).append(owner)
        for k in writes:
            self.writer[k] = owner
            self.readers[k] = []

    def _deps(self, eng, reads, writes):
        for k in reads:
            w = self.writer.get(k)
            if w:
                self._wait(eng, w, allow_self=True)
        for k in writes:
            w = self.writer.get(k)
            if w:
                self._wait(eng, w)
            for r in self.readers.get(k, ()):
                self._wait(eng, r)

    def op(self, eng, fn, reads=(), writes=()):
        self._deps(eng, reads, writes)
        self.cnt[eng] += 1
        self.thunks.append(
            (eng, lambda e, f=fn, g=eng: f(e).then_inc(self.sems[g], 1)))
        self._record(("e", eng, self.cnt[eng]), reads, writes)

    def raw(self, eng, fn):
        self.thunks.append((eng, lambda e, f=fn: f(e)))

    def dma(self, queue, out, in_, reads=(), writes=(), offset_reg=None):
        self._deps(queue, reads, writes)
        self.dcnt[queue] += 1
        slot = (self.dcnt[queue] - 1) % self.NDS
        prev_uses = self.duse[queue][slot]
        self.duse[queue][slot] = prev_uses + 1
        sem = self.dsems[queue][slot]
        if prev_uses:
            key = (queue, "d", queue, slot)
            if self.waited.get(key, 0) < 16 * prev_uses:
                self.waited[key] = 16 * prev_uses
                self.thunks.append(
                    (queue, lambda e, s=sem, t=16 * prev_uses: e.wait_ge(s, t)))
        if offset_reg is None:
            self.thunks.append(
                (queue, lambda e, o=out, i=in_, s=sem:
                 e.dma_start(out=o, in_=i).then_inc(s, 16)))
        else:
            self.thunks.append(
                (queue, lambda e, o=out, i=in_, s=sem, r=offset_reg:
                 e.dma_start(out=o, in_=bass.AP(i[0], r, i[1]))
                 .then_inc(s, 16)))
        self._record(("d", queue, self.dcnt[queue]), reads, writes)

    def collective(self, kind, rg, in_ap, out_ap, reads=(), writes=()):
        self._deps("gpsimd", reads, writes)
        self.cnt["gpsimd"] += 1

        def thunk(e, k=kind, r=rg, i=in_ap, o=out_ap):
            e.collective_compute(k, ALU.bypass, replica_groups=r,
                                 ins=[i], outs=[o]).then_inc(self.sems["gpsimd"], 1)
        self.thunks.append(("gpsimd", thunk))
        self._record(("e", "gpsimd", self.cnt["gpsimd"]), reads, writes)

    def wait_keys(self, eng, keys):
        for k in keys:
            w = self.writer.get(k)
            if w:
                self._wait(eng, w)

    def barrier(self):
        for e in self.ENGS:
            for o in self.ENGS:
                if o != e and self.cnt[o] > 0:
                    self._wait(e, ("e", o, self.cnt[o]))
            for q in self.QUEUES:
                n = self.dcnt[q]
                for back in range(min(self.NDS, n)):
                    self._wait(e, ("d", q, n - back))

    def finalize(self, block):
        per = {e: [] for e in self.ENGS}
        for eng, thunk in self.thunks:
            per[eng].append(thunk)

        def run(e, lst):
            for t in lst:
                t(e)
        block.tensor(lambda e: run(e, per["tensor"]))
        block.vector(lambda e: run(e, per["vector"]))
        block.scalar(lambda e: run(e, per["scalar"]))
        block.gpsimd(lambda e: run(e, per["gpsimd"]))
        block.sync(lambda e: run(e, per["sync"]))


def host_consts():
    idx = np.arange(CH)
    ms = []
    for w in LVLS:
        if w == 8:
            m = (idx[:, None] // 8 == idx[None, :] // 8) & (idx[:, None] > idx[None, :])
        else:
            blk, half = idx // w, (idx % w) >= (w // 2)
            m = (blk[:, None] == blk[None, :]) & half[:, None] & (~half)[None, :]
        ms.append(m)
    for m in list(ms):
        ms.append(m.T)
    m8x = (idx[:, None] // 8 == idx[None, :] // 8) & (idx[:, None] >= idx[None, :])
    ms.append(m8x.T)
    allm = np.tile(np.stack(ms).astype(np.float32), (1, 1, 4))
    masksf = np.ascontiguousarray(allm[[0, 5, 10]])
    masksu = np.ascontiguousarray(allm[[1, 2, 3, 4, 6, 7, 8, 9]]).astype(np.uint8)
    ident = np.eye(128, dtype=np.float32)
    ones = np.ones((128, 128), dtype=np.float32)
    stag = np.zeros((128, 31), dtype=np.float32)
    stag[:, 15] = 1.0
    rowsel = np.zeros((16, 16 * 128), dtype=np.float32)
    for s in range(16):
        rowsel[s, s * 128:(s + 1) * 128] = 1.0
    return ident, ones, stag, rowsel, masksf, masksu


def build(T, debug=()):
    TC = T // NCORE
    SEG = min(512, TC)
    NSEG = T // SEG
    NCH = T // CH
    NB = NCH // CB
    SB4 = CB * CH
    OSEG = min(512, TC)
    NOSEG = TC // OSEG
    NR = (T + 511) // 512
    NRo = (TC + 511) // 512
    WS = 12288 // NCORE

    nc = bass.Bass("TRN2", target_bir_lowering=False,
                   detect_race_conditions=False)

    def inp(name, shape, dt=F32):
        return nc.dram_tensor(name, shape, dt, kind="ExternalInput").ap()

    x_ownT = inp("x_ownT", [16, 128, TC])
    wqkv = inp("wqkv", [128, 16 * 3 * HC * 128])
    convw_i = inp("convw_i", [128, 3 * HC * 4])
    alog = inp("alog", [1, HC])
    dtb = inp("dtb", [128, HC])
    proxy = inp("proxy", [128, 128])
    uncw = inp("uncw", [128, 16 * BINS])
    he = inp("he", [4, HC])
    w1d_i = inp("w1d_i", [4, 32])
    w1e_i = inp("w1e_i", [4, 32])
    b1_i = inp("b1_i", [32, 1])
    w2_i = inp("w2_i", [32, 32])
    b2_i = inp("b2_i", [32, 1])
    w5_i = inp("w5_i", [32, 5])
    b5_i = inp("b5_i", [1, 5])
    g1_i = inp("g1_i", [128, 16 * 128])
    g2_i = inp("g2_i", [128, 16 * 128])
    g2b_i = inp("g2b_i", [128, 16])
    onorm_i = inp("onorm_i", [128, 1])
    wo_i = inp("wo_i", [16, 128, 16 * 128])
    wg8 = inp("wg8", [64, 128, 16 * 128], I8)
    wu8 = inp("wu8", [64, 128, 16 * 128], I8)
    wd8 = inp("wd8", [16, 4, 128, 16 * 128], I8)
    sg_i = inp("sg_i", [128, 64])
    su_i = inp("su_i", [128, 64])
    sd_i = inp("sd_i", [128, 16])
    cvecT = inp("cvecT", [128, 16])
    wmod_i = inp("wmod_i", [16, 128, WS])
    ident_i = inp("ident_i", [128, 128])
    ones_i = inp("ones_i", [128, 128])
    stag_i = inp("stag_i", [128, 31])
    rowsel_i = inp("rowsel_i", [16, 16 * 128])
    masksf_i = inp("masksf_i", [3, 128, 512])
    masksu_i = inp("masksu_i", [8, 128, 512], mybir.dt.uint8)
    rank_i = inp("rank_i", [1, 1], I32)

    out_sl = nc.dram_tensor("out_sl", [TC, D], F32, kind="ExternalOutput").ap()
    dbg_out = {}
    for name, shape in debug:
        dbg_out[name] = nc.dram_tensor("dbg_" + name, list(shape), F32,
                                       kind="ExternalOutput").ap()

    cc_x_in = nc.dram_tensor("cc_x_in", [16, 128, TC], F32).ap()
    cc_x_out = nc.dram_tensor("cc_x_out", [NCORE, 16, 128, TC], F32).ap()
    cc_m_in = nc.dram_tensor("cc_m_in", [1, WS], F32).ap()
    cc_m_out = nc.dram_tensor("cc_m_out", [NCORE, WS], F32).ap()
    cc_s_in = nc.dram_tensor("cc_s_in", [3, TC], F32).ap()
    cc_s_out = nc.dram_tensor("cc_s_out", [NCORE, 3, TC], F32).ap()
    cc_o_in = nc.dram_tensor("cc_o_in", [HC, T, DV], F32).ap()
    cc_o_out_h = nc.dram_tensor("cc_o_out", [NCORE * HC * T, DV], F32)
    cc_o_out = cc_o_out_h.ap()
    o_own = nc.dram_tensor("o_own", [16, min(512, T // NCORE) * DV], F32).ap()
    qkv_pre = nc.dram_tensor("qkv_pre", [3, HC, 128, T], F32).ap()
    qkv_post = nc.dram_tensor("qkv_post", [3, HC, 128, T], F32).ap()

    RG = [list(range(NCORE))]
    ctx = contextlib.ExitStack()
    P = Prog(nc)
    P.init_sems(ctx)

    def sb(name, p, f, dt=F32, cx=None):
        return (cx or ctx).enter_context(nc.sbuf_tensor(name, [p, f], dt))

    pb = [ctx.enter_context(nc.psum_tensor(f"pb{i}", [128, 512], F32))
          for i in range(6)]
    pm = [ctx.enter_context(nc.psum_tensor(f"pm{i}", [128, 512], F32))
          for i in range(2)]
    PBK = [f"pb{i}" for i in range(6)]
    PMK = ["pm0", "pm1"]

    ident = sb("ident", 128, 128)
    ones = sb("ones", 128, 128)
    stag = sb("stag", 128, 31)
    rowsel = sb("rowsel", 16, 16 * 128)

    modv = sb("modv", 128, 96)
    ca1 = sb("ca1", 128, 16)
    saca = sb("saca", 128, 16)
    cb1 = sb("cb1", 128, 16)
    gacl = sb("gacl", 128, 16)
    sdgb = sb("sdgb", 128, 16)
    small = sb("small", 128, 64)
    gchan = sb("gchan", 128, HC)
    onorms = sb("onorms", 128, 1)
    onbc = sb("onbc", 128, 128)
    convw = sb("convw", 128, 3 * HC * 4)
    proxys = sb("proxys", 128, 128)
    W5b = sb("W5b", 33, 5)
    W5Z = sb("W5Z", 33, 4 * 31)
    w1ds = sb("w1ds", 4, 32)
    w1es = sb("w1es", 4, 32)
    w2s = sb("w2s", 32, 32)
    b1s = sb("b1s", 32, 1)
    b2s = sb("b2s", 32, 1)
    hes = sb("hes", 4, HC)
    b1e = sb("b1e", 32, HC)

    def mm(out, lhsT, rhs, reads, writes, start=True, stop=True):
        assert isinstance(out, bass.AP), ("mm out", out)
        assert isinstance(lhsT, bass.AP), ("mm lhsT", lhsT)
        assert isinstance(rhs, bass.AP), ("mm rhs", rhs)
        P.op("tensor", lambda e, o=out, l=lhsT, r=rhs, s1=start, s2=stop:
             e.matmul(o, l, r, start=s1, stop=s2), reads=reads, writes=writes)

    def tr(out, in_, k, reads, writes):
        P.op("tensor", lambda e, o=out, i=in_, d=ident[0:k, 0:k]:
             e.transpose(o, i, d), reads=list(reads) + ["ident"], writes=writes)

    def tt(out, a, b, op, reads, writes, eng="vector"):
        P.op(eng, lambda e, o=out, x=a, y=b, p=op: e.tensor_tensor(o, x, y, p),
             reads=reads, writes=writes)

    def tsc(out, in_, s1, op0, reads, writes, s2=None, op1=None, eng="vector"):
        if s2 is None:
            P.op(eng, lambda e, o=out, i=in_, s=s1, p=op0:
                 e.tensor_single_scalar(o, i, s, p), reads=reads, writes=writes)
        else:
            P.op(eng, lambda e, o=out, i=in_, a=s1, b=s2, p0=op0, p1=op1:
                 e.tensor_scalar(o, i, a, b, p0, p1), reads=reads, writes=writes)

    def stt(out, in0, scalar, in1, op0, op1, reads, writes, eng="vector"):
        P.op(eng, lambda e, o=out, i0=in0, s=scalar, i1=in1, p0=op0, p1=op1:
             e.scalar_tensor_tensor(o, i0, s, i1, p0, p1),
             reads=reads, writes=writes)

    def act(out, in_, func, reads, writes, bias=0.0, scale=1.0, accum=None):
        P.op("scalar", lambda e, o=out, i=in_, f=func, b=bias, s=scale, a=accum:
             e.activation(out=o, in_=i, func=f, bias=b, scale=s, accum_out=a),
             reads=reads, writes=writes)

    def cpy(out, in_, reads, writes, eng="vector"):
        if eng == "scalar":
            P.op(eng, lambda e, o=out, i=in_: e.copy(o, i),
                 reads=reads, writes=writes)
        else:
            P.op(eng, lambda e, o=out, i=in_: e.tensor_copy(o, i),
                 reads=reads, writes=writes)

    def cpred(out, mask, data, reads, writes):
        P.op("vector", lambda e, o=out, m=mask, d=data:
             e.copy_predicated(o, m, d), reads=reads, writes=writes)

    def mems(ap_, val, writes, eng="vector"):
        P.op(eng, lambda e, a=ap_, v=val: e.memset(a, v), reads=(), writes=writes)

    def scan(out, d0, d1, init, reads, writes):
        P.op("vector", lambda e, o=out, a=d0, b=d1, i=init:
             e.tensor_tensor_scan(o, a, b, i, ALU.add, ALU.add),
             reads=reads, writes=writes)

    def red(out, in_, op, reads, writes):
        P.op("vector", lambda e, o=out, i=in_, p=op:
             e.tensor_reduce(o, i, AX.X, p), reads=reads, writes=writes)

    def recip(out, in_, reads, writes):
        P.op("vector", lambda e, o=out, i=in_: e.reciprocal(o, i),
             reads=reads, writes=writes)

    def dbgw(name, src_ap, reads):
        if name in dbg_out:
            P.dma("sync", dbg_out[name], src_ap, reads=reads)

    def rowbc(dst, dstk, rowt, rowk, s, c0, width, pmk=1):
        mm(pm[pmk][0:128, 0:width], rowsel[0:NR if rowt is not None else NR, 0:0],
           None, [], [])

    def bc(dst, dstk, rowt, rowk, s, c0, width, nr):
        mm(pm[1][0:128, 0:width], rowsel[0:nr, s * 128:s * 128 + 128],
           rowt[0:nr, c0:c0 + width], ["rowsel", rowk], ["pm1"])
        cpy(dst, pm[1][0:128, 0:width], ["pm1"], [dstk])

    P.dma("sync", ident[:, :], ident_i, writes=["ident"])
    P.dma("sync", ones[:, :], ones_i, writes=["ones"])
    P.dma("sync", stag[:, :], stag_i, writes=["stag"])
    P.dma("sync", rowsel[:, :], rowsel_i, writes=["rowsel"])
    P.dma("sync", convw[:, :], convw_i, writes=["convw"])
    P.dma("sync", proxys[:, :], proxy, writes=["proxys"])
    P.dma("sync", onorms[:, :], onorm_i, writes=["onorms"])
    P.dma("sync", w1ds[:, :], w1d_i, writes=["w1ds"])
    P.dma("sync", w1es[:, :], w1e_i, writes=["w1es"])
    P.dma("sync", w2s[:, :], w2_i, writes=["w2s"])
    P.dma("sync", b1s[:, :], b1_i, writes=["b1s"])
    P.dma("sync", b2s[:, :], b2_i, writes=["b2s"])
    P.dma("sync", hes[:, :], he, writes=["hes"])
    P.dma("sync", W5b[0:32, 0:5], w5_i, writes=["W5b"])
    P.dma("sync", W5b[32:33, 0:5], b5_i, writes=["W5b"])
    P.dma("sync", small[:, 0:HC], dtb, writes=["small"])

    mems(W5Z[:, :], 0.0, ["W5Z"])
    tt(W5Z[:, 0 * 31 + 15:0 * 31 + 16], W5b[:, 0:1], W5b[:, 1:2], ALU.add,
       ["W5b"], ["W5Z"])
    tt(W5Z[:, 1 * 31 + 15:1 * 31 + 16], W5b[:, 0:1], W5b[:, 1:2], ALU.subtract,
       ["W5b"], ["W5Z"])
    tt(W5Z[:, 2 * 31 + 15:2 * 31 + 16], W5b[:, 3:4], W5b[:, 4:5], ALU.add,
       ["W5b"], ["W5Z"])
    tt(W5Z[:, 3 * 31 + 15:3 * 31 + 16], W5b[:, 3:4], W5b[:, 4:5], ALU.subtract,
       ["W5b"], ["W5Z"])
    tt(small[0:33, 32:33], W5b[:, 0:1], W5b[:, 1:2], ALU.add, ["W5b"], ["small"])
    tt(small[0:33, 33:34], W5b[:, 0:1], W5b[:, 1:2], ALU.subtract, ["W5b"], ["small"])
    tt(small[0:33, 34:35], W5b[:, 3:4], W5b[:, 4:5], ALU.add, ["W5b"], ["small"])
    tt(small[0:33, 35:36], W5b[:, 3:4], W5b[:, 4:5], ALU.subtract, ["W5b"], ["small"])
    cpy(W5b[:, 0:1], small[0:33, 32:33], ["small"], ["W5b"])
    cpy(W5b[:, 3:4], small[0:33, 34:35], ["small"], ["W5b"])
    cpy(W5b[:, 4:5], small[0:33, 35:36], ["small"], ["W5b"])
    cpy(W5b[:, 1:2], small[0:33, 33:34], ["small"], ["W5b"])

    P.dma("gpsimd", cc_x_in, x_ownT, writes=["ccx"])
    P.collective("AllGather", RG, cc_x_in.opt(), cc_x_out.opt(),
                 reads=["ccx"], writes=["ccxo"])

    aexp = sb("aexp", 1, HC)
    P.dma("sync", aexp[:, :], alog, writes=["aexp"])
    act(aexp[:, :], aexp[:, :], AF.Exp, ["aexp"], ["aexp"])
    act(small[:, 16:16 + HC], small[:, 0:HC], AF.Abs, ["small"], ["small"])
    tsc(small[:, 24:24 + HC], small[:, 16:16 + HC], -1.0, ALU.mult,
        ["small"], ["small"])
    act(small[:, 24:24 + HC], small[:, 24:24 + HC], AF.Exp, ["small"], ["small"])
    tsc(small[:, 24:24 + HC], small[:, 24:24 + HC], 1.0, ALU.add,
        ["small"], ["small"])
    act(small[:, 16:16 + HC], small[:, 24:24 + HC], AF.Ln, ["small"], ["small"])
    tsc(small[:, 24:24 + HC], small[:, 0:HC], 0.0, ALU.max, ["small"], ["small"])
    tt(small[:, 0:HC], small[:, 16:16 + HC], small[:, 24:24 + HC], ALU.add,
       ["small"], ["small"])
    mm(pm[0][0:128, 0:HC], ones[0:1, 0:128], aexp[0:1, 0:HC],
       ["ones", "aexp"], ["pm0"])
    tt(gchan[:, :], small[:, 0:HC], pm[0][0:128, 0:HC], ALU.mult,
       ["small", "pm0"], ["gchan"])
    tsc(gchan[:, :], gchan[:, :], -1.0, ALU.mult, ["gchan"], ["gchan"])
    dbgw("gchan", gchan[:, :], ["gchan"])
    dbgw("sfp", small[:, 0:64], ["small"])

    tr(pm[0][0:1, 0:128], onorms[:, 0:1], 128, ["onorms"], ["pm0"])
    cpy(small[0:1, 0:1], pm[0][0:1, 0:1], ["pm0"], ["small"])
    cpy(aexp[0:1, 0:1], aexp[0:1, 0:1], ["aexp"], ["aexp"])
    onr = sb("onr", 1, 128)
    cpy(onr[0:1, :], pm[0][0:1, 0:128], ["pm0"], ["onr"])
    mm(pm[0][0:128, 0:128], ones[0:1, 0:128], onr[0:1, 0:128], ["ones", "onr"],
       ["pm0"])
    cpy(onbc[:, :], pm[0][0:128, 0:128], ["pm0"], ["onbc"])

    cvec_s = sb("cvec_s", 128, 16)
    modrow = sb("modrow", 1, WS)
    P.dma("sync", cvec_s[:, :], cvecT, writes=["cvec_s"])
    act(cvec_s[:, :], cvec_s[:, :], AF.Silu, ["cvec_s"], ["cvec_s"])
    nblk = (WS + 511) // 512
    with contextlib.ExitStack() as c0:
        wma = sb("wma", 128, WS, cx=c0)
        wmb = sb("wmb", 128, WS, cx=c0)
        for dc in range(16):
            wt, wk_ = (wma, "wma") if dc % 2 == 0 else (wmb, "wmb")
            P.dma("sync", wt[:, :], wmod_i[dc], writes=[wk_])
            for b in range(nblk):
                w = min(512, WS - b * 512)
                mm(pb[b][0:1, 0:w], cvec_s[:, dc:dc + 1],
                   wt[:, b * 512:b * 512 + w], reads=["cvec_s", wk_],
                   writes=[PBK[b]], start=(dc == 0), stop=(dc == 15))
        for b in range(nblk):
            w = min(512, WS - b * 512)
            cpy(modrow[0:1, b * 512:b * 512 + w], pb[b][0:1, 0:w], [PBK[b]],
                ["modrow"])
        dbgw("modrow", modrow[0:1, 0:WS], ["modrow"])
        P.dma("gpsimd", cc_m_in, modrow[0:1, 0:WS], reads=["modrow"], writes=["ccm"])
        P.collective("AllGather", RG, cc_m_in.opt(), cc_m_out.opt(),
                     reads=["ccm"], writes=["ccmo"])
        mvT = sb("mvT", 96, 128, cx=c0)
        P.dma("sync", mvT[:, :],
              cc_m_out.flatten().rearrange("(a b) -> a b", b=128),
              reads=["ccmo"], writes=["mvT"])
        dbgw("mvT", mvT[:, :], ["mvT"])
        tr(pm[0][0:128, 0:96], mvT[0:96, 0:128], 96, ["mvT"], ["pm0"])
        cpy(modv[:, :], pm[0][0:128, 0:96], ["pm0"], ["modv"])
        dbgw("modv", modv[:, :], ["modv"])
        P.barrier()
    tsc(ca1[:, :], modv[:, 16:32], 1.0, ALU.add, ["modv"], ["ca1"])
    tsc(cb1[:, :], modv[:, 64:80], 1.0, ALU.add, ["modv"], ["cb1"])
    recip(small[:, 0:16], ca1[:, :], ["ca1"], ["small"])
    tt(saca[:, :], modv[:, 0:16], small[:, 0:16], ALU.mult, ["modv", "small"],
       ["saca"])
    dbgw("ca1d", ca1[:, :], ["ca1"])
    dbgw("recd", small[:, 0:16], ["small"])
    dbgw("sacaE", saca[:, :], ["saca"])
    if "sacaX" in dbg_out:
        def vflush():
            P.thunks.append(("vector",
                             lambda e, t=P.cnt["vector"]:
                             e.wait_ge(P.sems["vector"], t)))
        rX = sb("rX", 128, 16)
        sacaX = sb("sacaX", 128, 16)
        recip(rX[:, :], ca1[:, :], ["ca1"], ["rX"])
        vflush()
        tt(sacaX[:, :], modv[:, 0:16], rX[:, :], ALU.mult, ["modv", "rX"],
           ["sacaX"])
        vflush()
        dbgw("sacaX", sacaX[:, :], ["sacaX"])
        rZ = sb("rZ", 128, 16)
        sacaZ = sb("sacaZ", 128, 16)
        recip(rZ[:, :], ca1[:, :], ["ca1"], ["rZ"])
        tt(sacaZ[:, :], modv[:, 0:16], rZ[:, :], ALU.mult, ["modv", "rZ"],
           ["sacaZ"])
        dbgw("sacaZ", sacaZ[:, :], ["sacaZ"])
    for srcc, isga in ((32, True), (80, False)):
        src = modv[:, srcc:srcc + 16]
        act(small[:, 0:16], src, AF.Abs, ["modv"], ["small"])
        tsc(small[:, 0:16], small[:, 0:16], 0.25, ALU.max, ["small"], ["small"])
        tsc(small[:, 16:32], src, 0.0, ALU.is_ge, ["modv"], ["small"])
        tsc(small[:, 16:32], small[:, 16:32], 2.0, ALU.mult, ["small"], ["small"],
            s2=1.0, op1=ALU.subtract)
        if isga:
            tt(gacl[:, :], small[:, 0:16], small[:, 16:32], ALU.mult,
               ["small"], ["gacl"])
        else:
            tt(small[:, 32:48], small[:, 0:16], small[:, 16:32], ALU.mult,
               ["small"], ["small"])
            P.dma("sync", sdgb[:, :], sd_i, writes=["sdgb"])
            tt(sdgb[:, :], sdgb[:, :], small[:, 32:48], ALU.mult,
               ["sdgb", "small"], ["sdgb"])

    mur = sb("mur", NRo, 512)
    rr_ = sb("rr_", NRo, 512)
    entr = sb("entr", 1, TC)
    with contextlib.ExitStack() as c1:
        xs0 = sb("xs0", 128, OSEG, cx=c1)
        sq0 = sb("sq0", 128, OSEG, cx=c1)
        for os_ in range(NOSEG):
            o0 = os_ * OSEG
            for dc in range(16):
                P.dma("sync", xs0[:, :], x_ownT[dc, :, o0:o0 + OSEG], writes=["xs0"])
                act(sq0[:, :], xs0[:, :], AF.Square, ["xs0"], ["sq0"])
                mm(pb[0][0:NRo, 0:OSEG],
                   stag[:, 15 - os_:15 - os_ + NRo], xs0[:, :],
                   ["stag", "xs0"], ["pb0"],
                   start=(dc == 0 and os_ == 0), stop=(dc == 15 and os_ == NOSEG - 1))
                mm(pb[1][0:NRo, 0:OSEG],
                   stag[:, 15 - os_:15 - os_ + NRo], sq0[:, :],
                   ["stag", "sq0"], ["pb1"],
                   start=(dc == 0 and os_ == 0), stop=(dc == 15 and os_ == NOSEG - 1))
        tsc(mur[:, 0:OSEG if NRo == 1 else 512], pb[0][0:NRo, 0:OSEG if NRo == 1 else 512],
            1.0 / D, ALU.mult, ["pb0"], ["mur"])
        tsc(rr_[:, 0:OSEG if NRo == 1 else 512], pb[1][0:NRo, 0:OSEG if NRo == 1 else 512],
            1.0 / D, ALU.mult, ["pb1"], ["rr_"])
        tt(sq0[0:NRo, 0:OSEG if NRo == 1 else 512], mur[:, 0:OSEG if NRo == 1 else 512],
           mur[:, 0:OSEG if NRo == 1 else 512], ALU.mult, ["mur"], ["sq0"])
        tt(rr_[:, 0:OSEG if NRo == 1 else 512], rr_[:, 0:OSEG if NRo == 1 else 512],
           sq0[0:NRo, 0:OSEG if NRo == 1 else 512], ALU.subtract,
           ["rr_", "sq0"], ["rr_"])
        tsc(rr_[:, 0:OSEG if NRo == 1 else 512], rr_[:, 0:OSEG if NRo == 1 else 512],
            LN_EPS, ALU.add, ["rr_"], ["rr_"])
        act(rr_[:, 0:OSEG if NRo == 1 else 512], rr_[:, 0:OSEG if NRo == 1 else 512],
            AF.Sqrt, ["rr_"], ["rr_"])
        recip(rr_[:, 0:OSEG if NRo == 1 else 512], rr_[:, 0:OSEG if NRo == 1 else 512],
              ["rr_"], ["rr_"])
        P.barrier()

    with contextlib.ExitStack() as c2:
        xs0 = sb("xs0b", 128, OSEG, cx=c2)
        ht0 = sb("ht0", 128, OSEG, cx=c2)
        bc1 = sb("bc1e", 128, OSEG, cx=c2)
        bc2 = sb("bc2e", 128, OSEG, cx=c2)
        uncs = sb("uncs", 128, 16 * BINS, cx=c2)
        zrows = sb("zrows", BINS, OSEG, cx=c2)
        zsc = sb("zsc", 128, BINS, cx=c2)
        ecol = sb("ecol", 128, BINS, cx=c2)
        tcol = sb("tcol", 128, 8, cx=c2)
        P.dma("sync", uncs[:, :], uncw, writes=["uncs"])
        for os_ in range(NOSEG):
            o0 = os_ * OSEG
            sro, cco = (o0 // 512, o0 % 512)
            bc(bc1[:, :], "bc1e", mur, "mur", sro, cco, OSEG, NRo)
            bc(bc2[:, :], "bc2e", rr_, "rr_", sro, cco, OSEG, NRo)
            for dc in range(16):
                P.dma("sync", xs0[:, :], x_ownT[dc, :, o0:o0 + OSEG], writes=["xs0b"])
                tt(ht0[:, :], xs0[:, :], bc1[:, :], ALU.subtract,
                   ["xs0b", "bc1e"], ["ht0"])
                tt(ht0[:, :], ht0[:, :], bc2[:, :], ALU.mult, ["ht0", "bc2e"], ["ht0"])
                tsc(ht0[:, :], ht0[:, :], ca1[:, dc:dc + 1], ALU.mult,
                    ["ht0", "ca1", "modv"], ["ht0"],
                    s2=modv[:, dc:dc + 1], op1=ALU.add)
                mm(pb[2][0:BINS, 0:OSEG], uncs[:, dc * BINS:(dc + 1) * BINS],
                   ht0[:, :], ["uncs", "ht0"], ["pb2"],
                   start=(dc == 0), stop=(dc == 15))
            cpy(zrows[:, :], pb[2][0:BINS, 0:OSEG], ["pb2"], ["zrows"])
            for tb in range(OSEG // 128):
                t0 = tb * 128
                tr(pm[0][0:128, 0:BINS], zrows[0:BINS, t0:t0 + 128], BINS,
                   ["zrows"], ["pm0"])
                cpy(zsc[:, :], pm[0][0:128, 0:BINS], ["pm0"], ["zsc"])
                if os_ == 0 and tb == 0:
                    dbgw("zsc0", zsc[:, :], ["zsc"])
                red(tcol[:, 0:1], zsc[:, :], ALU.max, ["zsc"], ["tcol"])
                tsc(tcol[:, 1:2], tcol[:, 0:1], -1.0, ALU.mult, ["tcol"], ["tcol"])
                act(ecol[:, :], zsc[:, :], AF.Exp, ["zsc", "tcol"],
                    ["ecol", "tcol"], bias=tcol[:, 1:2], accum=tcol[:, 2:3])
                tt(zsc[:, :], ecol[:, :], zsc[:, :], ALU.mult, ["ecol", "zsc"],
                   ["zsc"])
                red(tcol[:, 3:4], zsc[:, :], ALU.add, ["zsc"], ["tcol"])
                recip(tcol[:, 4:5], tcol[:, 2:3], ["tcol"], ["tcol"])
                tt(tcol[:, 5:6], tcol[:, 3:4], tcol[:, 4:5], ALU.mult,
                   ["tcol"], ["tcol"])
                act(tcol[:, 6:7], tcol[:, 2:3], AF.Ln, ["tcol"], ["tcol"])
                tt(tcol[:, 6:7], tcol[:, 6:7], tcol[:, 0:1], ALU.add,
                   ["tcol"], ["tcol"])
                tt(tcol[:, 7:8], tcol[:, 6:7], tcol[:, 5:6], ALU.subtract,
                   ["tcol"], ["tcol"])
                tsc(tcol[:, 7:8], tcol[:, 7:8], 1.0 / LOG64, ALU.mult,
                    ["tcol"], ["tcol"])
                if os_ == 0 and tb == 0:
                    dbgw("tcol0", tcol[:, :], ["tcol"])
                mm(pm[0][0:1, 0:128], tcol[:, 7:8], ident[:, :],
                   ["tcol", "ident"], ["pm0"])
                cpy(entr[0:1, o0 + t0:o0 + t0 + 128], pm[0][0:1, 0:128],
                    ["pm0"], ["entr"])
        P.barrier()

    st3 = sb("st3", 3, TC)
    for s in range(NRo):
        w = min(512, TC - s * 512)
        P.dma("sync", st3[0:1, s * 512:s * 512 + w], mur[s:s + 1, 0:w],
              reads=["mur"], writes=["st3"])
        P.dma("sync", st3[1:2, s * 512:s * 512 + w], rr_[s:s + 1, 0:w],
              reads=["rr_"], writes=["st3"])
    P.dma("sync", st3[2:3, 0:TC], entr[0:1, 0:TC], reads=["entr"], writes=["st3"])
    P.dma("gpsimd", cc_s_in, st3[0:3, 0:TC], reads=["st3"], writes=["ccs"])
    P.collective("AllGather", RG, cc_s_in.opt(), cc_s_out.opt(),
                 reads=["ccs"], writes=["ccso"])
    cmid = contextlib.ExitStack()
    murow = sb("murow", NR, 512, cx=cmid)
    rrow = sb("rrow", NR, 512, cx=cmid)
    entrow = sb("entrow", NR, 512, cx=cmid)
    masksf = sb("masksf", 128, 3 * 512, cx=cmid)
    masksu = sb("masksu", 128, 8 * 512, mybir.dt.uint8, cx=cmid)
    for i in range(3):
        P.dma("sync", masksf[:, i * 512:(i + 1) * 512], masksf_i[i], writes=["masks"])
    for i in range(8):
        P.dma("sync", masksu[:, i * 512:(i + 1) * 512], masksu_i[i], writes=["masks"])
    for r in range(NCORE):
        t0 = r * TC
        sr0, cc0 = t0 // 512, t0 % 512
        npart = max(1, TC // 512)
        w = min(512, TC)
        for pp in range(npart):
            P.dma("sync", murow[sr0 + pp:sr0 + pp + 1, cc0:cc0 + w],
                  cc_s_out[r, 0, pp * 512:pp * 512 + w], reads=["ccso"],
                  writes=["murow"])
            P.dma("sync", rrow[sr0 + pp:sr0 + pp + 1, cc0:cc0 + w],
                  cc_s_out[r, 1, pp * 512:pp * 512 + w], reads=["ccso"],
                  writes=["rrow"])
            P.dma("sync", entrow[sr0 + pp:sr0 + pp + 1, cc0:cc0 + w],
                  cc_s_out[r, 2, pp * 512:pp * 512 + w], reads=["ccso"],
                  writes=["entrow"])
    dbgw("murow", murow[0:NR, :], ["murow"])
    dbgw("rrow", rrow[0:NR, :], ["rrow"])
    dbgw("entrow", entrow[0:NR, :], ["entrow"])

    with contextlib.ExitStack() as c3:
        wtil = sb("wtil", 128, 16 * 3 * HC * 128, cx=c3)
        ncsum = sb("ncsum", 128, 3 * HC, cx=c3)
        cjt = sb("cjt", 128, 3 * HC, cx=c3)
        xsa = sb("xsa", 128, SEG, cx=c3)
        xsb = sb("xsb", 128, SEG, cx=c3)
        ppre = sb("ppre", 128, SEG, cx=c3)
        bc1 = sb("bc1q", 128, SEG, cx=c3)
        bc2 = sb("bc2q", 128, SEG, cx=c3)

        P.dma("sync", wtil[:, :], wqkv, writes=["wtil"])
        for dc in range(16):
            w0 = dc * (3 * HC * 128)
            tsc(wtil[:, w0:w0 + 3 * HC * 128], wtil[:, w0:w0 + 3 * HC * 128],
                ca1[:, dc:dc + 1], ALU.mult, ["wtil", "ca1"], ["wtil"])

        def wtsl(dc, proj, hcol):
            o = dc * (3 * HC * 128) + proj * (HC * 128) + hcol * 128
            return wtil[:, o:o + 128]

        for proj in range(3):
            for hcol in range(HC):
                j = proj * HC + hcol
                for dc in range(16):
                    mm(pm[0][0:128, j:j + 1], wtsl(dc, proj, hcol),
                       ones[:, 0:1], ["ones", "wtil"], ["pm0"],
                       start=(dc == 0), stop=(dc == 15))
                    mm(pm[1][0:128, j:j + 1], wtsl(dc, proj, hcol),
                       saca[:, dc:dc + 1], ["saca", "wtil"], ["pm1"],
                       start=(dc == 0), stop=(dc == 15))
        dbgw("onesd", ones[:, :], ["ones"])
        dbgw("sacad", saca[:, :], ["saca"])
        cpy(ncsum[:, :], pm[0][0:128, 0:3 * HC], ["pm0"], ["ncsum"])
        tsc(ncsum[:, :], ncsum[:, :], -1.0, ALU.mult, ["ncsum"], ["ncsum"])
        cpy(cjt[:, :], pm[1][0:128, 0:3 * HC], ["pm1"], ["cjt"])

        for sgi in range(NSEG):
            t0 = sgi * SEG
            rk, loc = t0 // TC, t0 % TC
            sr0, cc0 = t0 // 512, t0 % 512
            bc(bc1[:, :], "bc1q", murow, "murow", sr0, cc0, SEG, NR)
            bc(bc2[:, :], "bc2q", rrow, "rrow", sr0, cc0, SEG, NR)
            for dc in range(16):
                xt, xk = ((xsa, "xsa") if dc % 2 == 0 else (xsb, "xsb"))
                P.dma("sync", xt[:, :], cc_x_out[rk, dc, :, loc:loc + SEG],
                      reads=["ccxo"], writes=[xk])
                for proj in range(3):
                    for hcol in range(HC):
                        j = proj * HC + hcol
                        mm(pb[j][0:128, 0:SEG], wtsl(dc, proj, hcol), xt[:, :],
                           ["wtil", xk], [PBK[j]],
                           start=(dc == 0), stop=(dc == 15))
            if sgi == 0:
                dbgw("bc1q", bc1[:, :], ["bc1q"])
                dbgw("bc2q", bc2[:, :], ["bc2q"])
                dbgw("wtl", wtil[:, 0:2048], ["wtil"])
                dbgw("ncs", ncsum[:, :], ["ncsum"])
                dbgw("cjt", cjt[:, :], ["cjt"])
            for proj in range(3):
                for hcol in range(HC):
                    j = proj * HC + hcol
                    stt(ppre[:, :], bc1[:, :], ncsum[:, j:j + 1],
                        pb[j][0:128, 0:SEG], ALU.mult, ALU.add,
                        ["bc1q", "ncsum", PBK[j]], ["ppre"])
                    tt(ppre[:, :], ppre[:, :], bc2[:, :], ALU.mult,
                       ["ppre", "bc2q"], ["ppre"])
                    tsc(ppre[:, :], ppre[:, :], cjt[:, j:j + 1], ALU.add,
                        ["ppre", "cjt"], ["ppre"])
                    P.dma("scalar", qkv_pre[proj, hcol, :, t0:t0 + SEG],
                          ppre[:, :], reads=["ppre"], writes=["qkvpre"])
        P.barrier()

    dbgw("q_pre", qkv_pre[0, 0], ["qkvpre"])
    with contextlib.ExitStack() as c4:
        convin = sb("convin", 128, T + 3, cx=c4)
        convacc = sb("convacc", 128, T, cx=c4)
        for proj in range(3):
            for hcol in range(HC):
                j = proj * HC + hcol
                mems(convin[:, 0:3], 0.0, ["convin"])
                P.dma("sync", convin[:, 3:T + 3], qkv_pre[proj, hcol],
                      reads=["qkvpre"], writes=["convin"])
                tsc(convacc[:, :], convin[:, 0:T], convw[:, j * 4:j * 4 + 1],
                    ALU.mult, ["convin", "convw"], ["convacc"])
                for tau in range(1, 4):
                    stt(convacc[:, :], convin[:, tau:T + tau],
                        convw[:, j * 4 + tau:j * 4 + tau + 1], convacc[:, :],
                        ALU.mult, ALU.add, ["convin", "convw", "convacc"],
                        ["convacc"])
                act(convacc[:, :], convacc[:, :], AF.Silu, ["convacc"],
                    ["convacc"])
                P.dma("scalar", qkv_post[proj, hcol], convacc[:, :],
                      reads=["convacc"], writes=["qkvpost"])
        P.barrier()
    dbgw("q_post", qkv_post[0, 0], ["qkvpost"])
    dbgw("k_post", qkv_post[1, 0], ["qkvpost"])
    dbgw("v_post", qkv_post[2, 0], ["qkvpost"])

    def build_feats(h):
        with contextlib.ExitStack() as cf:
            kt = sb(f"f_kt{h}", 128, 512, cx=cf)
            vt = sb(f"f_vt{h}", 128, 512, cx=cf)
            vh = sb(f"f_vh{h}", 128, 512, cx=cf)
            sq = sb(f"f_sq{h}", 128, 512, cx=cf)
            kn2 = sb(f"f_kn2{h}", NR, 512, cx=cf)
            vn2 = sb(f"f_vn2{h}", NR, 512, cx=cf)
            vhn2 = sb(f"f_vhn2{h}", NR, 512, cx=cf)
            vd = sb(f"f_vd{h}", NR, 512, cx=cf)
            s1 = sb(f"f_s1{h}", NR, 512, cx=cf)
            s2 = sb(f"f_s2{h}", NR, 512, cx=cf)
            s3 = sb(f"f_s3{h}", NR, 512, cx=cf)
            h1t = sb(f"f_h1t{h}", 32, 512, cx=cf)
            afT = sb(f"f_afT{h}", NR, 512, cx=cf)
            asT = sb(f"f_asT{h}", NR, 512, cx=cf)
            zr = sb(f"f_zr{h}", 16, 128, cx=cf)
            feats = sb(f"feats{h}", 4, T, cx=cf)
            hf33 = sb(f"hf33{h}", 33, T, cx=cf)
            for s in range(NR):
                t0 = s * 512
                P.dma("sync", kt[:, :], qkv_post[1, h, :, t0:t0 + 512],
                      reads=["qkvpost"], writes=["f_kt"])
                P.dma("sync", vt[:, :], qkv_post[2, h, :, t0:t0 + 512],
                      reads=["qkvpost"], writes=["f_vt"])
                mm(pm[0][0:128, 0:512], proxys[:, :], kt[:, :], ["proxys", "f_kt"],
                   ["pm0"])
                cpy(vh[:, :], pm[0][0:128, 0:512], ["pm0"], ["f_vh"])
                st_, sp_ = (s == 0), (s == NR - 1)
                act(sq[:, :], kt[:, :], AF.Square, ["f_kt"], ["f_sq"])
                mm(pb[0][0:NR, 0:512], stag[:, 15 - s:15 - s + NR], sq[:, :],
                   ["stag", "f_sq"], ["pb0"], start=st_, stop=sp_)
                act(sq[:, :], vt[:, :], AF.Square, ["f_vt"], ["f_sq"])
                mm(pb[1][0:NR, 0:512], stag[:, 15 - s:15 - s + NR], sq[:, :],
                   ["stag", "f_sq"], ["pb1"], start=st_, stop=sp_)
                act(sq[:, :], vh[:, :], AF.Square, ["f_vh"], ["f_sq"])
                mm(pb[2][0:NR, 0:512], stag[:, 15 - s:15 - s + NR], sq[:, :],
                   ["stag", "f_sq"], ["pb2"], start=st_, stop=sp_)
                tt(sq[:, :], vt[:, :], vh[:, :], ALU.mult, ["f_vt", "f_vh"],
                   ["f_sq"])
                mm(pb[3][0:NR, 0:512], stag[:, 15 - s:15 - s + NR], sq[:, :],
                   ["stag", "f_sq"], ["pb3"], start=st_, stop=sp_)
            cpy(kn2[:, :], pb[0][0:NR, 0:512], ["pb0"], ["f_kn2"])
            cpy(vn2[:, :], pb[1][0:NR, 0:512], ["pb1"], ["f_vn2"])
            cpy(vhn2[:, :], pb[2][0:NR, 0:512], ["pb2"], ["f_vhn2"])
            cpy(vd[:, :], pb[3][0:NR, 0:512], ["pb3"], ["f_vd"])
            act(s1[:, :], kn2[:, :], AF.Sqrt, ["f_kn2"], ["f_s1"])
            tsc(s1[:, :], s1[:, :], EPS, ALU.add, ["f_s1"], ["f_s1"])
            recip(rknT[0:NR, h * 512:(h + 1) * 512], s1[:, :], ["f_s1"], ["rknT"])
            act(s1[:, :], vn2[:, :], AF.Sqrt, ["f_vn2"], ["f_s1"])
            act(s2[:, :], vhn2[:, :], AF.Sqrt, ["f_vhn2"], ["f_s2"])
            tt(s3[:, :], vn2[:, :], vhn2[:, :], ALU.add, ["f_vn2", "f_vhn2"],
               ["f_s3"])
            stt(s3[:, :], vd[:, :], -2.0, s3[:, :], ALU.mult, ALU.add,
                ["f_vd", "f_s3"], ["f_s3"])
            act(s3[:, :], s3[:, :], AF.Sqrt, ["f_s3"], ["f_s3"])
            tsc(vn2[:, :], s1[:, :], EPS, ALU.add, ["f_s1"], ["f_vn2"])
            recip(vn2[:, :], vn2[:, :], ["f_vn2"], ["f_vn2"])
            tt(vn2[:, :], s3[:, :], vn2[:, :], ALU.mult, ["f_s3", "f_vn2"],
               ["f_vn2"])
            tt(vhn2[:, :], s1[:, :], s2[:, :], ALU.mult, ["f_s1", "f_s2"],
               ["f_vhn2"])
            tsc(vhn2[:, :], vhn2[:, :], EPS, ALU.add, ["f_vhn2"], ["f_vhn2"])
            recip(vhn2[:, :], vhn2[:, :], ["f_vhn2"], ["f_vhn2"])
            tt(vhn2[:, :], vd[:, :], vhn2[:, :], ALU.mult, ["f_vd", "f_vhn2"],
               ["f_vhn2"])
            tsc(s3[:, :], s3[:, :], 1.0, ALU.add, ["f_s3"], ["f_s3"])
            act(s3[:, :], s3[:, :], AF.Ln, ["f_s3"], ["f_s3"])
            for s in range(NR):
                P.dma("sync", feats[0:1, s * 512:(s + 1) * 512],
                      entrow[s:s + 1, 0:512], reads=["entrow"], writes=["feats"])
                P.dma("sync", feats[1:2, s * 512:(s + 1) * 512],
                      vn2[s:s + 1, 0:512], reads=["f_vn2"], writes=["feats"])
                P.dma("sync", feats[2:3, s * 512:(s + 1) * 512],
                      vhn2[s:s + 1, 0:512], reads=["f_vhn2"], writes=["feats"])
                P.dma("sync", feats[3:4, s * 512:(s + 1) * 512],
                      s3[s:s + 1, 0:512], reads=["f_s3"], writes=["feats"])
            mm(pm[0][0:32, 0:1], w1es[:, :], hes[:, h:h + 1], ["w1es", "hes"], ["pm0"])
            tt(b1e[:, h:h + 1], pm[0][0:32, 0:1], b1s[:, :], ALU.add,
               ["pm0", "b1s"], ["b1e"])
            mems(hf33[32:33, 0:T], 1.0, ["hf33"])
            for s in range(NR):
                t0 = s * 512
                mm(pm[0][0:32, 0:512], w1ds[:, :], feats[0:4, t0:t0 + 512],
                   ["w1ds", "feats"], ["pm0"])
                act(h1t[:, :], pm[0][0:32, 0:512], AF.Silu, ["pm0", "b1e"],
                    ["f_h1t"], bias=b1e[:, h:h + 1])
                mm(pm[1][0:32, 0:512], w2s[:, :], h1t[:, :], ["w2s", "f_h1t"], ["pm1"])
                act(hf33[0:32, t0:t0 + 512], pm[1][0:32, 0:512], AF.Silu,
                    ["pm1", "b2s"], ["hf33"], bias=b2s[:, :])
            if h == 0:
                dbgw("hf33", hf33[0:33, 0:T], ["hf33"])
            for jj, (j, dstt, dk) in enumerate(
                    ((0, bfT, "bfT"), (1, bsT, "bsT"),
                     (2, afT, "f_afT"), (3, asT, "f_asT"))):
                for s in range(NR):
                    t0 = s * 512
                    mm(pb[jj][0:NR, 0:512],
                       W5Z[:, j * 31 + 15 - s:j * 31 + 15 - s + NR],
                       hf33[:, t0:t0 + 512], ["W5Z", "hf33"], [PBK[jj]],
                       start=(s == 0), stop=(s == NR - 1))
                coff = h * 512 if dstt in (bfT, bsT) else 0
                act(dstt[0:NR, coff:coff + 512], pb[jj][0:NR, 0:512],
                    AF.Sigmoid, [PBK[jj]], [dk])
            for cn in range(NCH):
                mm(pm[0][0:128, cn * 5:cn * 5 + 5], hf33[0:33, cn * 128:cn * 128 + 128],
                   W5b[0:33, 0:5], ["hf33", "W5b"], ["pm0"])
            act(colpk[:, h * 5 * NCH:(h + 1) * 5 * NCH], pm[0][0:128, 0:5 * NCH],
                AF.Sigmoid, ["pm0"], ["colpk"])
            tsc(omlam[:, h * NCH:(h + 1) * NCH],
                colpk[:, h * 5 * NCH + 2:(h + 1) * 5 * NCH:5], -1.0, ALU.mult,
                ["colpk"], ["omlam"], s2=1.0, op1=ALU.add)
            mems(zr[:, :], 0.0, ["f_zr"])
            zrb = zr[0:NR, 0:128]
            for r, srcT, sk in ((0, afT, "f_afT"), (1, asT, "f_asT")):
                for cc in range(4):
                    c0 = cc * 128
                    scan(arT[0:NR, (h * 2 + r) * 512 + c0:(h * 2 + r) * 512 + c0 + 128],
                         srcT[0:NR, c0:c0 + 128], zrb, 0.0,
                         [sk, "f_zr"], ["arT"])
            P.barrier()

    def build_delta(h):
        with contextlib.ExitStack() as cd:
            kap = sb(f"d_kap{h}", 128, SB4, cx=cd)
            qap = sb(f"d_qap{h}", 128, SB4, cx=cd)
            vap = sb(f"d_vap{h}", 128, SB4, cx=cd)
            kup = sb(f"d_kup{h}", 128, SB4, cx=cd)
            kbt = sb(f"d_kbt{h}", 128, SB4, cx=cd)
            khf = sb(f"d_khf{h}", 128, SB4, cx=cd)
            vT4 = sb(f"d_vT4{h}", 128, SB4, cx=cd)
            argt = sb(f"d_argt{h}", 128, 12 * SB4, cx=cd)
            expt = argt
            fkt = sb(f"d_fkt{h}", 128, 5 * SB4, cx=cd)
            fqt = sb(f"d_fqt{h}", 128, 5 * SB4, cx=cd)
            gt = sb(f"d_gt{h}", 128, 5 * SB4, cx=cd)
            khTt = sb(f"d_khTt{h}", 128, 2 * SB4, cx=cd)
            kck = sb(f"d_kck{h}", 128, 2 * SB4, cx=cd)
            qck = sb(f"d_qck{h}", 128, 2 * SB4, cx=cd)
            gend = sb(f"d_gend{h}", 128, 2 * CB, cx=cd)
            Mt = sb(f"d_Mt{h}", 128, 2 * SB4, cx=cd)
            MTt = sb(f"d_MTt{h}", 128, 2 * SB4, cx=cd)
            Xt = sb(f"d_Xt{h}", 128, 2 * SB4, cx=cd)
            powt = sb(f"d_powt{h}", 128, 8 * 768, cx=cd)
            powm = sb(f"d_powm{h}", 128, 5 * 128, cx=cd)
            Yt = sb(f"d_Yt{h}", 128, 128, cx=cd)
            Yt1 = sb(f"d_Yt1{h}", 128, 128, cx=cd)
            Wt0 = sb(f"d_Wt0{h}", 128, 128, cx=cd)
            Wt1 = sb(f"d_Wt1{h}", 128, 128, cx=cd)
            Wt2 = sb(f"d_Wt2{h}", 128, 128, cx=cd)
            Wt3 = sb(f"d_Wt3{h}", 128, 128, cx=cd)
            ocm = sb(f"d_ocm{h}", 128, 128, cx=cd)

            def m3(ap_, w):
                return ap_.rearrange("p (b w) -> p b w", w=w)

            GAM = argt[:, 11 * SB4:12 * SB4]
            for bi in range(NB):
                b0 = bi * SB4
                P.dma("sync", kap[:, :], qkv_post[1, h, :, b0:b0 + SB4],
                      reads=["qkvpost"], writes=["d_kap"])
                P.dma("sync", qap[:, :], qkv_post[0, h, :, b0:b0 + SB4],
                      reads=["qkvpost"], writes=["d_qap"])
                P.dma("sync", vap[:, :], qkv_post[2, h, :, b0:b0 + SB4],
                      reads=["qkvpost"], writes=["d_vap"])
                mm(pm[0][0:128, 0:SB4], rowsel[0:NR, bi * 128:bi * 128 + 128],
                   rknT[0:NR, h * 512:(h + 1) * 512], ["rowsel", "rknT"], ["pm0"])
                tt(kup[:, :], kap[:, :], pm[0][0:128, 0:SB4], ALU.mult,
                   ["d_kap", "pm0"], ["d_kup"])
                for c4 in range(CB):
                    tr(pm[1][0:128, c4 * 128:c4 * 128 + 128],
                       vap[:, c4 * 128:c4 * 128 + 128], 128, ["d_vap"], ["pm1"])
                cpy(vT4[:, :], pm[1][0:128, 0:SB4], ["pm1"], ["d_vT4"])
                for r in range(2):
                    blk = (h * 2 + r) * 512
                    mm(pm[0][0:128, 0:SB4], rowsel[0:NR, bi * 128:bi * 128 + 128],
                       arT[0:NR, blk:blk + 512], ["rowsel", "arT"], ["pm0"])
                    cpy(GAM, pm[0][0:128, 0:SB4], ["pm0"], ["d_argt"])
                    for li, w in enumerate(LVLS):
                        bnd = m3(GAM, w)[:, :, w // 2 - 1:w // 2] \
                            .broadcast_to([128, SB4 // w, w])
                        tt(m3(argt[:, li * SB4:(li + 1) * SB4], w), m3(GAM, w),
                           bnd, ALU.subtract, ["d_argt"], ["d_argt"])
                        tt(m3(argt[:, (5 + li) * SB4:(6 + li) * SB4], w), bnd,
                           m3(GAM, w), ALU.subtract, ["d_argt"], ["d_argt"])
                    bnd = m3(GAM, 128)[:, :, 127:128].broadcast_to([128, CB, 128])
                    tt(m3(argt[:, 10 * SB4:11 * SB4], 128), bnd, m3(GAM, 128),
                       ALU.subtract, ["d_argt"], ["d_argt"])
                    tsc(argt[:, 1 * SB4:5 * SB4], argt[:, 1 * SB4:5 * SB4], 0.0,
                        ALU.max, ["d_argt"], ["d_argt"])
                    tsc(argt[:, 6 * SB4:10 * SB4], argt[:, 6 * SB4:10 * SB4], 0.0,
                        ALU.max, ["d_argt"], ["d_argt"])
                    act(expt[:, :], argt[:, :], AF.Exp, ["d_argt", "gchan"],
                        ["d_argt"], scale=gchan[:, h:h + 1])
                    cpy(gend[:, r * CB:(r + 1) * CB],
                        expt[:, 11 * SB4 + 127:12 * SB4:128], ["d_argt"],
                        ["d_gend"])
                    mm(pm[0][0:128, 0:SB4], rowsel[0:NR, bi * 128:bi * 128 + 128],
                       (bfT if r == 0 else bsT)[0:NR, h * 512:(h + 1) * 512],
                       ["rowsel", "bfT" if r == 0 else "bsT"], ["pm0"])
                    tt(kbt[:, :], kup[:, :], pm[0][0:128, 0:SB4], ALU.mult,
                       ["d_kup", "pm0"], ["d_kbt"])
                    kbr = kbt[:, :].unsqueeze(1).broadcast_to([128, 5, SB4])
                    kur = kup[:, :].unsqueeze(1).broadcast_to([128, 5, SB4])
                    qar = qap[:, :].unsqueeze(1).broadcast_to([128, 5, SB4])
                    e5 = expt[:, 0:5 * SB4].rearrange("p (l t) -> p l t", t=SB4)
                    g5 = expt[:, 5 * SB4:10 * SB4].rearrange("p (l t) -> p l t", t=SB4)
                    tt(fkt[:, :].rearrange("p (l t) -> p l t", t=SB4), e5, kbr,
                       ALU.mult, ["d_argt", "d_kbt"], ["d_fkt"])
                    stt(fqt[:, :].rearrange("p (l t) -> p l t", t=SB4), e5, DKS,
                        qar, ALU.mult, ALU.mult, ["d_argt", "d_qap"], ["d_fqt"])
                    tt(gt[:, :].rearrange("p (l t) -> p l t", t=SB4), g5, kur,
                       ALU.mult, ["d_argt", "d_kup"], ["d_gt"])
                    tt(khf[:, :], expt[:, 10 * SB4:11 * SB4], kup[:, :], ALU.mult,
                       ["d_argt", "d_kup"], ["d_khf"])
                    for c4 in range(CB):
                        tr(pm[1][0:128, c4 * 128:c4 * 128 + 128],
                           khf[:, c4 * 128:c4 * 128 + 128], 128, ["d_khf"], ["pm1"])
                    cpy(khTt[:, r * SB4:(r + 1) * SB4], pm[1][0:128, 0:SB4],
                        ["pm1"], ["d_khTt"])
                    tt(kck[:, r * SB4:(r + 1) * SB4], expt[:, 11 * SB4:12 * SB4],
                       kup[:, :], ALU.mult, ["d_argt", "d_kup"], ["d_kck"])
                    stt(qck[:, r * SB4:(r + 1) * SB4], expt[:, 11 * SB4:12 * SB4],
                        DKS, qap[:, :], ALU.mult, ALU.mult, ["d_argt", "d_qap"],
                        ["d_qck"])
                    if h == 0 and bi == 0 and r == 0:
                        dbgw("d_arg", argt[:, :], ["d_argt"])
                        dbgw("d_kck", kck[:, 0:SB4], ["d_kck"])
                        dbgw("d_qck", qck[:, 0:SB4], ["d_qck"])
                        dbgw("d_khT", khTt[:, 0:SB4], ["d_khTt"])
                        dbgw("d_vT4", vT4[:, :], ["d_vT4"])
                        dbgw("d_fkt", fkt[:, :], ["d_fkt"])
                        dbgw("d_gt", gt[:, :], ["d_gt"])
                    LVB = (pm[0], "pm0"), (pm[1], "pm1"), (pb[4], "pb4")
                    for li, w in enumerate(LVLS):
                        for mi, (dst, dk, lh, lk, rh, rk_, f8) in enumerate((
                                (Mt, "d_Mt", fkt, "d_fkt", gt, "d_gt", 0),
                                (MTt, "d_MTt", gt, "d_gt", fkt, "d_fkt", 1),
                                (Xt, "d_Xt", gt, "d_gt", fqt, "d_fqt", 2))):
                            bk, bkk = LVB[mi]
                            for c4 in range(CB):
                                o = li * SB4 + c4 * 128
                                mm(bk[0:128, c4 * 128:c4 * 128 + 128],
                                   lh[:, o:o + 128], rh[:, o:o + 128],
                                   [lk, rk_], [bkk])
                        for mi, (dst, dk, lh, lk, rh, rk_, f8) in enumerate((
                                (Mt, "d_Mt", fkt, "d_fkt", gt, "d_gt", 0),
                                (MTt, "d_MTt", gt, "d_gt", fkt, "d_fkt", 1),
                                (Xt, "d_Xt", gt, "d_gt", fqt, "d_fqt", 2))):
                            bk, bkk = LVB[mi]
                            dsl = dst[:, r * SB4:(r + 1) * SB4]
                            if li == 0:
                                mrep = masksf[:, f8 * 512:f8 * 512 + 512]
                                tt(dsl, bk[0:128, 0:SB4], mrep, ALU.mult,
                                   [bkk, "masks"], [dk])
                            else:
                                mu_ = (li - 1) if f8 == 0 else (3 + li)
                                mrep = masksu[:, mu_ * 512:mu_ * 512 + 512]
                                cpred(dsl, mrep, bk[0:128, 0:SB4],
                                      [bkk, "masks", dk], [dk])
                    if h == 0 and bi == 0 and r == 0:
                        dbgw("d_M", Mt[:, 0:SB4], ["d_Mt"])
                        dbgw("d_MT", MTt[:, 0:SB4], ["d_MTt"])
                        dbgw("d_X", Xt[:, 0:SB4], ["d_Xt"])
                    pprev = {}
                    for c4 in range(CB):
                        base = r * SB4 + c4 * 128
                        pprev[c4] = (Mt[:, base:base + 128], "d_Mt",
                                     MTt[:, base:base + 128], "d_MTt")
                    for pi in range(6):
                        for c4 in range(CB):
                            prev, prevk, prevT, prevTk = pprev[c4]
                            cc = c4 * 128
                            mm(pm[1][0:128, cc:cc + 128], prev, prevT,
                               [prevk, prevTk], ["pm1"])
                            if pi < 5:
                                mm(pm[0][0:128, cc:cc + 128], prevT, prev,
                                   [prevk, prevTk], ["pm0"])
                        for c4 in range(CB):
                            cc = c4 * 128
                            pslot = (r * CB + c4) * 768
                            ptk = f"d_powt{r}{c4}"
                            pmk = f"d_powm{c4}"
                            po = powt[:, pslot + pi * 128:pslot + pi * 128 + 128]
                            if pi % 2 == 0:
                                cpy(po, pm[1][0:128, cc:cc + 128], ["pm1"], [ptk])
                            else:
                                act(po, pm[1][0:128, cc:cc + 128], AF.Copy,
                                    ["pm1"], [ptk])
                            if pi < 5:
                                cpy(powm[:, cc:cc + 128],
                                    pm[0][0:128, cc:cc + 128], ["pm0"], [pmk])
                                pprev[c4] = (powm[:, cc:cc + 128], pmk, po, ptk)
                BA = {0: (pm[0], "pm0"), 1: (pb[4], "pb4")}
                BB = {0: (pm[1], "pm1"), 1: (pb[5], "pb5")}
                YTS = {0: (Yt, "d_Yt0"), 1: (Yt1, "d_Yt1")}
                BUFS = {0: [(Wt0[:, :], "d_Wt0"), (Wt1[:, :], "d_Wt1")],
                        1: [(Wt2[:, :], "d_Wt2"), (Wt3[:, :], "d_Wt3")]}
                for c4 in range(CB):
                    cn = bi * CB + c4
                    E = {}
                    for r in range(2):
                        base = r * SB4 + c4 * 128
                        pslot = (r * CB + c4) * 768
                        E[r] = dict(
                            base=base, pslot=pslot,
                            Ssl=Sst[:, (h * 2 + r) * 128:(h * 2 + r) * 128 + 128],
                            sk=f"Sst{h * 2 + r}",
                            bcol=colpk[:, (h * NCH + cn) * 5 + r:
                                       (h * NCH + cn) * 5 + r + 1],
                            gcol=gend[:, r * CB + c4:r * CB + c4 + 1],
                            ptk=f"d_powt{r}{c4}")
                    lcol = colpk[:, (h * NCH + cn) * 5 + 2:(h * NCH + cn) * 5 + 3]
                    olcol = omlam[:, h * NCH + cn:h * NCH + cn + 1]
                    for r in range(2):
                        e = E[r]; ba, bak = BA[r]
                        mm(ba[0:128, 0:128], kck[:, e["base"]:e["base"] + 128],
                           e["Ssl"], ["d_kck", e["sk"]], [bak])
                    for r in range(2):
                        e = E[r]; ba, bak = BA[r]; yt, ytk = YTS[r]
                        tt(yt[:, :], vT4[:, c4 * 128:c4 * 128 + 128],
                           ba[0:128, 0:128], ALU.subtract, ["d_vT4", bak], [ytk])
                        tsc(yt[:, :], yt[:, :], e["bcol"], ALU.mult,
                            [ytk, "colpk"], [ytk])
                        e["cur"], e["curk"] = yt[:, :], ytk
                        e["bidx"] = 0
                    for pi in (5, 4, 3, 2, 1, 0):
                        for r in range(2):
                            e = E[r]; bb, bbk = BB[r]
                            mm(bb[0:128, 0:128],
                               powt[:, e["pslot"] + pi * 128:
                                    e["pslot"] + pi * 128 + 128],
                               e["cur"], [e["ptk"], e["curk"]], [bbk])
                        for r in range(2):
                            e = E[r]; bb, bbk = BB[r]
                            nxt, nxtk = BUFS[r][e["bidx"]]
                            tt(nxt, e["cur"], bb[0:128, 0:128], ALU.add,
                               [e["curk"], bbk], [nxtk])
                            e["cur"], e["curk"] = nxt, nxtk
                            e["bidx"] ^= 1
                    for r in range(2):
                        e = E[r]; bb, bbk = BB[r]
                        mm(bb[0:128, 0:128], MTt[:, e["base"]:e["base"] + 128],
                           e["cur"], ["d_MTt", e["curk"]], [bbk])
                    for r in range(2):
                        e = E[r]; bb, bbk = BB[r]
                        nxt, nxtk = BUFS[r][e["bidx"]]
                        tt(nxt, e["cur"], bb[0:128, 0:128], ALU.subtract,
                           [e["curk"], bbk], [nxtk])
                        e["Wf"], e["Wfk"] = nxt, nxtk
                    if h == 0 and bi == 0 and c4 == 0:
                        dbgw("d_Y", Yt[:, :], ["d_Yt0"])
                        dbgw("d_W", E[0]["Wf"], [E[0]["Wfk"]])
                        dbgw("d_powt", powt[:, 0:768], ["d_powt00"])
                    for r in range(2):
                        e = E[r]; ba, bak = BA[r]
                        mm(ba[0:128, 0:128], Xt[:, e["base"]:e["base"] + 128],
                           e["Wf"], ["d_Xt", e["Wfk"]], [bak],
                           start=True, stop=False)
                        mm(ba[0:128, 0:128], qck[:, e["base"]:e["base"] + 128],
                           e["Ssl"], ["d_qck", e["sk"]], [bak],
                           start=False, stop=True)
                    tsc(ocm[:, :], BA[0][0][0:128, 0:128], lcol, ALU.mult,
                        ["pm0", "colpk"], ["d_ocm"])
                    stt(ocm[:, :], BA[1][0][0:128, 0:128], olcol, ocm[:, :],
                        ALU.mult, ALU.add, ["pb4", "omlam", "d_ocm"], ["d_ocm"])
                    for r in range(2):
                        e = E[r]; bb, bbk = BB[r]
                        mm(bb[0:128, 0:128], khTt[:, e["base"]:e["base"] + 128],
                           e["Wf"], ["d_khTt", e["Wfk"]], [bbk])
                    for r in range(2):
                        e = E[r]; bb, bbk = BB[r]
                        stt(e["Ssl"], e["Ssl"], e["gcol"], bb[0:128, 0:128],
                            ALU.mult, ALU.add, [e["sk"], "d_gend", bbk],
                            [e["sk"]])
                    if h == 0 and bi == 0 and c4 == 0:
                        dbgw("d_S1", E[0]["Ssl"], ["Sst0"])
                    P.dma("scalar", cc_o_in[h, cn * 128:cn * 128 + 128, :],
                          ocm[:, :], reads=["d_ocm"], writes=["ccoin"])
            P.barrier()

    def build_phase4():
        rnk = sb("rnk", 1, 1, I32)
        P.dma("sync", rnk[:, :], rank_i, writes=["rnk"])
        greg = ctx.enter_context(nc.gpsimd.register("gr_rank"))
        greg2 = ctx.enter_context(nc.gpsimd.register("gr_off"))
        P.wait_keys("gpsimd", ["rnk", "ccoout"])
        P.raw("gpsimd", lambda e: e.reg_load(greg, rnk[0:1, 0:1]))
        P.raw("gpsimd", lambda e: e.reg_mul(greg, greg, TC * DV))
        with contextlib.ExitStack() as c5:
            otok = sb("otok", 128, OSEG, cx=c5)
            sqo = sb("sqo", 128, 128, cx=c5)
            rmsr = sb("rmsr", 128, 16 * (OSEG // 128), cx=c5)
            xg_ = sb("xg_", 128, OSEG, cx=c5)
            sgg = sb("sgg", 128, OSEG, cx=c5)
            ogT = sb("ogT", 128, 16 * OSEG, cx=c5)
            x2T = sb("x2T", 128, 16 * OSEG, cx=c5)
            h2hi = sb("h2hi", 128, 16 * OSEG, BF16, cx=c5)
            h2lo = sb("h2lo", 128, 16 * OSEG, BF16, cx=c5)
            p4pool = sb("p4pool", 128, max(16 * OSEG, 8192), cx=c5)
            fhi = p4pool[:, 0:8 * OSEG].bitcast(BF16)
            flo = p4pool[:, 8 * OSEG:16 * OSEG].bitcast(BF16)
            dacc = ogT
            wos = p4pool[:, 4096:6144]
            g1s = p4pool[:, 0:2048]
            g2s = p4pool[:, 2048:4096]
            g2bs = sb("g2bs", 128, 16, cx=c5)
            sgc = sb("sgc", 128, 64, cx=c5)
            suc = sb("suc", 128, 64, cx=c5)
            wga = sb("wga", 128, 16 * 128, BF16, cx=c5)
            wua = sb("wua", 128, 16 * 128, BF16, cx=c5)
            wda = sb("wda", 128, 16 * 128, BF16, cx=c5)
            wgb_, wub_, wdb_ = wga, wua, wda
            xs0 = sb("p4x", 128, OSEG, cx=c5)
            ht0 = sb("p4h", 128, OSEG, cx=c5)
            bc1 = sb("p4b1", 128, OSEG, cx=c5)
            bc2 = sb("p4b2", 128, OSEG, cx=c5)
            pt1 = sb("p4t1", 128, OSEG, cx=c5)
            mrow = sb("p4mr", 1, OSEG, cx=c5)
            rrow2 = sb("p4rr", 1, OSEG, cx=c5)
            outt = sb("outt", 128, 128, cx=c5)
            P.dma("sync", g2bs[:, :], g2b_i, writes=["g2bs"])
            P.dma("sync", sgc[:, :], sg_i, writes=["sgc"])
            P.dma("sync", suc[:, :], su_i, writes=["suc"])
            for os_ in range(NOSEG):
                o0 = os_ * OSEG
                sro, cco = o0 // 512, o0 % 512
                P.raw("gpsimd", lambda e, o=o0 * DV: e.reg_add(greg2, greg, o))
                P.dma("gpsimd", o_own,
                      (cc_o_out_h, [[T * DV, 16], [DV, OSEG], [1, DV]]),
                      reads=["ccoout"], writes=["o_own"], offset_reg=greg2)
                P.dma("sync", g1s[:, :], g1_i, writes=["p4pool"])
                P.dma("sync", g2s[:, :], g2_i, writes=["p4pool"])
                bc(bc1[:, :], "p4b1", mur, "mur", sro, cco, OSEG, NRo)
                bc(bc2[:, :], "p4b2", rr_, "rr_", sro, cco, OSEG, NRo)
                for dc in range(16):
                    P.dma("sync", xs0[:, :], x_ownT[dc, :, o0:o0 + OSEG],
                          writes=["p4x"])
                    tt(ht0[:, :], xs0[:, :], bc1[:, :], ALU.subtract,
                       ["p4x", "p4b1"], ["p4h"])
                    tt(ht0[:, :], ht0[:, :], bc2[:, :], ALU.mult,
                       ["p4h", "p4b2"], ["p4h"])
                    tsc(ht0[:, :], ht0[:, :], ca1[:, dc:dc + 1], ALU.mult,
                        ["p4h", "ca1", "modv"], ["p4h"],
                        s2=modv[:, dc:dc + 1], op1=ALU.add)
                    mm(pb[3][0:128, 0:OSEG], g1s[:, dc * 128:dc * 128 + 128],
                       ht0[:, :], ["p4pool", "p4h"], ["pb3"],
                       start=(dc == 0), stop=(dc == 15))
                cpy(xg_[:, :], pb[3][0:128, 0:OSEG], ["pb3"], ["xg_"])
                for hv in range(16):
                    mm(pb[4][0:128, 0:OSEG], g2s[:, hv * 128:hv * 128 + 128],
                       xg_[:, :], ["p4pool", "xg_"], ["pb4"])
                    act(sgg[:, :], pb[4][0:128, 0:OSEG], AF.Sigmoid,
                        ["pb4", "g2bs"], ["sgg"], bias=g2bs[:, hv:hv + 1])
                    for tb in range(OSEG // 128):
                        P.dma("sync", otok[:, tb * 128:tb * 128 + 128],
                              o_own[hv, tb * 128 * DV:(tb + 1) * 128 * DV]
                              .rearrange("(t v) -> t v", v=DV),
                              reads=["o_own"], writes=["otok"])
                        act(sqo[:, :], otok[:, tb * 128:tb * 128 + 128],
                            AF.Square, ["otok"], ["sqo", "rmsr"],
                            accum=rmsr[:, hv * (OSEG // 128) + tb:
                                       hv * (OSEG // 128) + tb + 1])
                    rsl = rmsr[:, hv * (OSEG // 128):(hv + 1) * (OSEG // 128)]
                    tsc(rsl, rsl, 1.0 / DV, ALU.mult, ["rmsr"], ["rmsr"],
                        s2=NORM_EPS, op1=ALU.add)
                    act(rsl, rsl, AF.Sqrt, ["rmsr"], ["rmsr"])
                    recip(rsl, rsl, ["rmsr"], ["rmsr"])
                    for tb in range(OSEG // 128):
                        stt(otok[:, tb * 128:tb * 128 + 128],
                            otok[:, tb * 128:tb * 128 + 128],
                            rmsr[:, hv * (OSEG // 128) + tb:
                                 hv * (OSEG // 128) + tb + 1],
                            onbc[:, :], ALU.mult, ALU.mult,
                            ["otok", "rmsr", "onbc"], ["otok"])
                        tr(pm[0][0:128, tb * 128:tb * 128 + 128],
                           otok[:, tb * 128:tb * 128 + 128], 128, ["otok"],
                           ["pm0"])
                    cpy(ogT[:, hv * OSEG:(hv + 1) * OSEG],
                        pm[0][0:128, 0:OSEG], ["pm0"], ["ogT"])
                    tt(ogT[:, hv * OSEG:(hv + 1) * OSEG],
                       ogT[:, hv * OSEG:(hv + 1) * OSEG], sgg[:, :], ALU.mult,
                       ["ogT", "sgg"], ["ogT"])
                for dc in range(16):
                    P.dma("sync", wos[:, :], wo_i[dc], writes=["p4pool"])
                    for hv in range(16):
                        mm(pb[5][0:128, 0:OSEG], wos[:, hv * 128:hv * 128 + 128],
                           ogT[:, hv * OSEG:(hv + 1) * OSEG], ["p4pool", "ogT"],
                           ["pb5"], start=(hv == 0), stop=(hv == 15))
                    P.dma("sync", xs0[:, :], x_ownT[dc, :, o0:o0 + OSEG],
                          writes=["p4x"])
                    stt(x2T[:, dc * OSEG:(dc + 1) * OSEG], pb[5][0:128, 0:OSEG],
                        gacl[:, dc:dc + 1], xs0[:, :], ALU.mult, ALU.add,
                        ["pb5", "gacl", "p4x"], ["x2T"])
                if os_ == 0:
                    dbgw("x2T", x2T[:, :], ["x2T"])
                for dc in range(16):
                    xsl = x2T[:, dc * OSEG:(dc + 1) * OSEG]
                    act(pt1[:, :], xsl, AF.Square, ["x2T"], ["p4t1"])
                    mm(pb[0][0:1, 0:OSEG], ones[:, 0:1], xsl, ["ones", "x2T"],
                       ["pb0"], start=(dc == 0), stop=(dc == 15))
                    mm(pb[1][0:1, 0:OSEG], ones[:, 0:1], pt1[:, :],
                       ["ones", "p4t1"], ["pb1"], start=(dc == 0), stop=(dc == 15))
                tsc(mrow[0:1, :], pb[0][0:1, 0:OSEG], 1.0 / D, ALU.mult,
                    ["pb0"], ["p4mr"])
                tsc(rrow2[0:1, :], pb[1][0:1, 0:OSEG], 1.0 / D, ALU.mult,
                    ["pb1"], ["p4rr"])
                tt(pt1[0:1, :], mrow[0:1, :], mrow[0:1, :], ALU.mult,
                   ["p4mr"], ["p4t1"])
                tt(rrow2[0:1, :], rrow2[0:1, :], pt1[0:1, :], ALU.subtract,
                   ["p4rr", "p4t1"], ["p4rr"])
                tsc(rrow2[0:1, :], rrow2[0:1, :], LN_EPS, ALU.add,
                    ["p4rr"], ["p4rr"])
                act(rrow2[0:1, :], rrow2[0:1, :], AF.Sqrt, ["p4rr"], ["p4rr"])
                recip(rrow2[0:1, :], rrow2[0:1, :], ["p4rr"], ["p4rr"])
                mm(pm[1][0:128, 0:OSEG], ones[0:1, 0:128], mrow[0:1, 0:OSEG],
                   ["ones", "p4mr"], ["pm1"])
                cpy(bc1[:, :], pm[1][0:128, 0:OSEG], ["pm1"], ["p4b1"])
                mm(pm[1][0:128, 0:OSEG], ones[0:1, 0:128], rrow2[0:1, 0:OSEG],
                   ["ones", "p4rr"], ["pm1"])
                cpy(bc2[:, :], pm[1][0:128, 0:OSEG], ["pm1"], ["p4b2"])
                for dc in range(16):
                    xsl = x2T[:, dc * OSEG:(dc + 1) * OSEG]
                    tt(pt1[:, :], xsl, bc1[:, :], ALU.subtract,
                       ["x2T", "p4b1"], ["p4t1"])
                    tt(pt1[:, :], pt1[:, :], bc2[:, :], ALU.mult,
                       ["p4t1", "p4b2"], ["p4t1"])
                    tsc(pt1[:, :], pt1[:, :], cb1[:, dc:dc + 1], ALU.mult,
                        ["p4t1", "cb1", "modv"], ["p4t1"],
                        s2=modv[:, 48 + dc:48 + dc + 1], op1=ALU.add)
                    cpy(h2hi[:, dc * OSEG:(dc + 1) * OSEG], pt1[:, :],
                        ["p4t1"], ["h2hi"], eng="scalar")
                    tt(h2lo[:, dc * OSEG:(dc + 1) * OSEG], pt1[:, :],
                       h2hi[:, dc * OSEG:(dc + 1) * OSEG], ALU.subtract,
                       ["p4t1", "h2hi"], ["h2lo"])
                for icb in range(4):
                    for iq in range(16):
                        ici = icb * 16 + iq
                        wgt, wgk = wga, "wga"
                        wut, wuk = wua, "wua"
                        P.dma("gpsimd", wgt[:, :], wg8[ici], writes=[wgk])
                        P.dma("gpsimd", wut[:, :], wu8[ici], writes=[wuk])
                        for dc in range(16):
                            hsl = h2hi[:, dc * OSEG:(dc + 1) * OSEG]
                            lsl = h2lo[:, dc * OSEG:(dc + 1) * OSEG]
                            wsl = wgt[:, dc * 128:dc * 128 + 128]
                            usl = wut[:, dc * 128:dc * 128 + 128]
                            mm(pb[0][0:128, 0:OSEG], wsl, hsl, [wgk, "h2hi"],
                               ["pb0"], start=(dc == 0), stop=False)
                            mm(pb[0][0:128, 0:OSEG], wsl, lsl, [wgk, "h2lo"],
                               ["pb0"], start=False, stop=(dc == 15))
                            mm(pb[1][0:128, 0:OSEG], usl, hsl, [wuk, "h2hi"],
                               ["pb1"], start=(dc == 0), stop=False)
                            mm(pb[1][0:128, 0:OSEG], usl, lsl, [wuk, "h2lo"],
                               ["pb1"], start=False, stop=(dc == 15))
                        act(sgg[:, :], pb[0][0:128, 0:OSEG], AF.Silu,
                            ["pb0", "sgc"], ["sgg"], scale=sgc[:, ici:ici + 1])
                        stt(pt1[:, :], pb[1][0:128, 0:OSEG], suc[:, ici:ici + 1],
                            sgg[:, :], ALU.mult, ALU.mult,
                            ["pb1", "suc", "sgg"], ["p4t1"])
                        cpy(fhi[:, iq * OSEG:(iq + 1) * OSEG], pt1[:, :],
                            ["p4t1"], ["p4pool"], eng="scalar")
                        tt(flo[:, iq * OSEG:(iq + 1) * OSEG], pt1[:, :],
                           fhi[:, iq * OSEG:(iq + 1) * OSEG], ALU.subtract,
                           ["p4t1", "p4pool"], ["p4pool"])
                    for dcol in range(16):
                        wdt, wdk = wda, "wda"
                        P.dma("gpsimd", wdt[:, :], wd8[dcol, icb], writes=[wdk])
                        pk = 2 + (dcol % 4)
                        for iq in range(16):
                            wsl = wdt[:, iq * 128:iq * 128 + 128]
                            mm(pb[pk][0:128, 0:OSEG], wsl,
                               fhi[:, iq * OSEG:(iq + 1) * OSEG], [wdk, "p4pool"],
                               [PBK[pk]], start=(iq == 0), stop=False)
                            mm(pb[pk][0:128, 0:OSEG], wsl,
                               flo[:, iq * OSEG:(iq + 1) * OSEG], [wdk, "p4pool"],
                               [PBK[pk]], start=False, stop=(iq == 15))
                        dsl = dacc[:, dcol * OSEG:(dcol + 1) * OSEG]
                        if icb == 0:
                            cpy(dsl, pb[pk][0:128, 0:OSEG], [PBK[pk]], ["ogT"])
                        else:
                            tt(dsl, dsl, pb[pk][0:128, 0:OSEG], ALU.add,
                               ["ogT", PBK[pk]], ["ogT"])
                for dcol in range(16):
                    stt(pt1[:, :], dacc[:, dcol * OSEG:(dcol + 1) * OSEG],
                        sdgb[:, dcol:dcol + 1],
                        x2T[:, dcol * OSEG:(dcol + 1) * OSEG], ALU.mult, ALU.add,
                        ["ogT", "sdgb", "x2T"], ["p4t1"])
                    for tb in range(OSEG // 128):
                        tr(pm[0][0:128, 0:128], pt1[:, tb * 128:tb * 128 + 128],
                           128, ["p4t1"], ["pm0"])
                        cpy(outt[:, :], pm[0][0:128, 0:128], ["pm0"], ["outt"])
                        P.dma("sync", out_sl[o0 + tb * 128:o0 + tb * 128 + 128,
                                             dcol * 128:dcol * 128 + 128],
                              outt[:, :], reads=["outt"], writes=["outdma"])
            P.barrier()

    with contextlib.ExitStack() as cdel:
        colpk = sb("colpk", 128, 5 * NCH * HC, cx=cdel)
        omlam = sb("omlam", 128, NCH * HC, cx=cdel)
        rknT = sb("rknT", NR, 512 * HC, cx=cdel)
        bfT = sb("bfT", NR, 512 * HC, cx=cdel)
        bsT = sb("bsT", NR, 512 * HC, cx=cdel)
        arT = sb("arT", NR, 512 * HC * 2, cx=cdel)
        Sst = sb("Sst", 128, HC * 2 * 128, cx=cdel)
        mems(Sst[:, :], 0.0, ["Sst", "Sst0", "Sst1", "Sst2", "Sst3"])
        for h in range(HC):
            build_feats(h)
            build_delta(h)
        dbgw("colpk", colpk[:, :], ["colpk"])
        dbgw("o_comb", cc_o_in, ["ccoin"])
        P.collective("AllGather", RG, cc_o_in.opt(), cc_o_out.opt(),
                     reads=["ccoin"], writes=["ccoout"])
        P.barrier()
    cmid.close()
    build_phase4()
    P.barrier()
    with nc.Block() as block:
        P.finalize(block)
    ctx.close()
    return nc


def stage_inputs(inputs, T):
    ident, ones_, stag, rowsel, masksf, masksu = host_consts()
    TC = T // NCORE
    WS = 12288 // NCORE
    f = np.float32
    x = np.asarray(inputs["L0_attn_hidden_states"], f)[0]
    wq = np.asarray(inputs["L0_attn_wq"], f)
    wk = np.asarray(inputs["L0_attn_wk"], f)
    wv = np.asarray(inputs["L0_attn_wv"], f)
    cq = np.asarray(inputs["L0_attn_conv_q_w"], f)
    ck = np.asarray(inputs["L0_attn_conv_k_w"], f)
    cv = np.asarray(inputs["L0_attn_conv_v_w"], f)
    alog = np.asarray(inputs["L0_attn_A_log"], f)
    dtb = np.asarray(inputs["L0_attn_dt_bias"], f)
    proxy = np.asarray(inputs["L0_attn_proxy_w"], f)
    uncw = np.asarray(inputs["L0_attn_unc_w"], f)
    he = np.asarray(inputs["L0_attn_head_embed"], f)
    w1 = np.asarray(inputs["L0_attn_mlp_w1"], f)
    b1 = np.asarray(inputs["L0_attn_mlp_b1"], f)
    w2 = np.asarray(inputs["L0_attn_mlp_w2"], f)
    b2 = np.asarray(inputs["L0_attn_mlp_b2"], f)
    w5 = np.stack([np.asarray(inputs["L0_attn_" + n], f)
                   for n in ("bb_w", "bd_w", "lam_w", "ab_w", "ad_w")], 1)
    b5 = np.stack([np.asarray(inputs["L0_attn_" + n], f)[0]
                   for n in ("bb_b", "bd_b", "lam_b", "ab_b", "ad_b")])[None]
    g1 = np.asarray(inputs["L0_attn_g1_w"], f)
    g2 = np.asarray(inputs["L0_attn_g2_w"], f)
    g2b = np.asarray(inputs["L0_attn_g2_b"], f)
    onorm = np.asarray(inputs["L0_attn_onorm_w"], f)
    wo = np.asarray(inputs["L0_attn_wo"], f)
    wg = np.asarray(inputs["L0_ffn_w_gate"])
    wu = np.asarray(inputs["L0_ffn_w_up"])
    wd = np.asarray(inputs["L0_ffn_w_down"])
    sgv = np.asarray(inputs["L0_ffn_s_gate"], f)
    suv = np.asarray(inputs["L0_ffn_s_up"], f)
    sdv = np.asarray(inputs["L0_ffn_s_down"], f)
    cvec = np.asarray(inputs["L0_c"], f)
    wmod = np.asarray(inputs["L0_wmod"], f)

    g1h = np.ascontiguousarray(
        g1.reshape(16, 128, 128).transpose(1, 0, 2).reshape(128, 2048))
    g2h = np.ascontiguousarray(g2.reshape(128, 2048))
    g2bh = np.ascontiguousarray(g2b.reshape(16, 128).T)
    woh = np.ascontiguousarray(
        wo.reshape(16, 128, 16, 128).transpose(2, 1, 0, 3).reshape(16, 128, 2048))
    wgh = np.ascontiguousarray(
        wg.reshape(16, 128, 64, 128).transpose(2, 1, 0, 3).reshape(64, 128, 2048))
    wuh = np.ascontiguousarray(
        wu.reshape(16, 128, 64, 128).transpose(2, 1, 0, 3).reshape(64, 128, 2048))
    wdh = np.ascontiguousarray(
        wd.reshape(4, 16, 128, 16, 128).transpose(3, 0, 2, 1, 4)
        .reshape(16, 4, 128, 2048))
    sgh = np.ascontiguousarray(sgv.reshape(64, 128).T)
    suh = np.ascontiguousarray(suv.reshape(64, 128).T)
    sdh = np.ascontiguousarray(sdv.reshape(16, 128).T)
    unch = np.ascontiguousarray(
        uncw.reshape(16, 128, 64).transpose(1, 0, 2).reshape(128, 1024))
    cvh = np.ascontiguousarray(cvec[0].reshape(16, 128).T)

    in_maps = []
    for c in range(NCORE):
        hc0 = c * HC * 128
        wq_c = wq[:, hc0:hc0 + HC * 128].reshape(16, 128, HC, 128)
        wk_c = wk[:, hc0:hc0 + HC * 128].reshape(16, 128, HC, 128)
        wv_c = wv[:, hc0:hc0 + HC * 128].reshape(16, 128, HC, 128)
        wqkv_c = np.stack([wq_c, wk_c, wv_c], 1)
        wqkv_c = np.ascontiguousarray(
            wqkv_c.transpose(2, 0, 1, 3, 4).reshape(128, -1))
        convw_c = np.stack([cq[hc0:hc0 + HC * 128].reshape(HC, 128, 4),
                            ck[hc0:hc0 + HC * 128].reshape(HC, 128, 4),
                            cv[hc0:hc0 + HC * 128].reshape(HC, 128, 4)], 0)
        convw_c = np.ascontiguousarray(
            convw_c.transpose(3, 0, 1, 2).reshape(-1, 3 * HC * 4, order="F")
            if False else
            convw_c.transpose(1, 0, 3, 2).reshape(128, -1))
        convw_c = np.ascontiguousarray(
            np.stack([cq[hc0:hc0 + HC * 128], ck[hc0:hc0 + HC * 128],
                      cv[hc0:hc0 + HC * 128]], 0)
            .reshape(3, HC, 128, 4).transpose(2, 0, 1, 3).reshape(128, -1))
        m = {
            "x_ownT": np.ascontiguousarray(
                x[c * TC:(c + 1) * TC].T.reshape(16, 128, TC)),
            "wqkv": wqkv_c,
            "convw_i": convw_c,
            "alog": np.ascontiguousarray(alog[c * HC:(c + 1) * HC][None]),
            "dtb": np.ascontiguousarray(
                dtb[hc0:hc0 + HC * 128].reshape(HC, 128).T),
            "proxy": proxy,
            "uncw": unch,
            "he": np.ascontiguousarray(he[c * HC:(c + 1) * HC].T),
            "w1d_i": np.ascontiguousarray(w1[0:4]),
            "w1e_i": np.ascontiguousarray(w1[4:8]),
            "b1_i": np.ascontiguousarray(b1[:, None]),
            "w2_i": w2,
            "b2_i": np.ascontiguousarray(b2[:, None]),
            "w5_i": np.ascontiguousarray(w5),
            "b5_i": np.ascontiguousarray(b5),
            "g1_i": g1h,
            "g2_i": g2h,
            "g2b_i": g2bh,
            "onorm_i": np.ascontiguousarray(onorm[:, None]),
            "wo_i": woh,
            "wg8": wgh,
            "wu8": wuh,
            "wd8": wdh,
            "sg_i": sgh,
            "su_i": suh,
            "sd_i": sdh,
            "cvecT": cvh,
            "wmod_i": np.ascontiguousarray(
                wmod[:, c * WS:(c + 1) * WS].reshape(16, 128, WS)),
            "ident_i": ident,
            "ones_i": ones_,
            "stag_i": stag,
            "rowsel_i": rowsel,
            "masksf_i": masksf,
            "masksu_i": masksu,
            "rank_i": np.array([[c]], dtype=np.int32),
        }
        in_maps.append(m)
    return in_maps


def kernel(**inputs):
    T = int(np.asarray(inputs["L0_attn_hidden_states"]).shape[1])
    nc = build(T)
    in_maps = stage_inputs(inputs, T)
    res = run_bass_kernel_spmd(nc, in_maps, core_ids=list(range(NCORE)))
    out = np.concatenate(
        [np.asarray(res.results[i]["out_sl"]) for i in range(NCORE)], 0)
    return out.reshape(1, T, D).astype(np.float32)


# version 17
# speedup vs baseline: 1.4000x; 1.0725x over previous
import contextlib
import numpy as np
import concourse.bass as bass
import concourse.mybir as mybir
from concourse.bass_utils import run_bass_kernel_spmd

F32 = mybir.dt.float32
BF16 = mybir.dt.bfloat16
I8 = mybir.dt.int8
I32 = mybir.dt.int32
AF = mybir.ActivationFunctionType
ALU = mybir.AluOpType
AX = mybir.AxisListType

NCORE = 8
D = 2048
H = 16
HC = H // NCORE
DK = 128
DV = 128
BINS = 64
CH = 128
CB = 4
EPS = 1e-6
NORM_EPS = 1e-5
LN_EPS = 1e-5
DKS = float(DK) ** -0.5
LVLS = (8, 16, 32, 64, 128)
LOG64 = float(np.log(64.0))


class Prog:

    ENGS = ("tensor", "vector", "scalar", "gpsimd", "sync")
    QUEUES = ("sync", "scalar", "gpsimd")

    def __init__(self, nc):
        self.nc = nc
        self.thunks = []
        self.cnt = {e: 0 for e in self.ENGS}
        self.dcnt = {q: 0 for q in self.QUEUES}
        self.waited = {}
        self.writer = {}
        self.readers = {}
        self.sems = {}
        self.dsems = {}

    NDS = 8

    def init_sems(self, ctx):
        for e in self.ENGS:
            self.sems[e] = ctx.enter_context(self.nc.semaphore(f"se_{e}"))
        for q in self.QUEUES:
            self.dsems[q] = [ctx.enter_context(self.nc.semaphore(f"sd_{q}{j}"))
                             for j in range(self.NDS)]
        self.duse = {q: [0] * self.NDS for q in self.QUEUES}

    def _wait(self, eng, dep, allow_self=False):
        kind, name, cnt = dep
        if kind == "e" and name == eng and not allow_self:
            return
        if kind == "e":
            key = (eng, kind, name)
            if self.waited.get(key, 0) >= cnt:
                return
            self.waited[key] = cnt
            sem = self.sems[name]
            self.thunks.append((eng, lambda e, s=sem, t=cnt: e.wait_ge(s, t)))
        else:
            q, slot = name, (cnt - 1) % self.NDS
            tgt = 16 * ((cnt - 1) // self.NDS + 1)
            key = (eng, "d", q, slot)
            if self.waited.get(key, 0) >= tgt:
                return
            self.waited[key] = tgt
            sem = self.dsems[q][slot]
            self.thunks.append((eng, lambda e, s=sem, t=tgt: e.wait_ge(s, t)))

    def _record(self, owner, reads, writes):
        for k in reads:
            self.readers.setdefault(k, []).append(owner)
        for k in writes:
            self.writer[k] = owner
            self.readers[k] = []

    def _deps(self, eng, reads, writes):
        for k in reads:
            w = self.writer.get(k)
            if w:
                self._wait(eng, w, allow_self=True)
        for k in writes:
            w = self.writer.get(k)
            if w:
                self._wait(eng, w)
            for r in self.readers.get(k, ()):
                self._wait(eng, r)

    def op(self, eng, fn, reads=(), writes=()):
        self._deps(eng, reads, writes)
        self.cnt[eng] += 1
        self.thunks.append(
            (eng, lambda e, f=fn, g=eng: f(e).then_inc(self.sems[g], 1)))
        self._record(("e", eng, self.cnt[eng]), reads, writes)

    def raw(self, eng, fn):
        self.thunks.append((eng, lambda e, f=fn: f(e)))

    def dma(self, queue, out, in_, reads=(), writes=(), offset_reg=None):
        self._deps(queue, reads, writes)
        self.dcnt[queue] += 1
        slot = (self.dcnt[queue] - 1) % self.NDS
        prev_uses = self.duse[queue][slot]
        self.duse[queue][slot] = prev_uses + 1
        sem = self.dsems[queue][slot]
        if prev_uses:
            key = (queue, "d", queue, slot)
            if self.waited.get(key, 0) < 16 * prev_uses:
                self.waited[key] = 16 * prev_uses
                self.thunks.append(
                    (queue, lambda e, s=sem, t=16 * prev_uses: e.wait_ge(s, t)))
        if offset_reg is None:
            self.thunks.append(
                (queue, lambda e, o=out, i=in_, s=sem:
                 e.dma_start(out=o, in_=i).then_inc(s, 16)))
        else:
            self.thunks.append(
                (queue, lambda e, o=out, i=in_, s=sem, r=offset_reg:
                 e.dma_start(out=o, in_=bass.AP(i[0], r, i[1]))
                 .then_inc(s, 16)))
        self._record(("d", queue, self.dcnt[queue]), reads, writes)

    def collective(self, kind, rg, in_ap, out_ap, reads=(), writes=()):
        self._deps("gpsimd", reads, writes)
        self.cnt["gpsimd"] += 1

        def thunk(e, k=kind, r=rg, i=in_ap, o=out_ap):
            e.collective_compute(k, ALU.bypass, replica_groups=r,
                                 ins=[i], outs=[o]).then_inc(self.sems["gpsimd"], 1)
        self.thunks.append(("gpsimd", thunk))
        self._record(("e", "gpsimd", self.cnt["gpsimd"]), reads, writes)

    def wait_keys(self, eng, keys):
        for k in keys:
            w = self.writer.get(k)
            if w:
                self._wait(eng, w)

    def barrier(self):
        for e in self.ENGS:
            for o in self.ENGS:
                if o != e and self.cnt[o] > 0:
                    self._wait(e, ("e", o, self.cnt[o]))
            for q in self.QUEUES:
                n = self.dcnt[q]
                for back in range(min(self.NDS, n)):
                    self._wait(e, ("d", q, n - back))

    def finalize(self, block):
        per = {e: [] for e in self.ENGS}
        for eng, thunk in self.thunks:
            per[eng].append(thunk)

        def run(e, lst):
            for t in lst:
                t(e)
        block.tensor(lambda e: run(e, per["tensor"]))
        block.vector(lambda e: run(e, per["vector"]))
        block.scalar(lambda e: run(e, per["scalar"]))
        block.gpsimd(lambda e: run(e, per["gpsimd"]))
        block.sync(lambda e: run(e, per["sync"]))


def host_consts():
    idx = np.arange(CH)
    ms = []
    for w in LVLS:
        if w == 8:
            m = (idx[:, None] // 8 == idx[None, :] // 8) & (idx[:, None] > idx[None, :])
        else:
            blk, half = idx // w, (idx % w) >= (w // 2)
            m = (blk[:, None] == blk[None, :]) & half[:, None] & (~half)[None, :]
        ms.append(m)
    for m in list(ms):
        ms.append(m.T)
    m8x = (idx[:, None] // 8 == idx[None, :] // 8) & (idx[:, None] >= idx[None, :])
    ms.append(m8x.T)
    allm = np.tile(np.stack(ms).astype(np.float32), (1, 1, 4))
    masksf = np.ascontiguousarray(allm[[0, 5, 10]])
    masksu = np.ascontiguousarray(allm[[1, 2, 3, 4, 6, 7, 8, 9]]).astype(np.uint8)
    ident = np.eye(128, dtype=np.float32)
    ones = np.ones((128, 128), dtype=np.float32)
    stag = np.zeros((128, 31), dtype=np.float32)
    stag[:, 15] = 1.0
    rowsel = np.zeros((16, 16 * 128), dtype=np.float32)
    for s in range(16):
        rowsel[s, s * 128:(s + 1) * 128] = 1.0
    return ident, ones, stag, rowsel, masksf, masksu


def build(T, debug=()):
    TC = T // NCORE
    SEG = min(512, TC)
    NSEG = T // SEG
    NCH = T // CH
    NB = NCH // CB
    SB4 = CB * CH
    OSEG = min(512, TC)
    NOSEG = TC // OSEG
    NR = (T + 511) // 512
    NRo = (TC + 511) // 512
    WS = 12288 // NCORE

    nc = bass.Bass("TRN2", target_bir_lowering=False,
                   detect_race_conditions=False)

    def inp(name, shape, dt=F32):
        return nc.dram_tensor(name, shape, dt, kind="ExternalInput").ap()

    x_ownT = inp("x_ownT", [16, 128, TC])
    wqkv = inp("wqkv", [128, 16 * 3 * HC * 128])
    convw_i = inp("convw_i", [128, 3 * HC * 4])
    alog = inp("alog", [1, HC])
    dtb = inp("dtb", [128, HC])
    proxy = inp("proxy", [128, 128])
    uncw = inp("uncw", [128, 16 * BINS])
    he = inp("he", [4, HC])
    w1d_i = inp("w1d_i", [4, 32])
    w1e_i = inp("w1e_i", [4, 32])
    b1_i = inp("b1_i", [32, 1])
    w2_i = inp("w2_i", [32, 32])
    b2_i = inp("b2_i", [32, 1])
    w5_i = inp("w5_i", [32, 5])
    b5_i = inp("b5_i", [1, 5])
    g1_i = inp("g1_i", [128, 16 * 128])
    g2_i = inp("g2_i", [128, 16 * 128])
    g2b_i = inp("g2b_i", [128, 16])
    onorm_i = inp("onorm_i", [128, 1])
    wo_i = inp("wo_i", [16, 128, 16 * 128])
    wg8 = inp("wg8", [64, 128, 16 * 128], I8)
    wu8 = inp("wu8", [64, 128, 16 * 128], I8)
    wd8 = inp("wd8", [16, 4, 128, 16 * 128], I8)
    sg_i = inp("sg_i", [128, 64])
    su_i = inp("su_i", [128, 64])
    sd_i = inp("sd_i", [128, 16])
    cvecT = inp("cvecT", [128, 16])
    wmod_i = inp("wmod_i", [16, 128, WS])
    ident_i = inp("ident_i", [128, 128])
    ones_i = inp("ones_i", [128, 128])
    stag_i = inp("stag_i", [128, 31])
    rowsel_i = inp("rowsel_i", [16, 16 * 128])
    masksf_i = inp("masksf_i", [3, 128, 512])
    masksu_i = inp("masksu_i", [8, 128, 512], mybir.dt.uint8)
    rank_i = inp("rank_i", [1, 1], I32)

    out_sl = nc.dram_tensor("out_sl", [TC, D], F32, kind="ExternalOutput").ap()
    dbg_out = {}
    for name, shape in debug:
        dbg_out[name] = nc.dram_tensor("dbg_" + name, list(shape), F32,
                                       kind="ExternalOutput").ap()

    cc_x_in = nc.dram_tensor("cc_x_in", [16, 128, TC], F32).ap()
    cc_x_out = nc.dram_tensor("cc_x_out", [NCORE, 16, 128, TC], F32).ap()
    cc_m_in = nc.dram_tensor("cc_m_in", [1, WS], F32).ap()
    cc_m_out = nc.dram_tensor("cc_m_out", [NCORE, WS], F32).ap()
    cc_s_in = nc.dram_tensor("cc_s_in", [3, TC], F32).ap()
    cc_s_out = nc.dram_tensor("cc_s_out", [NCORE, 3, TC], F32).ap()
    cc_o_in = nc.dram_tensor("cc_o_in", [HC, T, DV], F32).ap()
    cc_o_out_h = nc.dram_tensor("cc_o_out", [NCORE * HC * T, DV], F32)
    cc_o_out = cc_o_out_h.ap()
    o_own = nc.dram_tensor("o_own", [16, min(512, T // NCORE) * DV], F32).ap()
    qkv_pre = nc.dram_tensor("qkv_pre", [3, HC, 128, T], F32).ap()
    qkv_post = nc.dram_tensor("qkv_post", [3, HC, 128, T], F32).ap()

    RG = [list(range(NCORE))]
    ctx = contextlib.ExitStack()
    P = Prog(nc)
    P.init_sems(ctx)

    def sb(name, p, f, dt=F32, cx=None):
        return (cx or ctx).enter_context(nc.sbuf_tensor(name, [p, f], dt))

    pb = [ctx.enter_context(nc.psum_tensor(f"pb{i}", [128, 512], F32))
          for i in range(6)]
    pm = [ctx.enter_context(nc.psum_tensor(f"pm{i}", [128, 512], F32))
          for i in range(2)]
    PBK = [f"pb{i}" for i in range(6)]
    PMK = ["pm0", "pm1"]

    ident = sb("ident", 128, 128)
    ones = sb("ones", 128, 128)
    stag = sb("stag", 128, 31)
    rowsel = sb("rowsel", 16, 16 * 128)

    modv = sb("modv", 128, 96)
    ca1 = sb("ca1", 128, 16)
    saca = sb("saca", 128, 16)
    cb1 = sb("cb1", 128, 16)
    gacl = sb("gacl", 128, 16)
    sdgb = sb("sdgb", 128, 16)
    small = sb("small", 128, 64)
    gchan = sb("gchan", 128, HC)
    onorms = sb("onorms", 128, 1)
    onbc = sb("onbc", 128, 128)
    convw = sb("convw", 128, 3 * HC * 4)
    proxys = sb("proxys", 128, 128)
    W5b = sb("W5b", 33, 5)
    W5Z = sb("W5Z", 33, 4 * 31)
    w1ds = sb("w1ds", 4, 32)
    w1es = sb("w1es", 4, 32)
    w2s = sb("w2s", 32, 32)
    b1s = sb("b1s", 32, 1)
    b2s = sb("b2s", 32, 1)
    hes = sb("hes", 4, HC)
    b1e = sb("b1e", 32, HC)

    def mm(out, lhsT, rhs, reads, writes, start=True, stop=True):
        assert isinstance(out, bass.AP), ("mm out", out)
        assert isinstance(lhsT, bass.AP), ("mm lhsT", lhsT)
        assert isinstance(rhs, bass.AP), ("mm rhs", rhs)
        P.op("tensor", lambda e, o=out, l=lhsT, r=rhs, s1=start, s2=stop:
             e.matmul(o, l, r, start=s1, stop=s2), reads=reads, writes=writes)

    def tr(out, in_, k, reads, writes):
        P.op("tensor", lambda e, o=out, i=in_, d=ident[0:k, 0:k]:
             e.transpose(o, i, d), reads=list(reads) + ["ident"], writes=writes)

    def tt(out, a, b, op, reads, writes, eng="vector"):
        P.op(eng, lambda e, o=out, x=a, y=b, p=op: e.tensor_tensor(o, x, y, p),
             reads=reads, writes=writes)

    def tsc(out, in_, s1, op0, reads, writes, s2=None, op1=None, eng="vector"):
        if s2 is None:
            P.op(eng, lambda e, o=out, i=in_, s=s1, p=op0:
                 e.tensor_single_scalar(o, i, s, p), reads=reads, writes=writes)
        else:
            P.op(eng, lambda e, o=out, i=in_, a=s1, b=s2, p0=op0, p1=op1:
                 e.tensor_scalar(o, i, a, b, p0, p1), reads=reads, writes=writes)

    def stt(out, in0, scalar, in1, op0, op1, reads, writes, eng="vector"):
        P.op(eng, lambda e, o=out, i0=in0, s=scalar, i1=in1, p0=op0, p1=op1:
             e.scalar_tensor_tensor(o, i0, s, i1, p0, p1),
             reads=reads, writes=writes)

    def act(out, in_, func, reads, writes, bias=0.0, scale=1.0, accum=None):
        P.op("scalar", lambda e, o=out, i=in_, f=func, b=bias, s=scale, a=accum:
             e.activation(out=o, in_=i, func=f, bias=b, scale=s, accum_out=a),
             reads=reads, writes=writes)

    def cpy(out, in_, reads, writes, eng="vector"):
        if eng == "scalar":
            P.op(eng, lambda e, o=out, i=in_: e.copy(o, i),
                 reads=reads, writes=writes)
        else:
            P.op(eng, lambda e, o=out, i=in_: e.tensor_copy(o, i),
                 reads=reads, writes=writes)

    def cpred(out, mask, data, reads, writes):
        P.op("vector", lambda e, o=out, m=mask, d=data:
             e.copy_predicated(o, m, d), reads=reads, writes=writes)

    def mems(ap_, val, writes, eng="vector"):
        P.op(eng, lambda e, a=ap_, v=val: e.memset(a, v), reads=(), writes=writes)

    def scan(out, d0, d1, init, reads, writes):
        P.op("vector", lambda e, o=out, a=d0, b=d1, i=init:
             e.tensor_tensor_scan(o, a, b, i, ALU.add, ALU.add),
             reads=reads, writes=writes)

    def red(out, in_, op, reads, writes):
        P.op("vector", lambda e, o=out, i=in_, p=op:
             e.tensor_reduce(o, i, AX.X, p), reads=reads, writes=writes)

    def recip(out, in_, reads, writes):
        P.op("vector", lambda e, o=out, i=in_: e.reciprocal(o, i),
             reads=reads, writes=writes)

    def dbgw(name, src_ap, reads):
        if name in dbg_out:
            P.dma("sync", dbg_out[name], src_ap, reads=reads)

    def rowbc(dst, dstk, rowt, rowk, s, c0, width, pmk=1):
        mm(pm[pmk][0:128, 0:width], rowsel[0:NR if rowt is not None else NR, 0:0],
           None, [], [])

    def bc(dst, dstk, rowt, rowk, s, c0, width, nr):
        mm(pm[1][0:128, 0:width], rowsel[0:nr, s * 128:s * 128 + 128],
           rowt[0:nr, c0:c0 + width], ["rowsel", rowk], ["pm1"])
        cpy(dst, pm[1][0:128, 0:width], ["pm1"], [dstk])

    P.dma("sync", ident[:, :], ident_i, writes=["ident"])
    P.dma("sync", ones[:, :], ones_i, writes=["ones"])
    P.dma("sync", stag[:, :], stag_i, writes=["stag"])
    P.dma("sync", rowsel[:, :], rowsel_i, writes=["rowsel"])
    P.dma("sync", convw[:, :], convw_i, writes=["convw"])
    P.dma("sync", proxys[:, :], proxy, writes=["proxys"])
    P.dma("sync", onorms[:, :], onorm_i, writes=["onorms"])
    P.dma("sync", w1ds[:, :], w1d_i, writes=["w1ds"])
    P.dma("sync", w1es[:, :], w1e_i, writes=["w1es"])
    P.dma("sync", w2s[:, :], w2_i, writes=["w2s"])
    P.dma("sync", b1s[:, :], b1_i, writes=["b1s"])
    P.dma("sync", b2s[:, :], b2_i, writes=["b2s"])
    P.dma("sync", hes[:, :], he, writes=["hes"])
    P.dma("sync", W5b[0:32, 0:5], w5_i, writes=["W5b"])
    P.dma("sync", W5b[32:33, 0:5], b5_i, writes=["W5b"])
    P.dma("sync", small[:, 0:HC], dtb, writes=["small"])

    mems(W5Z[:, :], 0.0, ["W5Z"])
    tt(W5Z[:, 0 * 31 + 15:0 * 31 + 16], W5b[:, 0:1], W5b[:, 1:2], ALU.add,
       ["W5b"], ["W5Z"])
    tt(W5Z[:, 1 * 31 + 15:1 * 31 + 16], W5b[:, 0:1], W5b[:, 1:2], ALU.subtract,
       ["W5b"], ["W5Z"])
    tt(W5Z[:, 2 * 31 + 15:2 * 31 + 16], W5b[:, 3:4], W5b[:, 4:5], ALU.add,
       ["W5b"], ["W5Z"])
    tt(W5Z[:, 3 * 31 + 15:3 * 31 + 16], W5b[:, 3:4], W5b[:, 4:5], ALU.subtract,
       ["W5b"], ["W5Z"])
    tt(small[0:33, 32:33], W5b[:, 0:1], W5b[:, 1:2], ALU.add, ["W5b"], ["small"])
    tt(small[0:33, 33:34], W5b[:, 0:1], W5b[:, 1:2], ALU.subtract, ["W5b"], ["small"])
    tt(small[0:33, 34:35], W5b[:, 3:4], W5b[:, 4:5], ALU.add, ["W5b"], ["small"])
    tt(small[0:33, 35:36], W5b[:, 3:4], W5b[:, 4:5], ALU.subtract, ["W5b"], ["small"])
    cpy(W5b[:, 0:1], small[0:33, 32:33], ["small"], ["W5b"])
    cpy(W5b[:, 3:4], small[0:33, 34:35], ["small"], ["W5b"])
    cpy(W5b[:, 4:5], small[0:33, 35:36], ["small"], ["W5b"])
    cpy(W5b[:, 1:2], small[0:33, 33:34], ["small"], ["W5b"])

    P.dma("gpsimd", cc_x_in, x_ownT, writes=["ccx"])
    P.collective("AllGather", RG, cc_x_in.opt(), cc_x_out.opt(),
                 reads=["ccx"], writes=["ccxo"])

    aexp = sb("aexp", 1, HC)
    P.dma("sync", aexp[:, :], alog, writes=["aexp"])
    act(aexp[:, :], aexp[:, :], AF.Exp, ["aexp"], ["aexp"])
    act(small[:, 16:16 + HC], small[:, 0:HC], AF.Abs, ["small"], ["small"])
    tsc(small[:, 24:24 + HC], small[:, 16:16 + HC], -1.0, ALU.mult,
        ["small"], ["small"])
    act(small[:, 24:24 + HC], small[:, 24:24 + HC], AF.Exp, ["small"], ["small"])
    tsc(small[:, 24:24 + HC], small[:, 24:24 + HC], 1.0, ALU.add,
        ["small"], ["small"])
    act(small[:, 16:16 + HC], small[:, 24:24 + HC], AF.Ln, ["small"], ["small"])
    tsc(small[:, 24:24 + HC], small[:, 0:HC], 0.0, ALU.max, ["small"], ["small"])
    tt(small[:, 0:HC], small[:, 16:16 + HC], small[:, 24:24 + HC], ALU.add,
       ["small"], ["small"])
    mm(pm[0][0:128, 0:HC], ones[0:1, 0:128], aexp[0:1, 0:HC],
       ["ones", "aexp"], ["pm0"])
    tt(gchan[:, :], small[:, 0:HC], pm[0][0:128, 0:HC], ALU.mult,
       ["small", "pm0"], ["gchan"])
    tsc(gchan[:, :], gchan[:, :], -1.0, ALU.mult, ["gchan"], ["gchan"])
    dbgw("gchan", gchan[:, :], ["gchan"])
    dbgw("sfp", small[:, 0:64], ["small"])

    tr(pm[0][0:1, 0:128], onorms[:, 0:1], 128, ["onorms"], ["pm0"])
    cpy(small[0:1, 0:1], pm[0][0:1, 0:1], ["pm0"], ["small"])
    cpy(aexp[0:1, 0:1], aexp[0:1, 0:1], ["aexp"], ["aexp"])
    onr = sb("onr", 1, 128)
    cpy(onr[0:1, :], pm[0][0:1, 0:128], ["pm0"], ["onr"])
    mm(pm[0][0:128, 0:128], ones[0:1, 0:128], onr[0:1, 0:128], ["ones", "onr"],
       ["pm0"])
    cpy(onbc[:, :], pm[0][0:128, 0:128], ["pm0"], ["onbc"])

    cvec_s = sb("cvec_s", 128, 16)
    modrow = sb("modrow", 1, WS)
    P.dma("sync", cvec_s[:, :], cvecT, writes=["cvec_s"])
    act(cvec_s[:, :], cvec_s[:, :], AF.Silu, ["cvec_s"], ["cvec_s"])
    nblk = (WS + 511) // 512
    with contextlib.ExitStack() as c0:
        wma = sb("wma", 128, WS, cx=c0)
        wmb = sb("wmb", 128, WS, cx=c0)
        for dc in range(16):
            wt, wk_ = (wma, "wma") if dc % 2 == 0 else (wmb, "wmb")
            P.dma("sync", wt[:, :], wmod_i[dc], writes=[wk_])
            for b in range(nblk):
                w = min(512, WS - b * 512)
                mm(pb[b][0:1, 0:w], cvec_s[:, dc:dc + 1],
                   wt[:, b * 512:b * 512 + w], reads=["cvec_s", wk_],
                   writes=[PBK[b]], start=(dc == 0), stop=(dc == 15))
        for b in range(nblk):
            w = min(512, WS - b * 512)
            cpy(modrow[0:1, b * 512:b * 512 + w], pb[b][0:1, 0:w], [PBK[b]],
                ["modrow"])
        dbgw("modrow", modrow[0:1, 0:WS], ["modrow"])
        P.dma("gpsimd", cc_m_in, modrow[0:1, 0:WS], reads=["modrow"], writes=["ccm"])
        P.collective("AllGather", RG, cc_m_in.opt(), cc_m_out.opt(),
                     reads=["ccm"], writes=["ccmo"])
        mvT = sb("mvT", 96, 128, cx=c0)
        P.dma("sync", mvT[:, :],
              cc_m_out.flatten().rearrange("(a b) -> a b", b=128),
              reads=["ccmo"], writes=["mvT"])
        dbgw("mvT", mvT[:, :], ["mvT"])
        tr(pm[0][0:128, 0:96], mvT[0:96, 0:128], 96, ["mvT"], ["pm0"])
        cpy(modv[:, :], pm[0][0:128, 0:96], ["pm0"], ["modv"])
        dbgw("modv", modv[:, :], ["modv"])
        P.barrier()
    tsc(ca1[:, :], modv[:, 16:32], 1.0, ALU.add, ["modv"], ["ca1"])
    tsc(cb1[:, :], modv[:, 64:80], 1.0, ALU.add, ["modv"], ["cb1"])
    recip(small[:, 0:16], ca1[:, :], ["ca1"], ["small"])
    tt(saca[:, :], modv[:, 0:16], small[:, 0:16], ALU.mult, ["modv", "small"],
       ["saca"])
    dbgw("ca1d", ca1[:, :], ["ca1"])
    dbgw("recd", small[:, 0:16], ["small"])
    dbgw("sacaE", saca[:, :], ["saca"])
    if "sacaX" in dbg_out:
        def vflush():
            P.thunks.append(("vector",
                             lambda e, t=P.cnt["vector"]:
                             e.wait_ge(P.sems["vector"], t)))
        rX = sb("rX", 128, 16)
        sacaX = sb("sacaX", 128, 16)
        recip(rX[:, :], ca1[:, :], ["ca1"], ["rX"])
        vflush()
        tt(sacaX[:, :], modv[:, 0:16], rX[:, :], ALU.mult, ["modv", "rX"],
           ["sacaX"])
        vflush()
        dbgw("sacaX", sacaX[:, :], ["sacaX"])
        rZ = sb("rZ", 128, 16)
        sacaZ = sb("sacaZ", 128, 16)
        recip(rZ[:, :], ca1[:, :], ["ca1"], ["rZ"])
        tt(sacaZ[:, :], modv[:, 0:16], rZ[:, :], ALU.mult, ["modv", "rZ"],
           ["sacaZ"])
        dbgw("sacaZ", sacaZ[:, :], ["sacaZ"])
    for srcc, isga in ((32, True), (80, False)):
        src = modv[:, srcc:srcc + 16]
        act(small[:, 0:16], src, AF.Abs, ["modv"], ["small"])
        tsc(small[:, 0:16], small[:, 0:16], 0.25, ALU.max, ["small"], ["small"])
        tsc(small[:, 16:32], src, 0.0, ALU.is_ge, ["modv"], ["small"])
        tsc(small[:, 16:32], small[:, 16:32], 2.0, ALU.mult, ["small"], ["small"],
            s2=1.0, op1=ALU.subtract)
        if isga:
            tt(gacl[:, :], small[:, 0:16], small[:, 16:32], ALU.mult,
               ["small"], ["gacl"])
        else:
            tt(small[:, 32:48], small[:, 0:16], small[:, 16:32], ALU.mult,
               ["small"], ["small"])
            P.dma("sync", sdgb[:, :], sd_i, writes=["sdgb"])
            tt(sdgb[:, :], sdgb[:, :], small[:, 32:48], ALU.mult,
               ["sdgb", "small"], ["sdgb"])

    mur = sb("mur", NRo, 512)
    rr_ = sb("rr_", NRo, 512)
    entr = sb("entr", 1, TC)
    with contextlib.ExitStack() as c1:
        xs0 = sb("xs0", 128, OSEG, cx=c1)
        sq0 = sb("sq0", 128, OSEG, cx=c1)
        for os_ in range(NOSEG):
            o0 = os_ * OSEG
            for dc in range(16):
                P.dma("sync", xs0[:, :], x_ownT[dc, :, o0:o0 + OSEG], writes=["xs0"])
                act(sq0[:, :], xs0[:, :], AF.Square, ["xs0"], ["sq0"])
                mm(pb[0][0:NRo, 0:OSEG],
                   stag[:, 15 - os_:15 - os_ + NRo], xs0[:, :],
                   ["stag", "xs0"], ["pb0"],
                   start=(dc == 0 and os_ == 0), stop=(dc == 15 and os_ == NOSEG - 1))
                mm(pb[1][0:NRo, 0:OSEG],
                   stag[:, 15 - os_:15 - os_ + NRo], sq0[:, :],
                   ["stag", "sq0"], ["pb1"],
                   start=(dc == 0 and os_ == 0), stop=(dc == 15 and os_ == NOSEG - 1))
        tsc(mur[:, 0:OSEG if NRo == 1 else 512], pb[0][0:NRo, 0:OSEG if NRo == 1 else 512],
            1.0 / D, ALU.mult, ["pb0"], ["mur"])
        tsc(rr_[:, 0:OSEG if NRo == 1 else 512], pb[1][0:NRo, 0:OSEG if NRo == 1 else 512],
            1.0 / D, ALU.mult, ["pb1"], ["rr_"])
        tt(sq0[0:NRo, 0:OSEG if NRo == 1 else 512], mur[:, 0:OSEG if NRo == 1 else 512],
           mur[:, 0:OSEG if NRo == 1 else 512], ALU.mult, ["mur"], ["sq0"])
        tt(rr_[:, 0:OSEG if NRo == 1 else 512], rr_[:, 0:OSEG if NRo == 1 else 512],
           sq0[0:NRo, 0:OSEG if NRo == 1 else 512], ALU.subtract,
           ["rr_", "sq0"], ["rr_"])
        tsc(rr_[:, 0:OSEG if NRo == 1 else 512], rr_[:, 0:OSEG if NRo == 1 else 512],
            LN_EPS, ALU.add, ["rr_"], ["rr_"])
        act(rr_[:, 0:OSEG if NRo == 1 else 512], rr_[:, 0:OSEG if NRo == 1 else 512],
            AF.Sqrt, ["rr_"], ["rr_"])
        recip(rr_[:, 0:OSEG if NRo == 1 else 512], rr_[:, 0:OSEG if NRo == 1 else 512],
              ["rr_"], ["rr_"])
        P.barrier()

    with contextlib.ExitStack() as c2:
        xs0 = sb("xs0b", 128, OSEG, cx=c2)
        ht0 = sb("ht0", 128, OSEG, cx=c2)
        bc1 = sb("bc1e", 128, OSEG, cx=c2)
        bc2 = sb("bc2e", 128, OSEG, cx=c2)
        uncs = sb("uncs", 128, 16 * BINS, cx=c2)
        zrows = sb("zrows", BINS, OSEG, cx=c2)
        zsc = sb("zsc", 128, BINS, cx=c2)
        ecol = sb("ecol", 128, BINS, cx=c2)
        tcol = sb("tcol", 128, 8, cx=c2)
        P.dma("sync", uncs[:, :], uncw, writes=["uncs"])
        for os_ in range(NOSEG):
            o0 = os_ * OSEG
            sro, cco = (o0 // 512, o0 % 512)
            bc(bc1[:, :], "bc1e", mur, "mur", sro, cco, OSEG, NRo)
            bc(bc2[:, :], "bc2e", rr_, "rr_", sro, cco, OSEG, NRo)
            for dc in range(16):
                P.dma("sync", xs0[:, :], x_ownT[dc, :, o0:o0 + OSEG], writes=["xs0b"])
                tt(ht0[:, :], xs0[:, :], bc1[:, :], ALU.subtract,
                   ["xs0b", "bc1e"], ["ht0"])
                tt(ht0[:, :], ht0[:, :], bc2[:, :], ALU.mult, ["ht0", "bc2e"], ["ht0"])
                tsc(ht0[:, :], ht0[:, :], ca1[:, dc:dc + 1], ALU.mult,
                    ["ht0", "ca1", "modv"], ["ht0"],
                    s2=modv[:, dc:dc + 1], op1=ALU.add)
                mm(pb[2][0:BINS, 0:OSEG], uncs[:, dc * BINS:(dc + 1) * BINS],
                   ht0[:, :], ["uncs", "ht0"], ["pb2"],
                   start=(dc == 0), stop=(dc == 15))
            cpy(zrows[:, :], pb[2][0:BINS, 0:OSEG], ["pb2"], ["zrows"])
            for tb in range(OSEG // 128):
                t0 = tb * 128
                tr(pm[0][0:128, 0:BINS], zrows[0:BINS, t0:t0 + 128], BINS,
                   ["zrows"], ["pm0"])
                cpy(zsc[:, :], pm[0][0:128, 0:BINS], ["pm0"], ["zsc"])
                if os_ == 0 and tb == 0:
                    dbgw("zsc0", zsc[:, :], ["zsc"])
                red(tcol[:, 0:1], zsc[:, :], ALU.max, ["zsc"], ["tcol"])
                tsc(tcol[:, 1:2], tcol[:, 0:1], -1.0, ALU.mult, ["tcol"], ["tcol"])
                act(ecol[:, :], zsc[:, :], AF.Exp, ["zsc", "tcol"],
                    ["ecol", "tcol"], bias=tcol[:, 1:2], accum=tcol[:, 2:3])
                tt(zsc[:, :], ecol[:, :], zsc[:, :], ALU.mult, ["ecol", "zsc"],
                   ["zsc"])
                red(tcol[:, 3:4], zsc[:, :], ALU.add, ["zsc"], ["tcol"])
                recip(tcol[:, 4:5], tcol[:, 2:3], ["tcol"], ["tcol"])
                tt(tcol[:, 5:6], tcol[:, 3:4], tcol[:, 4:5], ALU.mult,
                   ["tcol"], ["tcol"])
                act(tcol[:, 6:7], tcol[:, 2:3], AF.Ln, ["tcol"], ["tcol"])
                tt(tcol[:, 6:7], tcol[:, 6:7], tcol[:, 0:1], ALU.add,
                   ["tcol"], ["tcol"])
                tt(tcol[:, 7:8], tcol[:, 6:7], tcol[:, 5:6], ALU.subtract,
                   ["tcol"], ["tcol"])
                tsc(tcol[:, 7:8], tcol[:, 7:8], 1.0 / LOG64, ALU.mult,
                    ["tcol"], ["tcol"])
                if os_ == 0 and tb == 0:
                    dbgw("tcol0", tcol[:, :], ["tcol"])
                mm(pm[0][0:1, 0:128], tcol[:, 7:8], ident[:, :],
                   ["tcol", "ident"], ["pm0"])
                cpy(entr[0:1, o0 + t0:o0 + t0 + 128], pm[0][0:1, 0:128],
                    ["pm0"], ["entr"])
        P.barrier()

    st3 = sb("st3", 3, TC)
    for s in range(NRo):
        w = min(512, TC - s * 512)
        P.dma("sync", st3[0:1, s * 512:s * 512 + w], mur[s:s + 1, 0:w],
              reads=["mur"], writes=["st3"])
        P.dma("sync", st3[1:2, s * 512:s * 512 + w], rr_[s:s + 1, 0:w],
              reads=["rr_"], writes=["st3"])
    P.dma("sync", st3[2:3, 0:TC], entr[0:1, 0:TC], reads=["entr"], writes=["st3"])
    P.dma("gpsimd", cc_s_in, st3[0:3, 0:TC], reads=["st3"], writes=["ccs"])
    P.collective("AllGather", RG, cc_s_in.opt(), cc_s_out.opt(),
                 reads=["ccs"], writes=["ccso"])
    cmid = contextlib.ExitStack()
    murow = sb("murow", NR, 512, cx=cmid)
    rrow = sb("rrow", NR, 512, cx=cmid)
    entrow = sb("entrow", NR, 512, cx=cmid)
    masksf = sb("masksf", 128, 3 * 512, cx=cmid)
    masksu = sb("masksu", 128, 8 * 512, mybir.dt.uint8, cx=cmid)
    for i in range(3):
        P.dma("sync", masksf[:, i * 512:(i + 1) * 512], masksf_i[i], writes=["masks"])
    for i in range(8):
        P.dma("sync", masksu[:, i * 512:(i + 1) * 512], masksu_i[i], writes=["masks"])
    for r in range(NCORE):
        t0 = r * TC
        sr0, cc0 = t0 // 512, t0 % 512
        npart = max(1, TC // 512)
        w = min(512, TC)
        for pp in range(npart):
            P.dma("sync", murow[sr0 + pp:sr0 + pp + 1, cc0:cc0 + w],
                  cc_s_out[r, 0, pp * 512:pp * 512 + w], reads=["ccso"],
                  writes=["murow"])
            P.dma("sync", rrow[sr0 + pp:sr0 + pp + 1, cc0:cc0 + w],
                  cc_s_out[r, 1, pp * 512:pp * 512 + w], reads=["ccso"],
                  writes=["rrow"])
            P.dma("sync", entrow[sr0 + pp:sr0 + pp + 1, cc0:cc0 + w],
                  cc_s_out[r, 2, pp * 512:pp * 512 + w], reads=["ccso"],
                  writes=["entrow"])
    dbgw("murow", murow[0:NR, :], ["murow"])
    dbgw("rrow", rrow[0:NR, :], ["rrow"])
    dbgw("entrow", entrow[0:NR, :], ["entrow"])

    with contextlib.ExitStack() as c3:
        wtil = sb("wtil", 128, 16 * 3 * HC * 128, cx=c3)
        ncsum = sb("ncsum", 128, 3 * HC, cx=c3)
        cjt = sb("cjt", 128, 3 * HC, cx=c3)
        xsa = sb("xsa", 128, SEG, cx=c3)
        xsb = sb("xsb", 128, SEG, cx=c3)
        ppre = sb("ppre", 128, SEG, cx=c3)
        bc1 = sb("bc1q", 128, SEG, cx=c3)
        bc2 = sb("bc2q", 128, SEG, cx=c3)

        P.dma("sync", wtil[:, :], wqkv, writes=["wtil"])
        for dc in range(16):
            w0 = dc * (3 * HC * 128)
            tsc(wtil[:, w0:w0 + 3 * HC * 128], wtil[:, w0:w0 + 3 * HC * 128],
                ca1[:, dc:dc + 1], ALU.mult, ["wtil", "ca1"], ["wtil"])

        def wtsl(dc, proj, hcol):
            o = dc * (3 * HC * 128) + proj * (HC * 128) + hcol * 128
            return wtil[:, o:o + 128]

        for proj in range(3):
            for hcol in range(HC):
                j = proj * HC + hcol
                for dc in range(16):
                    mm(pm[0][0:128, j:j + 1], wtsl(dc, proj, hcol),
                       ones[:, 0:1], ["ones", "wtil"], ["pm0"],
                       start=(dc == 0), stop=(dc == 15))
                    mm(pm[1][0:128, j:j + 1], wtsl(dc, proj, hcol),
                       saca[:, dc:dc + 1], ["saca", "wtil"], ["pm1"],
                       start=(dc == 0), stop=(dc == 15))
        dbgw("onesd", ones[:, :], ["ones"])
        dbgw("sacad", saca[:, :], ["saca"])
        cpy(ncsum[:, :], pm[0][0:128, 0:3 * HC], ["pm0"], ["ncsum"])
        tsc(ncsum[:, :], ncsum[:, :], -1.0, ALU.mult, ["ncsum"], ["ncsum"])
        cpy(cjt[:, :], pm[1][0:128, 0:3 * HC], ["pm1"], ["cjt"])

        for sgi in range(NSEG):
            t0 = sgi * SEG
            rk, loc = t0 // TC, t0 % TC
            sr0, cc0 = t0 // 512, t0 % 512
            bc(bc1[:, :], "bc1q", murow, "murow", sr0, cc0, SEG, NR)
            bc(bc2[:, :], "bc2q", rrow, "rrow", sr0, cc0, SEG, NR)
            for dc in range(16):
                xt, xk = ((xsa, "xsa") if dc % 2 == 0 else (xsb, "xsb"))
                P.dma("sync", xt[:, :], cc_x_out[rk, dc, :, loc:loc + SEG],
                      reads=["ccxo"], writes=[xk])
                for proj in range(3):
                    for hcol in range(HC):
                        j = proj * HC + hcol
                        mm(pb[j][0:128, 0:SEG], wtsl(dc, proj, hcol), xt[:, :],
                           ["wtil", xk], [PBK[j]],
                           start=(dc == 0), stop=(dc == 15))
            if sgi == 0:
                dbgw("bc1q", bc1[:, :], ["bc1q"])
                dbgw("bc2q", bc2[:, :], ["bc2q"])
                dbgw("wtl", wtil[:, 0:2048], ["wtil"])
                dbgw("ncs", ncsum[:, :], ["ncsum"])
                dbgw("cjt", cjt[:, :], ["cjt"])
            for proj in range(3):
                for hcol in range(HC):
                    j = proj * HC + hcol
                    stt(ppre[:, :], bc1[:, :], ncsum[:, j:j + 1],
                        pb[j][0:128, 0:SEG], ALU.mult, ALU.add,
                        ["bc1q", "ncsum", PBK[j]], ["ppre"])
                    tt(ppre[:, :], ppre[:, :], bc2[:, :], ALU.mult,
                       ["ppre", "bc2q"], ["ppre"])
                    tsc(ppre[:, :], ppre[:, :], cjt[:, j:j + 1], ALU.add,
                        ["ppre", "cjt"], ["ppre"])
                    P.dma("scalar", qkv_pre[proj, hcol, :, t0:t0 + SEG],
                          ppre[:, :], reads=["ppre"], writes=["qkvpre"])
        P.barrier()

    dbgw("q_pre", qkv_pre[0, 0], ["qkvpre"])
    with contextlib.ExitStack() as c4:
        convin = sb("convin", 128, T + 3, cx=c4)
        convacc = sb("convacc", 128, T, cx=c4)
        for proj in range(3):
            for hcol in range(HC):
                j = proj * HC + hcol
                mems(convin[:, 0:3], 0.0, ["convin"])
                P.dma("sync", convin[:, 3:T + 3], qkv_pre[proj, hcol],
                      reads=["qkvpre"], writes=["convin"])
                tsc(convacc[:, :], convin[:, 0:T], convw[:, j * 4:j * 4 + 1],
                    ALU.mult, ["convin", "convw"], ["convacc"])
                for tau in range(1, 4):
                    stt(convacc[:, :], convin[:, tau:T + tau],
                        convw[:, j * 4 + tau:j * 4 + tau + 1], convacc[:, :],
                        ALU.mult, ALU.add, ["convin", "convw", "convacc"],
                        ["convacc"])
                act(convacc[:, :], convacc[:, :], AF.Silu, ["convacc"],
                    ["convacc"])
                P.dma("scalar", qkv_post[proj, hcol], convacc[:, :],
                      reads=["convacc"], writes=["qkvpost"])
        P.barrier()
    dbgw("q_post", qkv_post[0, 0], ["qkvpost"])
    dbgw("k_post", qkv_post[1, 0], ["qkvpost"])
    dbgw("v_post", qkv_post[2, 0], ["qkvpost"])

    def build_feats(h):
        with contextlib.ExitStack() as cf:
            kt = sb(f"f_kt{h}", 128, 512, cx=cf)
            vt = sb(f"f_vt{h}", 128, 512, cx=cf)
            vh = sb(f"f_vh{h}", 128, 512, cx=cf)
            sq = sb(f"f_sq{h}", 128, 512, cx=cf)
            kn2 = sb(f"f_kn2{h}", NR, 512, cx=cf)
            vn2 = sb(f"f_vn2{h}", NR, 512, cx=cf)
            vhn2 = sb(f"f_vhn2{h}", NR, 512, cx=cf)
            vd = sb(f"f_vd{h}", NR, 512, cx=cf)
            s1 = sb(f"f_s1{h}", NR, 512, cx=cf)
            s2 = sb(f"f_s2{h}", NR, 512, cx=cf)
            s3 = sb(f"f_s3{h}", NR, 512, cx=cf)
            h1t = sb(f"f_h1t{h}", 32, 512, cx=cf)
            afT = sb(f"f_afT{h}", NR, 512, cx=cf)
            asT = sb(f"f_asT{h}", NR, 512, cx=cf)
            zr = sb(f"f_zr{h}", 16, 128, cx=cf)
            feats = sb(f"feats{h}", 4, T, cx=cf)
            hf33 = sb(f"hf33{h}", 33, T, cx=cf)
            for s in range(NR):
                t0 = s * 512
                P.dma("sync", kt[:, :], qkv_post[1, h, :, t0:t0 + 512],
                      reads=["qkvpost"], writes=["f_kt"])
                P.dma("sync", vt[:, :], qkv_post[2, h, :, t0:t0 + 512],
                      reads=["qkvpost"], writes=["f_vt"])
                mm(pm[0][0:128, 0:512], proxys[:, :], kt[:, :], ["proxys", "f_kt"],
                   ["pm0"])
                cpy(vh[:, :], pm[0][0:128, 0:512], ["pm0"], ["f_vh"])
                st_, sp_ = (s == 0), (s == NR - 1)
                act(sq[:, :], kt[:, :], AF.Square, ["f_kt"], ["f_sq"])
                mm(pb[0][0:NR, 0:512], stag[:, 15 - s:15 - s + NR], sq[:, :],
                   ["stag", "f_sq"], ["pb0"], start=st_, stop=sp_)
                act(sq[:, :], vt[:, :], AF.Square, ["f_vt"], ["f_sq"])
                mm(pb[1][0:NR, 0:512], stag[:, 15 - s:15 - s + NR], sq[:, :],
                   ["stag", "f_sq"], ["pb1"], start=st_, stop=sp_)
                act(sq[:, :], vh[:, :], AF.Square, ["f_vh"], ["f_sq"])
                mm(pb[2][0:NR, 0:512], stag[:, 15 - s:15 - s + NR], sq[:, :],
                   ["stag", "f_sq"], ["pb2"], start=st_, stop=sp_)
                tt(sq[:, :], vt[:, :], vh[:, :], ALU.mult, ["f_vt", "f_vh"],
                   ["f_sq"])
                mm(pb[3][0:NR, 0:512], stag[:, 15 - s:15 - s + NR], sq[:, :],
                   ["stag", "f_sq"], ["pb3"], start=st_, stop=sp_)
            cpy(kn2[:, :], pb[0][0:NR, 0:512], ["pb0"], ["f_kn2"])
            cpy(vn2[:, :], pb[1][0:NR, 0:512], ["pb1"], ["f_vn2"])
            cpy(vhn2[:, :], pb[2][0:NR, 0:512], ["pb2"], ["f_vhn2"])
            cpy(vd[:, :], pb[3][0:NR, 0:512], ["pb3"], ["f_vd"])
            act(s1[:, :], kn2[:, :], AF.Sqrt, ["f_kn2"], ["f_s1"])
            tsc(s1[:, :], s1[:, :], EPS, ALU.add, ["f_s1"], ["f_s1"])
            recip(rknT[0:NR, h * 512:(h + 1) * 512], s1[:, :], ["f_s1"], ["rknT"])
            act(s1[:, :], vn2[:, :], AF.Sqrt, ["f_vn2"], ["f_s1"])
            act(s2[:, :], vhn2[:, :], AF.Sqrt, ["f_vhn2"], ["f_s2"])
            tt(s3[:, :], vn2[:, :], vhn2[:, :], ALU.add, ["f_vn2", "f_vhn2"],
               ["f_s3"])
            stt(s3[:, :], vd[:, :], -2.0, s3[:, :], ALU.mult, ALU.add,
                ["f_vd", "f_s3"], ["f_s3"])
            act(s3[:, :], s3[:, :], AF.Sqrt, ["f_s3"], ["f_s3"])
            tsc(vn2[:, :], s1[:, :], EPS, ALU.add, ["f_s1"], ["f_vn2"])
            recip(vn2[:, :], vn2[:, :], ["f_vn2"], ["f_vn2"])
            tt(vn2[:, :], s3[:, :], vn2[:, :], ALU.mult, ["f_s3", "f_vn2"],
               ["f_vn2"])
            tt(vhn2[:, :], s1[:, :], s2[:, :], ALU.mult, ["f_s1", "f_s2"],
               ["f_vhn2"])
            tsc(vhn2[:, :], vhn2[:, :], EPS, ALU.add, ["f_vhn2"], ["f_vhn2"])
            recip(vhn2[:, :], vhn2[:, :], ["f_vhn2"], ["f_vhn2"])
            tt(vhn2[:, :], vd[:, :], vhn2[:, :], ALU.mult, ["f_vd", "f_vhn2"],
               ["f_vhn2"])
            tsc(s3[:, :], s3[:, :], 1.0, ALU.add, ["f_s3"], ["f_s3"])
            act(s3[:, :], s3[:, :], AF.Ln, ["f_s3"], ["f_s3"])
            for s in range(NR):
                P.dma("sync", feats[0:1, s * 512:(s + 1) * 512],
                      entrow[s:s + 1, 0:512], reads=["entrow"], writes=["feats"])
                P.dma("sync", feats[1:2, s * 512:(s + 1) * 512],
                      vn2[s:s + 1, 0:512], reads=["f_vn2"], writes=["feats"])
                P.dma("sync", feats[2:3, s * 512:(s + 1) * 512],
                      vhn2[s:s + 1, 0:512], reads=["f_vhn2"], writes=["feats"])
                P.dma("sync", feats[3:4, s * 512:(s + 1) * 512],
                      s3[s:s + 1, 0:512], reads=["f_s3"], writes=["feats"])
            mm(pm[0][0:32, 0:1], w1es[:, :], hes[:, h:h + 1], ["w1es", "hes"], ["pm0"])
            tt(b1e[:, h:h + 1], pm[0][0:32, 0:1], b1s[:, :], ALU.add,
               ["pm0", "b1s"], ["b1e"])
            mems(hf33[32:33, 0:T], 1.0, ["hf33"])
            for s in range(NR):
                t0 = s * 512
                mm(pm[0][0:32, 0:512], w1ds[:, :], feats[0:4, t0:t0 + 512],
                   ["w1ds", "feats"], ["pm0"])
                act(h1t[:, :], pm[0][0:32, 0:512], AF.Silu, ["pm0", "b1e"],
                    ["f_h1t"], bias=b1e[:, h:h + 1])
                mm(pm[1][0:32, 0:512], w2s[:, :], h1t[:, :], ["w2s", "f_h1t"], ["pm1"])
                act(hf33[0:32, t0:t0 + 512], pm[1][0:32, 0:512], AF.Silu,
                    ["pm1", "b2s"], ["hf33"], bias=b2s[:, :])
            if h == 0:
                dbgw("hf33", hf33[0:33, 0:T], ["hf33"])
            for jj, (j, dstt, dk) in enumerate(
                    ((0, bfT, "bfT"), (1, bsT, "bsT"),
                     (2, afT, "f_afT"), (3, asT, "f_asT"))):
                for s in range(NR):
                    t0 = s * 512
                    mm(pb[jj][0:NR, 0:512],
                       W5Z[:, j * 31 + 15 - s:j * 31 + 15 - s + NR],
                       hf33[:, t0:t0 + 512], ["W5Z", "hf33"], [PBK[jj]],
                       start=(s == 0), stop=(s == NR - 1))
                coff = h * 512 if dstt in (bfT, bsT) else 0
                act(dstt[0:NR, coff:coff + 512], pb[jj][0:NR, 0:512],
                    AF.Sigmoid, [PBK[jj]], [dk])
            for cn in range(NCH):
                mm(pm[0][0:128, cn * 5:cn * 5 + 5], hf33[0:33, cn * 128:cn * 128 + 128],
                   W5b[0:33, 0:5], ["hf33", "W5b"], ["pm0"])
            act(colpk[:, h * 5 * NCH:(h + 1) * 5 * NCH], pm[0][0:128, 0:5 * NCH],
                AF.Sigmoid, ["pm0"], ["colpk"])
            tsc(omlam[:, h * NCH:(h + 1) * NCH],
                colpk[:, h * 5 * NCH + 2:(h + 1) * 5 * NCH:5], -1.0, ALU.mult,
                ["colpk"], ["omlam"], s2=1.0, op1=ALU.add)
            mems(zr[:, :], 0.0, ["f_zr"])
            zrb = zr[0:NR, 0:128]
            for r, srcT, sk in ((0, afT, "f_afT"), (1, asT, "f_asT")):
                for cc in range(4):
                    c0 = cc * 128
                    scan(arT[0:NR, (h * 2 + r) * 512 + c0:(h * 2 + r) * 512 + c0 + 128],
                         srcT[0:NR, c0:c0 + 128], zrb, 0.0,
                         [sk, "f_zr"], ["arT"])
            P.barrier()

    def build_delta(h):
        with contextlib.ExitStack() as cd:
            kap = sb(f"d_kap{h}", 128, SB4, cx=cd)
            qap = sb(f"d_qap{h}", 128, SB4, cx=cd)
            vap = sb(f"d_vap{h}", 128, SB4, cx=cd)
            kup = sb(f"d_kup{h}", 128, SB4, cx=cd)
            kbt = sb(f"d_kbt{h}", 128, SB4, cx=cd)
            khf = sb(f"d_khf{h}", 128, SB4, cx=cd)
            vT4 = sb(f"d_vT4{h}", 128, SB4, cx=cd)
            argt = sb(f"d_argt{h}", 128, 12 * SB4, cx=cd)
            expt = argt
            fkt = sb(f"d_fkt{h}", 128, 5 * SB4, cx=cd)
            fqt = sb(f"d_fqt{h}", 128, 5 * SB4, cx=cd)
            gt = sb(f"d_gt{h}", 128, 5 * SB4, cx=cd)
            khTt = sb(f"d_khTt{h}", 128, 2 * SB4, cx=cd)
            kck = sb(f"d_kck{h}", 128, 2 * SB4, cx=cd)
            qck = sb(f"d_qck{h}", 128, 2 * SB4, cx=cd)
            gend = sb(f"d_gend{h}", 128, 2 * CB, cx=cd)
            Mt = sb(f"d_Mt{h}", 128, 2 * SB4, cx=cd)
            MTt = sb(f"d_MTt{h}", 128, 2 * SB4, cx=cd)
            Xt = sb(f"d_Xt{h}", 128, 2 * SB4, cx=cd)
            powt = sb(f"d_powt{h}", 128, 8 * 768, cx=cd)
            powm = sb(f"d_powm{h}", 128, 5 * 128, cx=cd)
            Yt = sb(f"d_Yt{h}", 128, 128, cx=cd)
            Yt1 = sb(f"d_Yt1{h}", 128, 128, cx=cd)
            Wt0 = sb(f"d_Wt0{h}", 128, 128, cx=cd)
            Wt1 = sb(f"d_Wt1{h}", 128, 128, cx=cd)
            Wt2 = sb(f"d_Wt2{h}", 128, 128, cx=cd)
            Wt3 = sb(f"d_Wt3{h}", 128, 128, cx=cd)
            ocm = sb(f"d_ocm{h}", 128, 128, cx=cd)

            def m3(ap_, w):
                return ap_.rearrange("p (b w) -> p b w", w=w)

            GAM = argt[:, 11 * SB4:12 * SB4]
            for bi in range(NB):
                b0 = bi * SB4
                P.dma("sync", kap[:, :], qkv_post[1, h, :, b0:b0 + SB4],
                      reads=["qkvpost"], writes=["d_kap"])
                P.dma("sync", qap[:, :], qkv_post[0, h, :, b0:b0 + SB4],
                      reads=["qkvpost"], writes=["d_qap"])
                P.dma("sync", vap[:, :], qkv_post[2, h, :, b0:b0 + SB4],
                      reads=["qkvpost"], writes=["d_vap"])
                mm(pm[0][0:128, 0:SB4], rowsel[0:NR, bi * 128:bi * 128 + 128],
                   rknT[0:NR, h * 512:(h + 1) * 512], ["rowsel", "rknT"], ["pm0"])
                tt(kup[:, :], kap[:, :], pm[0][0:128, 0:SB4], ALU.mult,
                   ["d_kap", "pm0"], ["d_kup"])
                for c4 in range(CB):
                    tr(pm[1][0:128, c4 * 128:c4 * 128 + 128],
                       vap[:, c4 * 128:c4 * 128 + 128], 128, ["d_vap"], ["pm1"])
                cpy(vT4[:, :], pm[1][0:128, 0:SB4], ["pm1"], ["d_vT4"])
                for r in range(2):
                    blk = (h * 2 + r) * 512
                    mm(pm[0][0:128, 0:SB4], rowsel[0:NR, bi * 128:bi * 128 + 128],
                       arT[0:NR, blk:blk + 512], ["rowsel", "arT"], ["pm0"])
                    cpy(GAM, pm[0][0:128, 0:SB4], ["pm0"], ["d_argt"])
                    for li, w in enumerate(LVLS):
                        bnd = m3(GAM, w)[:, :, w // 2 - 1:w // 2] \
                            .broadcast_to([128, SB4 // w, w])
                        tt(m3(argt[:, li * SB4:(li + 1) * SB4], w), m3(GAM, w),
                           bnd, ALU.subtract, ["d_argt"], ["d_argt"])
                        tt(m3(argt[:, (5 + li) * SB4:(6 + li) * SB4], w), bnd,
                           m3(GAM, w), ALU.subtract, ["d_argt"], ["d_argt"])
                    bnd = m3(GAM, 128)[:, :, 127:128].broadcast_to([128, CB, 128])
                    tt(m3(argt[:, 10 * SB4:11 * SB4], 128), bnd, m3(GAM, 128),
                       ALU.subtract, ["d_argt"], ["d_argt"])
                    tsc(argt[:, 1 * SB4:5 * SB4], argt[:, 1 * SB4:5 * SB4], 0.0,
                        ALU.max, ["d_argt"], ["d_argt"])
                    tsc(argt[:, 6 * SB4:10 * SB4], argt[:, 6 * SB4:10 * SB4], 0.0,
                        ALU.max, ["d_argt"], ["d_argt"])
                    act(expt[:, :], argt[:, :], AF.Exp, ["d_argt", "gchan"],
                        ["d_argt"], scale=gchan[:, h:h + 1])
                    cpy(gend[:, r * CB:(r + 1) * CB],
                        expt[:, 11 * SB4 + 127:12 * SB4:128], ["d_argt"],
                        ["d_gend"])
                    mm(pm[0][0:128, 0:SB4], rowsel[0:NR, bi * 128:bi * 128 + 128],
                       (bfT if r == 0 else bsT)[0:NR, h * 512:(h + 1) * 512],
                       ["rowsel", "bfT" if r == 0 else "bsT"], ["pm0"])
                    tt(kbt[:, :], kup[:, :], pm[0][0:128, 0:SB4], ALU.mult,
                       ["d_kup", "pm0"], ["d_kbt"])
                    kbr = kbt[:, :].unsqueeze(1).broadcast_to([128, 5, SB4])
                    kur = kup[:, :].unsqueeze(1).broadcast_to([128, 5, SB4])
                    qar = qap[:, :].unsqueeze(1).broadcast_to([128, 5, SB4])
                    e5 = expt[:, 0:5 * SB4].rearrange("p (l t) -> p l t", t=SB4)
                    g5 = expt[:, 5 * SB4:10 * SB4].rearrange("p (l t) -> p l t", t=SB4)
                    tt(fkt[:, :].rearrange("p (l t) -> p l t", t=SB4), e5, kbr,
                       ALU.mult, ["d_argt", "d_kbt"], ["d_fkt"])
                    stt(fqt[:, :].rearrange("p (l t) -> p l t", t=SB4), e5, DKS,
                        qar, ALU.mult, ALU.mult, ["d_argt", "d_qap"], ["d_fqt"])
                    tt(gt[:, :].rearrange("p (l t) -> p l t", t=SB4), g5, kur,
                       ALU.mult, ["d_argt", "d_kup"], ["d_gt"])
                    tt(khf[:, :], expt[:, 10 * SB4:11 * SB4], kup[:, :], ALU.mult,
                       ["d_argt", "d_kup"], ["d_khf"])
                    for c4 in range(CB):
                        tr(pm[1][0:128, c4 * 128:c4 * 128 + 128],
                           khf[:, c4 * 128:c4 * 128 + 128], 128, ["d_khf"], ["pm1"])
                    cpy(khTt[:, r * SB4:(r + 1) * SB4], pm[1][0:128, 0:SB4],
                        ["pm1"], ["d_khTt"])
                    tt(kck[:, r * SB4:(r + 1) * SB4], expt[:, 11 * SB4:12 * SB4],
                       kup[:, :], ALU.mult, ["d_argt", "d_kup"], ["d_kck"])
                    stt(qck[:, r * SB4:(r + 1) * SB4], expt[:, 11 * SB4:12 * SB4],
                        DKS, qap[:, :], ALU.mult, ALU.mult, ["d_argt", "d_qap"],
                        ["d_qck"])
                    if h == 0 and bi == 0 and r == 0:
                        dbgw("d_arg", argt[:, :], ["d_argt"])
                        dbgw("d_kck", kck[:, 0:SB4], ["d_kck"])
                        dbgw("d_qck", qck[:, 0:SB4], ["d_qck"])
                        dbgw("d_khT", khTt[:, 0:SB4], ["d_khTt"])
                        dbgw("d_vT4", vT4[:, :], ["d_vT4"])
                        dbgw("d_fkt", fkt[:, :], ["d_fkt"])
                        dbgw("d_gt", gt[:, :], ["d_gt"])
                    LVB = (pm[0], "pm0"), (pm[1], "pm1"), (pb[4], "pb4")
                    for li, w in enumerate(LVLS):
                        for mi, (dst, dk, lh, lk, rh, rk_, f8) in enumerate((
                                (Mt, "d_Mt", fkt, "d_fkt", gt, "d_gt", 0),
                                (MTt, "d_MTt", gt, "d_gt", fkt, "d_fkt", 1),
                                (Xt, "d_Xt", gt, "d_gt", fqt, "d_fqt", 2))):
                            bk, bkk = LVB[mi]
                            for c4 in range(CB):
                                o = li * SB4 + c4 * 128
                                mm(bk[0:128, c4 * 128:c4 * 128 + 128],
                                   lh[:, o:o + 128], rh[:, o:o + 128],
                                   [lk, rk_], [bkk])
                        for mi, (dst, dk, lh, lk, rh, rk_, f8) in enumerate((
                                (Mt, "d_Mt", fkt, "d_fkt", gt, "d_gt", 0),
                                (MTt, "d_MTt", gt, "d_gt", fkt, "d_fkt", 1),
                                (Xt, "d_Xt", gt, "d_gt", fqt, "d_fqt", 2))):
                            bk, bkk = LVB[mi]
                            dsl = dst[:, r * SB4:(r + 1) * SB4]
                            if li == 0:
                                mrep = masksf[:, f8 * 512:f8 * 512 + 512]
                                tt(dsl, bk[0:128, 0:SB4], mrep, ALU.mult,
                                   [bkk, "masks"], [dk])
                            else:
                                mu_ = (li - 1) if f8 == 0 else (3 + li)
                                mrep = masksu[:, mu_ * 512:mu_ * 512 + 512]
                                cpred(dsl, mrep, bk[0:128, 0:SB4],
                                      [bkk, "masks", dk], [dk])
                    if h == 0 and bi == 0 and r == 0:
                        dbgw("d_M", Mt[:, 0:SB4], ["d_Mt"])
                        dbgw("d_MT", MTt[:, 0:SB4], ["d_MTt"])
                        dbgw("d_X", Xt[:, 0:SB4], ["d_Xt"])
                    pprev = {}
                    for c4 in range(CB):
                        base = r * SB4 + c4 * 128
                        pprev[c4] = (Mt[:, base:base + 128], "d_Mt",
                                     MTt[:, base:base + 128], "d_MTt")
                    for pi in range(6):
                        for c4 in range(CB):
                            prev, prevk, prevT, prevTk = pprev[c4]
                            cc = c4 * 128
                            mm(pm[1][0:128, cc:cc + 128], prev, prevT,
                               [prevk, prevTk], ["pm1"])
                            if pi < 5:
                                mm(pm[0][0:128, cc:cc + 128], prevT, prev,
                                   [prevk, prevTk], ["pm0"])
                        for c4 in range(CB):
                            cc = c4 * 128
                            pslot = (r * CB + c4) * 768
                            ptk = f"d_powt{r}{c4}"
                            pmk = f"d_powm{c4}"
                            po = powt[:, pslot + pi * 128:pslot + pi * 128 + 128]
                            if pi % 2 == 0:
                                cpy(po, pm[1][0:128, cc:cc + 128], ["pm1"], [ptk])
                            else:
                                act(po, pm[1][0:128, cc:cc + 128], AF.Copy,
                                    ["pm1"], [ptk])
                            if pi < 5:
                                cpy(powm[:, cc:cc + 128],
                                    pm[0][0:128, cc:cc + 128], ["pm0"], [pmk])
                                pprev[c4] = (powm[:, cc:cc + 128], pmk, po, ptk)
                BA = {0: (pm[0], "pm0"), 1: (pb[4], "pb4")}
                BB = {0: (pm[1], "pm1"), 1: (pb[5], "pb5")}
                YTS = {0: (Yt, "d_Yt0"), 1: (Yt1, "d_Yt1")}
                BUFS = {0: [(Wt0[:, :], "d_Wt0"), (Wt1[:, :], "d_Wt1")],
                        1: [(Wt2[:, :], "d_Wt2"), (Wt3[:, :], "d_Wt3")]}
                for c4 in range(CB):
                    cn = bi * CB + c4
                    E = {}
                    for r in range(2):
                        base = r * SB4 + c4 * 128
                        pslot = (r * CB + c4) * 768
                        E[r] = dict(
                            base=base, pslot=pslot,
                            Ssl=Sst[:, (h * 2 + r) * 128:(h * 2 + r) * 128 + 128],
                            sk=f"Sst{h * 2 + r}",
                            bcol=colpk[:, (h * NCH + cn) * 5 + r:
                                       (h * NCH + cn) * 5 + r + 1],
                            gcol=gend[:, r * CB + c4:r * CB + c4 + 1],
                            ptk=f"d_powt{r}{c4}")
                    lcol = colpk[:, (h * NCH + cn) * 5 + 2:(h * NCH + cn) * 5 + 3]
                    olcol = omlam[:, h * NCH + cn:h * NCH + cn + 1]
                    for r in range(2):
                        e = E[r]; ba, bak = BA[r]
                        mm(ba[0:128, 0:128], kck[:, e["base"]:e["base"] + 128],
                           e["Ssl"], ["d_kck", e["sk"]], [bak])
                    for r in range(2):
                        e = E[r]; ba, bak = BA[r]; yt, ytk = YTS[r]
                        tt(yt[:, :], vT4[:, c4 * 128:c4 * 128 + 128],
                           ba[0:128, 0:128], ALU.subtract, ["d_vT4", bak], [ytk])
                        tsc(yt[:, :], yt[:, :], e["bcol"], ALU.mult,
                            [ytk, "colpk"], [ytk])
                        e["cur"], e["curk"] = yt[:, :], ytk
                        e["bidx"] = 0
                    for pi in (5, 4, 3, 2, 1, 0):
                        for r in range(2):
                            e = E[r]; bb, bbk = BB[r]
                            mm(bb[0:128, 0:128],
                               powt[:, e["pslot"] + pi * 128:
                                    e["pslot"] + pi * 128 + 128],
                               e["cur"], [e["ptk"], e["curk"]], [bbk])
                        for r in range(2):
                            e = E[r]; bb, bbk = BB[r]
                            nxt, nxtk = BUFS[r][e["bidx"]]
                            tt(nxt, e["cur"], bb[0:128, 0:128], ALU.add,
                               [e["curk"], bbk], [nxtk])
                            e["cur"], e["curk"] = nxt, nxtk
                            e["bidx"] ^= 1
                    for r in range(2):
                        e = E[r]; bb, bbk = BB[r]
                        mm(bb[0:128, 0:128], MTt[:, e["base"]:e["base"] + 128],
                           e["cur"], ["d_MTt", e["curk"]], [bbk])
                    for r in range(2):
                        e = E[r]; bb, bbk = BB[r]
                        nxt, nxtk = BUFS[r][e["bidx"]]
                        tt(nxt, e["cur"], bb[0:128, 0:128], ALU.subtract,
                           [e["curk"], bbk], [nxtk])
                        e["Wf"], e["Wfk"] = nxt, nxtk
                    if h == 0 and bi == 0 and c4 == 0:
                        dbgw("d_Y", Yt[:, :], ["d_Yt0"])
                        dbgw("d_W", E[0]["Wf"], [E[0]["Wfk"]])
                        dbgw("d_powt", powt[:, 0:768], ["d_powt00"])
                    for r in range(2):
                        e = E[r]; ba, bak = BA[r]
                        mm(ba[0:128, 0:128], Xt[:, e["base"]:e["base"] + 128],
                           e["Wf"], ["d_Xt", e["Wfk"]], [bak],
                           start=True, stop=False)
                        mm(ba[0:128, 0:128], qck[:, e["base"]:e["base"] + 128],
                           e["Ssl"], ["d_qck", e["sk"]], [bak],
                           start=False, stop=True)
                    tsc(ocm[:, :], BA[0][0][0:128, 0:128], lcol, ALU.mult,
                        ["pm0", "colpk"], ["d_ocm"])
                    stt(ocm[:, :], BA[1][0][0:128, 0:128], olcol, ocm[:, :],
                        ALU.mult, ALU.add, ["pb4", "omlam", "d_ocm"], ["d_ocm"])
                    for r in range(2):
                        e = E[r]; bb, bbk = BB[r]
                        mm(bb[0:128, 0:128], khTt[:, e["base"]:e["base"] + 128],
                           e["Wf"], ["d_khTt", e["Wfk"]], [bbk])
                    for r in range(2):
                        e = E[r]; bb, bbk = BB[r]
                        stt(e["Ssl"], e["Ssl"], e["gcol"], bb[0:128, 0:128],
                            ALU.mult, ALU.add, [e["sk"], "d_gend", bbk],
                            [e["sk"]])
                    if h == 0 and bi == 0 and c4 == 0:
                        dbgw("d_S1", E[0]["Ssl"], ["Sst0"])
                    P.dma("scalar", cc_o_in[h, cn * 128:cn * 128 + 128, :],
                          ocm[:, :], reads=["d_ocm"], writes=["ccoin"])
            P.barrier()

    def build_phase4():
        rnk = sb("rnk", 1, 1, I32)
        P.dma("sync", rnk[:, :], rank_i, writes=["rnk"])
        greg = ctx.enter_context(nc.gpsimd.register("gr_rank"))
        greg2 = ctx.enter_context(nc.gpsimd.register("gr_off"))
        P.wait_keys("gpsimd", ["rnk", "ccoout"])
        P.raw("gpsimd", lambda e: e.reg_load(greg, rnk[0:1, 0:1]))
        P.raw("gpsimd", lambda e: e.reg_mul(greg, greg, TC * DV))
        with contextlib.ExitStack() as c5:
            otok = sb("otok", 128, OSEG, cx=c5)
            sqo = sb("sqo", 128, 128, cx=c5)
            rmsr = sb("rmsr", 128, 16 * (OSEG // 128), cx=c5)
            xg_ = sb("xg_", 128, OSEG, cx=c5)
            sgg = sb("sgg", 128, OSEG, cx=c5)
            ogT = sb("ogT", 128, 16 * OSEG, cx=c5)
            x2T = sb("x2T", 128, 16 * OSEG, cx=c5)
            h2hi = sb("h2hi", 128, 16 * OSEG, BF16, cx=c5)
            h2lo = sb("h2lo", 128, 16 * OSEG, BF16, cx=c5)
            p4pool = sb("p4pool", 128, max(16 * OSEG, 8192), cx=c5)
            fhi = p4pool[:, 0:8 * OSEG].bitcast(BF16)
            flo = p4pool[:, 8 * OSEG:16 * OSEG].bitcast(BF16)
            dacc = ogT
            wos = p4pool[:, 4096:6144]
            g1s = p4pool[:, 0:2048]
            g2s = p4pool[:, 2048:4096]
            g2bs = sb("g2bs", 128, 16, cx=c5)
            sgc = sb("sgc", 128, 64, cx=c5)
            suc = sb("suc", 128, 64, cx=c5)
            wga = sb("wga", 128, 16 * 128, BF16, cx=c5)
            wua = sb("wua", 128, 16 * 128, BF16, cx=c5)
            wda = sb("wda", 128, 16 * 128, BF16, cx=c5)
            wgb_ = sb("wgb", 128, 16 * 128, BF16, cx=c5)
            wub_ = sb("wub", 128, 16 * 128, BF16, cx=c5)
            wdb_ = sb("wdb", 128, 16 * 128, BF16, cx=c5)
            xs0 = sb("p4x", 128, OSEG, cx=c5)
            ht0 = sb("p4h", 128, OSEG, cx=c5)
            bc1 = sb("p4b1", 128, OSEG, cx=c5)
            bc2 = sb("p4b2", 128, OSEG, cx=c5)
            pt1 = sb("p4t1", 128, OSEG, cx=c5)
            mrow = sb("p4mr", 1, OSEG, cx=c5)
            rrow2 = sb("p4rr", 1, OSEG, cx=c5)
            outt = sb("outt", 128, 128, cx=c5)
            P.dma("sync", g2bs[:, :], g2b_i, writes=["g2bs"])
            P.dma("sync", sgc[:, :], sg_i, writes=["sgc"])
            P.dma("sync", suc[:, :], su_i, writes=["suc"])
            for os_ in range(NOSEG):
                o0 = os_ * OSEG
                sro, cco = o0 // 512, o0 % 512
                P.raw("gpsimd", lambda e, o=o0 * DV: e.reg_add(greg2, greg, o))
                P.dma("gpsimd", o_own,
                      (cc_o_out_h, [[T * DV, 16], [DV, OSEG], [1, DV]]),
                      reads=["ccoout"], writes=["o_own"], offset_reg=greg2)
                P.dma("sync", g1s[:, :], g1_i, writes=["p4pool"])
                P.dma("sync", g2s[:, :], g2_i, writes=["p4pool"])
                bc(bc1[:, :], "p4b1", mur, "mur", sro, cco, OSEG, NRo)
                bc(bc2[:, :], "p4b2", rr_, "rr_", sro, cco, OSEG, NRo)
                for dc in range(16):
                    P.dma("sync", xs0[:, :], x_ownT[dc, :, o0:o0 + OSEG],
                          writes=["p4x"])
                    tt(ht0[:, :], xs0[:, :], bc1[:, :], ALU.subtract,
                       ["p4x", "p4b1"], ["p4h"])
                    tt(ht0[:, :], ht0[:, :], bc2[:, :], ALU.mult,
                       ["p4h", "p4b2"], ["p4h"])
                    tsc(ht0[:, :], ht0[:, :], ca1[:, dc:dc + 1], ALU.mult,
                        ["p4h", "ca1", "modv"], ["p4h"],
                        s2=modv[:, dc:dc + 1], op1=ALU.add)
                    mm(pb[3][0:128, 0:OSEG], g1s[:, dc * 128:dc * 128 + 128],
                       ht0[:, :], ["p4pool", "p4h"], ["pb3"],
                       start=(dc == 0), stop=(dc == 15))
                cpy(xg_[:, :], pb[3][0:128, 0:OSEG], ["pb3"], ["xg_"])
                for hv in range(16):
                    mm(pb[4][0:128, 0:OSEG], g2s[:, hv * 128:hv * 128 + 128],
                       xg_[:, :], ["p4pool", "xg_"], ["pb4"])
                    act(sgg[:, :], pb[4][0:128, 0:OSEG], AF.Sigmoid,
                        ["pb4", "g2bs"], ["sgg"], bias=g2bs[:, hv:hv + 1])
                    for tb in range(OSEG // 128):
                        P.dma("sync", otok[:, tb * 128:tb * 128 + 128],
                              o_own[hv, tb * 128 * DV:(tb + 1) * 128 * DV]
                              .rearrange("(t v) -> t v", v=DV),
                              reads=["o_own"], writes=["otok"])
                        act(sqo[:, :], otok[:, tb * 128:tb * 128 + 128],
                            AF.Square, ["otok"], ["sqo", "rmsr"],
                            accum=rmsr[:, hv * (OSEG // 128) + tb:
                                       hv * (OSEG // 128) + tb + 1])
                    rsl = rmsr[:, hv * (OSEG // 128):(hv + 1) * (OSEG // 128)]
                    tsc(rsl, rsl, 1.0 / DV, ALU.mult, ["rmsr"], ["rmsr"],
                        s2=NORM_EPS, op1=ALU.add)
                    act(rsl, rsl, AF.Sqrt, ["rmsr"], ["rmsr"])
                    recip(rsl, rsl, ["rmsr"], ["rmsr"])
                    for tb in range(OSEG // 128):
                        stt(otok[:, tb * 128:tb * 128 + 128],
                            otok[:, tb * 128:tb * 128 + 128],
                            rmsr[:, hv * (OSEG // 128) + tb:
                                 hv * (OSEG // 128) + tb + 1],
                            onbc[:, :], ALU.mult, ALU.mult,
                            ["otok", "rmsr", "onbc"], ["otok"])
                        tr(pm[0][0:128, tb * 128:tb * 128 + 128],
                           otok[:, tb * 128:tb * 128 + 128], 128, ["otok"],
                           ["pm0"])
                    cpy(ogT[:, hv * OSEG:(hv + 1) * OSEG],
                        pm[0][0:128, 0:OSEG], ["pm0"], ["ogT"])
                    tt(ogT[:, hv * OSEG:(hv + 1) * OSEG],
                       ogT[:, hv * OSEG:(hv + 1) * OSEG], sgg[:, :], ALU.mult,
                       ["ogT", "sgg"], ["ogT"])
                for dc in range(16):
                    P.dma("sync", wos[:, :], wo_i[dc], writes=["p4pool"])
                    for hv in range(16):
                        mm(pb[5][0:128, 0:OSEG], wos[:, hv * 128:hv * 128 + 128],
                           ogT[:, hv * OSEG:(hv + 1) * OSEG], ["p4pool", "ogT"],
                           ["pb5"], start=(hv == 0), stop=(hv == 15))
                    P.dma("sync", xs0[:, :], x_ownT[dc, :, o0:o0 + OSEG],
                          writes=["p4x"])
                    stt(x2T[:, dc * OSEG:(dc + 1) * OSEG], pb[5][0:128, 0:OSEG],
                        gacl[:, dc:dc + 1], xs0[:, :], ALU.mult, ALU.add,
                        ["pb5", "gacl", "p4x"], ["x2T"])
                if os_ == 0:
                    dbgw("x2T", x2T[:, :], ["x2T"])
                for dc in range(16):
                    xsl = x2T[:, dc * OSEG:(dc + 1) * OSEG]
                    act(pt1[:, :], xsl, AF.Square, ["x2T"], ["p4t1"])
                    mm(pb[0][0:1, 0:OSEG], ones[:, 0:1], xsl, ["ones", "x2T"],
                       ["pb0"], start=(dc == 0), stop=(dc == 15))
                    mm(pb[1][0:1, 0:OSEG], ones[:, 0:1], pt1[:, :],
                       ["ones", "p4t1"], ["pb1"], start=(dc == 0), stop=(dc == 15))
                tsc(mrow[0:1, :], pb[0][0:1, 0:OSEG], 1.0 / D, ALU.mult,
                    ["pb0"], ["p4mr"])
                tsc(rrow2[0:1, :], pb[1][0:1, 0:OSEG], 1.0 / D, ALU.mult,
                    ["pb1"], ["p4rr"])
                tt(pt1[0:1, :], mrow[0:1, :], mrow[0:1, :], ALU.mult,
                   ["p4mr"], ["p4t1"])
                tt(rrow2[0:1, :], rrow2[0:1, :], pt1[0:1, :], ALU.subtract,
                   ["p4rr", "p4t1"], ["p4rr"])
                tsc(rrow2[0:1, :], rrow2[0:1, :], LN_EPS, ALU.add,
                    ["p4rr"], ["p4rr"])
                act(rrow2[0:1, :], rrow2[0:1, :], AF.Sqrt, ["p4rr"], ["p4rr"])
                recip(rrow2[0:1, :], rrow2[0:1, :], ["p4rr"], ["p4rr"])
                mm(pm[1][0:128, 0:OSEG], ones[0:1, 0:128], mrow[0:1, 0:OSEG],
                   ["ones", "p4mr"], ["pm1"])
                cpy(bc1[:, :], pm[1][0:128, 0:OSEG], ["pm1"], ["p4b1"])
                mm(pm[1][0:128, 0:OSEG], ones[0:1, 0:128], rrow2[0:1, 0:OSEG],
                   ["ones", "p4rr"], ["pm1"])
                cpy(bc2[:, :], pm[1][0:128, 0:OSEG], ["pm1"], ["p4b2"])
                for dc in range(16):
                    xsl = x2T[:, dc * OSEG:(dc + 1) * OSEG]
                    tt(pt1[:, :], xsl, bc1[:, :], ALU.subtract,
                       ["x2T", "p4b1"], ["p4t1"])
                    tt(pt1[:, :], pt1[:, :], bc2[:, :], ALU.mult,
                       ["p4t1", "p4b2"], ["p4t1"])
                    tsc(pt1[:, :], pt1[:, :], cb1[:, dc:dc + 1], ALU.mult,
                        ["p4t1", "cb1", "modv"], ["p4t1"],
                        s2=modv[:, 48 + dc:48 + dc + 1], op1=ALU.add)
                    cpy(h2hi[:, dc * OSEG:(dc + 1) * OSEG], pt1[:, :],
                        ["p4t1"], ["h2hi"], eng="scalar")
                    tt(h2lo[:, dc * OSEG:(dc + 1) * OSEG], pt1[:, :],
                       h2hi[:, dc * OSEG:(dc + 1) * OSEG], ALU.subtract,
                       ["p4t1", "h2hi"], ["h2lo"])
                for icb in range(4):
                    for iq in range(16):
                        ici = icb * 16 + iq
                        if iq % 2 == 0:
                            wgt, wgk, wut, wuk = wga, "wga", wua, "wua"
                            bg, bgk, bu, buk = pb[0], "pb0", pb[1], "pb1"
                        else:
                            wgt, wgk, wut, wuk = wgb_, "wgb", wub_, "wub"
                            bg, bgk, bu, buk = pb[2], "pb2", pb[3], "pb3"
                        P.dma("gpsimd", wgt[:, :], wg8[ici], writes=[wgk])
                        P.dma("gpsimd", wut[:, :], wu8[ici], writes=[wuk])
                        for dc in range(16):
                            hsl = h2hi[:, dc * OSEG:(dc + 1) * OSEG]
                            lsl = h2lo[:, dc * OSEG:(dc + 1) * OSEG]
                            wsl = wgt[:, dc * 128:dc * 128 + 128]
                            usl = wut[:, dc * 128:dc * 128 + 128]
                            mm(bg[0:128, 0:OSEG], wsl, hsl, [wgk, "h2hi"],
                               [bgk], start=(dc == 0), stop=False)
                            mm(bg[0:128, 0:OSEG], wsl, lsl, [wgk, "h2lo"],
                               [bgk], start=False, stop=(dc == 15))
                            mm(bu[0:128, 0:OSEG], usl, hsl, [wuk, "h2hi"],
                               [buk], start=(dc == 0), stop=False)
                            mm(bu[0:128, 0:OSEG], usl, lsl, [wuk, "h2lo"],
                               [buk], start=False, stop=(dc == 15))
                        act(sgg[:, :], bg[0:128, 0:OSEG], AF.Silu,
                            [bgk, "sgc"], ["sgg"], scale=sgc[:, ici:ici + 1])
                        stt(pt1[:, :], bu[0:128, 0:OSEG], suc[:, ici:ici + 1],
                            sgg[:, :], ALU.mult, ALU.mult,
                            [buk, "suc", "sgg"], ["p4t1"])
                        cpy(fhi[:, iq * OSEG:(iq + 1) * OSEG], pt1[:, :],
                            ["p4t1"], ["p4pool"], eng="scalar")
                        tt(flo[:, iq * OSEG:(iq + 1) * OSEG], pt1[:, :],
                           fhi[:, iq * OSEG:(iq + 1) * OSEG], ALU.subtract,
                           ["p4t1", "p4pool"], ["p4pool"])
                    for dcol in range(16):
                        wdt, wdk = (wda, "wda") if dcol % 2 == 0 else (wdb_, "wdb")
                        P.dma("gpsimd", wdt[:, :], wd8[dcol, icb], writes=[wdk])
                        pk = 2 + (dcol % 4)
                        for iq in range(16):
                            wsl = wdt[:, iq * 128:iq * 128 + 128]
                            mm(pb[pk][0:128, 0:OSEG], wsl,
                               fhi[:, iq * OSEG:(iq + 1) * OSEG], [wdk, "p4pool"],
                               [PBK[pk]], start=(iq == 0), stop=False)
                            mm(pb[pk][0:128, 0:OSEG], wsl,
                               flo[:, iq * OSEG:(iq + 1) * OSEG], [wdk, "p4pool"],
                               [PBK[pk]], start=False, stop=(iq == 15))
                        dsl = dacc[:, dcol * OSEG:(dcol + 1) * OSEG]
                        if icb == 0:
                            cpy(dsl, pb[pk][0:128, 0:OSEG], [PBK[pk]], ["ogT"])
                        else:
                            tt(dsl, dsl, pb[pk][0:128, 0:OSEG], ALU.add,
                               ["ogT", PBK[pk]], ["ogT"])
                for dcol in range(16):
                    stt(pt1[:, :], dacc[:, dcol * OSEG:(dcol + 1) * OSEG],
                        sdgb[:, dcol:dcol + 1],
                        x2T[:, dcol * OSEG:(dcol + 1) * OSEG], ALU.mult, ALU.add,
                        ["ogT", "sdgb", "x2T"], ["p4t1"])
                    for tb in range(OSEG // 128):
                        tr(pm[0][0:128, 0:128], pt1[:, tb * 128:tb * 128 + 128],
                           128, ["p4t1"], ["pm0"])
                        cpy(outt[:, :], pm[0][0:128, 0:128], ["pm0"], ["outt"])
                        P.dma("sync", out_sl[o0 + tb * 128:o0 + tb * 128 + 128,
                                             dcol * 128:dcol * 128 + 128],
                              outt[:, :], reads=["outt"], writes=["outdma"])
            P.barrier()

    with contextlib.ExitStack() as cdel:
        colpk = sb("colpk", 128, 5 * NCH * HC, cx=cdel)
        omlam = sb("omlam", 128, NCH * HC, cx=cdel)
        rknT = sb("rknT", NR, 512 * HC, cx=cdel)
        bfT = sb("bfT", NR, 512 * HC, cx=cdel)
        bsT = sb("bsT", NR, 512 * HC, cx=cdel)
        arT = sb("arT", NR, 512 * HC * 2, cx=cdel)
        Sst = sb("Sst", 128, HC * 2 * 128, cx=cdel)
        mems(Sst[:, :], 0.0, ["Sst", "Sst0", "Sst1", "Sst2", "Sst3"])
        for h in range(HC):
            build_feats(h)
            build_delta(h)
        dbgw("colpk", colpk[:, :], ["colpk"])
        dbgw("o_comb", cc_o_in, ["ccoin"])
        P.collective("AllGather", RG, cc_o_in.opt(), cc_o_out.opt(),
                     reads=["ccoin"], writes=["ccoout"])
        P.barrier()
    cmid.close()
    build_phase4()
    P.barrier()
    with nc.Block() as block:
        P.finalize(block)
    ctx.close()
    return nc


def stage_inputs(inputs, T):
    ident, ones_, stag, rowsel, masksf, masksu = host_consts()
    TC = T // NCORE
    WS = 12288 // NCORE
    f = np.float32
    x = np.asarray(inputs["L0_attn_hidden_states"], f)[0]
    wq = np.asarray(inputs["L0_attn_wq"], f)
    wk = np.asarray(inputs["L0_attn_wk"], f)
    wv = np.asarray(inputs["L0_attn_wv"], f)
    cq = np.asarray(inputs["L0_attn_conv_q_w"], f)
    ck = np.asarray(inputs["L0_attn_conv_k_w"], f)
    cv = np.asarray(inputs["L0_attn_conv_v_w"], f)
    alog = np.asarray(inputs["L0_attn_A_log"], f)
    dtb = np.asarray(inputs["L0_attn_dt_bias"], f)
    proxy = np.asarray(inputs["L0_attn_proxy_w"], f)
    uncw = np.asarray(inputs["L0_attn_unc_w"], f)
    he = np.asarray(inputs["L0_attn_head_embed"], f)
    w1 = np.asarray(inputs["L0_attn_mlp_w1"], f)
    b1 = np.asarray(inputs["L0_attn_mlp_b1"], f)
    w2 = np.asarray(inputs["L0_attn_mlp_w2"], f)
    b2 = np.asarray(inputs["L0_attn_mlp_b2"], f)
    w5 = np.stack([np.asarray(inputs["L0_attn_" + n], f)
                   for n in ("bb_w", "bd_w", "lam_w", "ab_w", "ad_w")], 1)
    b5 = np.stack([np.asarray(inputs["L0_attn_" + n], f)[0]
                   for n in ("bb_b", "bd_b", "lam_b", "ab_b", "ad_b")])[None]
    g1 = np.asarray(inputs["L0_attn_g1_w"], f)
    g2 = np.asarray(inputs["L0_attn_g2_w"], f)
    g2b = np.asarray(inputs["L0_attn_g2_b"], f)
    onorm = np.asarray(inputs["L0_attn_onorm_w"], f)
    wo = np.asarray(inputs["L0_attn_wo"], f)
    wg = np.asarray(inputs["L0_ffn_w_gate"])
    wu = np.asarray(inputs["L0_ffn_w_up"])
    wd = np.asarray(inputs["L0_ffn_w_down"])
    sgv = np.asarray(inputs["L0_ffn_s_gate"], f)
    suv = np.asarray(inputs["L0_ffn_s_up"], f)
    sdv = np.asarray(inputs["L0_ffn_s_down"], f)
    cvec = np.asarray(inputs["L0_c"], f)
    wmod = np.asarray(inputs["L0_wmod"], f)

    g1h = np.ascontiguousarray(
        g1.reshape(16, 128, 128).transpose(1, 0, 2).reshape(128, 2048))
    g2h = np.ascontiguousarray(g2.reshape(128, 2048))
    g2bh = np.ascontiguousarray(g2b.reshape(16, 128).T)
    woh = np.ascontiguousarray(
        wo.reshape(16, 128, 16, 128).transpose(2, 1, 0, 3).reshape(16, 128, 2048))
    wgh = np.ascontiguousarray(
        wg.reshape(16, 128, 64, 128).transpose(2, 1, 0, 3).reshape(64, 128, 2048))
    wuh = np.ascontiguousarray(
        wu.reshape(16, 128, 64, 128).transpose(2, 1, 0, 3).reshape(64, 128, 2048))
    wdh = np.ascontiguousarray(
        wd.reshape(4, 16, 128, 16, 128).transpose(3, 0, 2, 1, 4)
        .reshape(16, 4, 128, 2048))
    sgh = np.ascontiguousarray(sgv.reshape(64, 128).T)
    suh = np.ascontiguousarray(suv.reshape(64, 128).T)
    sdh = np.ascontiguousarray(sdv.reshape(16, 128).T)
    unch = np.ascontiguousarray(
        uncw.reshape(16, 128, 64).transpose(1, 0, 2).reshape(128, 1024))
    cvh = np.ascontiguousarray(cvec[0].reshape(16, 128).T)

    in_maps = []
    for c in range(NCORE):
        hc0 = c * HC * 128
        wq_c = wq[:, hc0:hc0 + HC * 128].reshape(16, 128, HC, 128)
        wk_c = wk[:, hc0:hc0 + HC * 128].reshape(16, 128, HC, 128)
        wv_c = wv[:, hc0:hc0 + HC * 128].reshape(16, 128, HC, 128)
        wqkv_c = np.stack([wq_c, wk_c, wv_c], 1)
        wqkv_c = np.ascontiguousarray(
            wqkv_c.transpose(2, 0, 1, 3, 4).reshape(128, -1))
        convw_c = np.stack([cq[hc0:hc0 + HC * 128].reshape(HC, 128, 4),
                            ck[hc0:hc0 + HC * 128].reshape(HC, 128, 4),
                            cv[hc0:hc0 + HC * 128].reshape(HC, 128, 4)], 0)
        convw_c = np.ascontiguousarray(
            convw_c.transpose(3, 0, 1, 2).reshape(-1, 3 * HC * 4, order="F")
            if False else
            convw_c.transpose(1, 0, 3, 2).reshape(128, -1))
        convw_c = np.ascontiguousarray(
            np.stack([cq[hc0:hc0 + HC * 128], ck[hc0:hc0 + HC * 128],
                      cv[hc0:hc0 + HC * 128]], 0)
            .reshape(3, HC, 128, 4).transpose(2, 0, 1, 3).reshape(128, -1))
        m = {
            "x_ownT": np.ascontiguousarray(
                x[c * TC:(c + 1) * TC].T.reshape(16, 128, TC)),
            "wqkv": wqkv_c,
            "convw_i": convw_c,
            "alog": np.ascontiguousarray(alog[c * HC:(c + 1) * HC][None]),
            "dtb": np.ascontiguousarray(
                dtb[hc0:hc0 + HC * 128].reshape(HC, 128).T),
            "proxy": proxy,
            "uncw": unch,
            "he": np.ascontiguousarray(he[c * HC:(c + 1) * HC].T),
            "w1d_i": np.ascontiguousarray(w1[0:4]),
            "w1e_i": np.ascontiguousarray(w1[4:8]),
            "b1_i": np.ascontiguousarray(b1[:, None]),
            "w2_i": w2,
            "b2_i": np.ascontiguousarray(b2[:, None]),
            "w5_i": np.ascontiguousarray(w5),
            "b5_i": np.ascontiguousarray(b5),
            "g1_i": g1h,
            "g2_i": g2h,
            "g2b_i": g2bh,
            "onorm_i": np.ascontiguousarray(onorm[:, None]),
            "wo_i": woh,
            "wg8": wgh,
            "wu8": wuh,
            "wd8": wdh,
            "sg_i": sgh,
            "su_i": suh,
            "sd_i": sdh,
            "cvecT": cvh,
            "wmod_i": np.ascontiguousarray(
                wmod[:, c * WS:(c + 1) * WS].reshape(16, 128, WS)),
            "ident_i": ident,
            "ones_i": ones_,
            "stag_i": stag,
            "rowsel_i": rowsel,
            "masksf_i": masksf,
            "masksu_i": masksu,
            "rank_i": np.array([[c]], dtype=np.int32),
        }
        in_maps.append(m)
    return in_maps


def kernel(**inputs):
    T = int(np.asarray(inputs["L0_attn_hidden_states"]).shape[1])
    nc = build(T)
    in_maps = stage_inputs(inputs, T)
    res = run_bass_kernel_spmd(nc, in_maps, core_ids=list(range(NCORE)))
    out = np.concatenate(
        [np.asarray(res.results[i]["out_sl"]) for i in range(NCORE)], 0)
    return out.reshape(1, T, D).astype(np.float32)
